# version 01
# baseline (speedup 1.0000x reference)
import contextlib

import numpy as np
import ml_dtypes

import concourse.bass as bass
import concourse.mybir as mybir
from concourse.bass_utils import run_bass_kernel_spmd

F32 = mybir.dt.float32
BF16 = mybir.dt.bfloat16
FP8 = mybir.dt.float8e4
U8 = mybir.dt.uint8
AF = mybir.ActivationFunctionType
ALU = mybir.AluOpType
AX = mybir.AxisListType

B, S, D = 4, 4096, 2048
H, HD = 16, 128
M = 4
NCHUNK = 16
NEC = 16
TOK = 2048
SCALE = np.float32(np.float32(HD) ** np.float32(-0.5))
EPS_A = 1e-6
EPS_LN = 1e-5
FP8_C = float(np.float32(448.0) * np.float32(0.8))
MAGIC = float(np.float32(1.5 * 2.0 ** 23))
RFP8_C = float(np.float32(1.0) / np.float32(448.0 * np.float32(0.8)))
NEG_BIG = -1e30
_SIM_ALL8 = None
_SIM_LOCAL_AMOUT = False


class Sem:
    def __init__(self, handle):
        self.h = handle
        self.n = 0


class Rec:

    def __init__(self):
        self.ops = []
        self.own = {}

    def _em(self, key, fn, inc=None, k=1):
        own = self.own.get(key)
        if own is not None and own.n > 0:
            self.ops.append((key, lambda e, h=own.h, v=own.n: e.wait_ge(h, v)))
        if inc is not None:
            inc.n += k
            h, kk = inc.h, k
            self.ops.append((key, lambda e, f=fn, h=h, kk=kk: f(e).then_inc(h, kk)))
        else:
            self.ops.append((key, fn))

    def pe(self, fn, inc=None, k=1):
        self._em('pe', fn, inc, k)

    def dve(self, fn, inc=None, k=1):
        self._em('dve', fn, inc, k)

    def act(self, fn, inc=None, k=1):
        self._em('act', fn, inc, k)

    def pool(self, fn, inc=None, k=1):
        self._em('pool', fn, inc, k)

    def sp(self, fn, inc=None, k=1):
        self._em('sp', fn, inc, k)

    def wait(self, key, sem, val=None):
        v = sem.n if val is None else val
        if v <= 0:
            return
        self.ops.append((key, lambda e, h=sem.h, v=v: e.wait_ge(h, v)))

    def replay(self, key, eng):
        for k, fn in self.ops:
            if k == key:
                fn(eng)


def build_nc():
    nc = bass.Bass("TRN2", target_bir_lowering=False, debug=False)

    def din(name, shape, dt=F32):
        return nc.dram_tensor(name, shape, dt, kind="ExternalInput").ap()

    x_tm = din("x_tm", [TOK, D])
    x_fm = din("x_fm", [D, TOK])
    wq_r = din("wq_r", [H, 128, D])
    wk_r = din("wk_r", [H, 128, D])
    wv_r = din("wv_r", [H, 128, D])
    wo_r = din("wo_r", [H, 128, D])
    wf_r = din("wf_r", [NEC, 128, D])
    mem_r = din("mem_r", [H, 128, M * 128])
    mnT_r = din("mnT_r", [128, H * M])
    gate_r = din("gate_r", [1, H])
    bias_r = din("bias_r", [1, D])
    maskT_r = din("maskT_r", [128, 128], BF16)
    idf_r = din("idf_r", [128, 128])
    flag_r = din("flag_r", [1, 1])
    idb_r = din("idb_r", [128, 128], BF16)
    ones_r = din("ones_r", [128, 128], BF16)

    out_d = nc.dram_tensor("out", [TOK, D], F32, kind="ExternalOutput").ap()

    x1_d = nc.dram_tensor("x1_d", [TOK, D], F32, kind="Internal").ap()
    statb_m = nc.dram_tensor("statb_m", [1, TOK], F32, kind="Internal").ap()
    statb_r = nc.dram_tensor("statb_r", [1, TOK], F32, kind="Internal").ap()
    gb_d = nc.dram_tensor("gb_d", [1, 2 * H], F32, kind="Internal").ap()
    wgb_d = nc.dram_tensor("wgb_d", [1, H * M], F32, kind="Internal").ap()
    scb_d = nc.dram_tensor("scb_d", [1, 8], F32, kind="Internal").ap()
    sta_in = [nc.dram_tensor(f"sta_in{h}", [128, 129], F32, kind="Internal").ap()
              for h in range(H)]
    sta_out = [nc.dram_tensor(f"sta_out{h}", [256, 129], F32, kind="Internal").ap()
               for h in range(H)]
    rel_in = nc.dram_tensor("rel_in", [4, 16], F32, kind="Internal").ap()
    rel_out = nc.dram_tensor("rel_out", [8, 16], F32, kind="Internal").ap()
    am_in = nc.dram_tensor("am_in", [1, 1], F32, kind="Internal").ap()
    am_out = nc.dram_tensor("am_out", [1, 1], F32, kind="Internal",
                            **({} if _SIM_LOCAL_AMOUT else {"addr_space": "Shared"})).ap()

    PAIRS = [[0, 1], [2, 3], [4, 5], [6, 7]]
    ALL8 = _SIM_ALL8 or [[0, 1, 2, 3, 4, 5, 6, 7]]

    with contextlib.ExitStack() as _st:
        pool_a = _st.enter_context(nc.sbuf_tensor("pool_a", [128, 32768], BF16))
        pool_b = _st.enter_context(nc.sbuf_tensor("pool_b", [128, 32768], BF16))
        wp = _st.enter_context(nc.sbuf_tensor("wp", [128, 13376], BF16))
        xst = _st.enter_context(nc.sbuf_tensor("xst", [128, 2048], F32))
        xfm2 = _st.enter_context(nc.sbuf_tensor("xfm2", [128, 4096], F32))
        s1 = _st.enter_context(nc.sbuf_tensor("s1", [128, 2048], F32))
        s2 = _st.enter_context(nc.sbuf_tensor("s2", [128, 2048], F32))
        s3 = _st.enter_context(nc.sbuf_tensor("s3", [128, 2048], F32))
        vec = _st.enter_context(nc.sbuf_tensor("vec", [128, 56], F32))
        stats_m = _st.enter_context(nc.sbuf_tensor("stats_m", [128, 16], F32))
        stats_r = _st.enter_context(nc.sbuf_tensor("stats_r", [128, 16], F32))
        stat_t = _st.enter_context(nc.sbuf_tensor("stat_t", [16, 128], F32))
        mem_tm = _st.enter_context(nc.sbuf_tensor("mem_tm", [128, 128], F32))
        loc_tm = _st.enter_context(nc.sbuf_tensor("loc_tm", [128, 128], F32))
        maskT = _st.enter_context(nc.sbuf_tensor("maskT", [128, 128], BF16))
        idf = _st.enter_context(nc.sbuf_tensor("idf", [128, 128], F32))
        Wb = _st.enter_context(nc.sbuf_tensor("Wb", [128, 64], F32))
        Wb4 = _st.enter_context(nc.sbuf_tensor("Wb4", [128, 4], F32))
        gWb = _st.enter_context(nc.sbuf_tensor("gWb", [128, 16], F32))
        gLb = _st.enter_context(nc.sbuf_tensor("gLb", [128, 16], F32))
        flagv = _st.enter_context(nc.sbuf_tensor("flagv", [128, 1], F32))
        relp = _st.enter_context(nc.sbuf_tensor("relp", [4, 16], F32))
        relB = _st.enter_context(nc.sbuf_tensor("relB", [4, 16], F32))
        relT = _st.enter_context(nc.sbuf_tensor("relT", [16, 4], F32))
        relT2 = _st.enter_context(nc.sbuf_tensor("relT2", [16, 4], F32))
        grow = _st.enter_context(nc.sbuf_tensor("grow", [1, 2 * H], F32))
        wgrow = _st.enter_context(nc.sbuf_tensor("wgrow", [16, 4], F32))
        mrow = _st.enter_context(nc.sbuf_tensor("mrow", [1, 4], F32))
        mrow_b = _st.enter_context(nc.sbuf_tensor("mrow_b", [1, 4], BF16))
        scrow = _st.enter_context(nc.sbuf_tensor("scrow", [1, 8], F32))
        idb = _st.enter_context(nc.sbuf_tensor("idb", [128, 128], BF16))
        onesb = _st.enter_context(nc.sbuf_tensor("onesb", [128, 128], BF16))
        outc_b = _st.enter_context(nc.sbuf_tensor("outc_b", [128, 128], BF16))
        scb16 = _st.enter_context(nc.sbuf_tensor("scb16", [128, 128], BF16))
        sxln_b = _st.enter_context(nc.sbuf_tensor("sxln_b", [128, 16], BF16))
        ph_b = _st.enter_context(nc.sbuf_tensor("ph_b", [128, 1], BF16))
        mnT_b = _st.enter_context(nc.sbuf_tensor("mnT_b", [128, 64], BF16))
        pA0 = _st.enter_context(nc.psum_tensor("pA0", [128, 512], F32))
        pA1 = _st.enter_context(nc.psum_tensor("pA1", [128, 512], F32))
        pB0 = _st.enter_context(nc.psum_tensor("pB0", [128, 512], F32))
        pB1 = _st.enter_context(nc.psum_tensor("pB1", [128, 512], F32))
        pC0 = _st.enter_context(nc.psum_tensor("pC0", [128, 512], F32))
        pC1 = _st.enter_context(nc.psum_tensor("pC1", [128, 512], F32))
        pTa = _st.enter_context(nc.psum_tensor("pTa", [128, 1024], BF16))
        pTb = _st.enter_context(nc.psum_tensor("pTb", [128, 1024], BF16))
        _sp = _st.enter_context(nc.semaphore("m_sp"))
        _gq = _st.enter_context(nc.semaphore("m_gq"))
        _cc = _st.enter_context(nc.semaphore("m_cc"))
        _pe = _st.enter_context(nc.semaphore("m_pe"))
        _dve = _st.enter_context(nc.semaphore("m_dve"))
        _act = _st.enter_context(nc.semaphore("m_act"))
        _pool = _st.enter_context(nc.semaphore("m_pool"))
        block = _st.enter_context(nc.Block())

        r = Rec()
        cc_s = Sem(_cc)
        pe_s, dv_s, ac_s, po_s = Sem(_pe), Sem(_dve), Sem(_act), Sem(_pool)
        r.own = {'dve': dv_s, 'act': ac_s, 'pool': po_s}
        MS = Sem(_st.enter_context(nc.semaphore("m_ms")))
        XP = [Sem(_st.enter_context(nc.semaphore("m_x0"))),
              Sem(_st.enter_context(nc.semaphore("m_x1")))]
        WP = [Sem(_st.enter_context(nc.semaphore("m_w0"))),
              Sem(_st.enter_context(nc.semaphore("m_w1")))]
        XSTP = [Sem(_st.enter_context(nc.semaphore("m_xs0"))),
                Sem(_st.enter_context(nc.semaphore("m_xs1")))]
        OUTP = [Sem(_st.enter_context(nc.semaphore("m_o0"))),
                Sem(_st.enter_context(nc.semaphore("m_o1")))]
        GWQ = Sem(_st.enter_context(nc.semaphore("m_gwq")))
        GWK = Sem(_st.enter_context(nc.semaphore("m_gwk")))
        GWV = Sem(_st.enter_context(nc.semaphore("m_gwv")))
        GME = Sem(_st.enter_context(nc.semaphore("m_gme")))
        GWO = Sem(_st.enter_context(nc.semaphore("m_gwo")))
        GST = Sem(_st.enter_context(nc.semaphore("m_gst")))
        GRC = Sem(_st.enter_context(nc.semaphore("m_grc")))
        GMC = Sem(_st.enter_context(nc.semaphore("m_gmc")))

        def V(i):
            return vec[:, i:i + 1]
        C_EPSLN, C_NEGHALF, C_TWO, C_NEGEPSA, C_NEGONE = V(0), V(1), V(2), V(3), V(4)
        T_SUM, T_SSQ, T_NEGM, T_RSTD, T_MSQ, T_T0 = V(5), V(6), V(7), V(8), V(9), V(10)
        T_R, T_DEN = V(11), V(12)
        AMX_ACC, T_DEV = V(13), V(14)
        AW_ACC, AW_T = V(15), V(16)
        SC_AX, SC_AW, SC_SX, SC_SW, SC_A = V(17), V(18), V(19), V(20), V(21)
        HRSX, HRSW, AVEC, RAVEC = V(22), V(23), V(24), V(25)
        T_M4, T_V4 = V(26), V(27)

        SQ = wp[:, 0:2048]
        KPF = wp[:, 2048:4096]
        KPT = wp[:, 4096:6144]
        VAUG = wp[:, 6144:8208]
        WKH = wp[:, 8208:10256]
        WVH = wp[:, 10256:12304]
        MEMA = [wp[:, 12304:12816], wp[:, 12816:13328]]
        BIAS_HI = wp[0:1, 2048:4096]
        BIAS_LO = wp[0:1, 4096:6144]
        wpf = wp[:, 8208:12304].bitcast(F32)
        msku = wp[:, 12304:13328].bitcast(U8)
        t2_tm = s3[:, 1664:1792]
        recvf = s2[:, 0:129]
        totf = s2[:, 256:385]
        state_b = s2[:, 512:577].bitcast(BF16)[:, 0:129]
        partner_b = s2[:, 640:705].bitcast(BF16)[:, 0:129]
        fp8t = wp[:, 6144:7168].bitcast(FP8)

        r.wait('sp', MS, MS.n)
        r.sp(lambda e: e.dma_start(out=maskT[:], in_=maskT_r), inc=MS, k=16)
        r.wait('sp', MS, MS.n)
        r.sp(lambda e: e.dma_start(out=idf[:], in_=idf_r), inc=MS, k=16)
        r.wait('sp', MS, MS.n)
        r.sp(lambda e: e.dma_start(out=idb[:], in_=idb_r), inc=MS, k=16)
        r.wait('sp', MS, MS.n)
        r.sp(lambda e: e.dma_start(out=onesb[:], in_=ones_r), inc=MS, k=16)
        r.wait('sp', MS, MS.n)
        r.sp(lambda e: e.dma_start(out=grow[:, 0:16], in_=gate_r), inc=MS, k=16)
        r.wait('sp', MS, MS.n)
        r.sp(lambda e: e.dma_start(out=flagv[:],
                                   in_=flag_r[0:1, 0:1].partition_broadcast(128)),
             inc=MS, k=16)
        con_loads = MS.n

        for val, ap in [(EPS_LN, C_EPSLN), (-0.5, C_NEGHALF), (2.0, C_TWO),
                        (-EPS_A, C_NEGEPSA), (-1.0, C_NEGONE), (0.0, AMX_ACC),
                        (0.0, AW_ACC)]:
            r.dve(lambda e, v=val, a=ap: e.memset(a, v), inc=dv_s)
        r.dve(lambda e: e.memset(VAUG, 1.0), inc=dv_s)
        r.dve(lambda e: e.memset(state_b, 0.0), inc=dv_s)
        memset_done = dv_s.n

        r.wait('act', MS, con_loads)
        r.wait('act', dv_s, memset_done)
        r.act(lambda e: e.activation(grow[0:1, 16:32], grow[0:1, 0:16], AF.Exp,
                                     bias=0.0, scale=C_NEGONE[0:1, :]), inc=ac_s)
        g_exp = ac_s.n
        r.wait('dve', ac_s, g_exp)
        r.dve(lambda e: e.tensor_scalar(grow[0:1, 16:32], grow[0:1, 16:32], 1.0, None,
                                        ALU.add), inc=dv_s)
        r.dve(lambda e: e.reciprocal(grow[0:1, 0:16], grow[0:1, 16:32]), inc=dv_s)
        r.dve(lambda e: e.tensor_scalar(grow[0:1, 16:32], grow[0:1, 0:16], 1.0, -1.0,
                                        ALU.subtract, ALU.mult), inc=dv_s)
        g_done = dv_s.n
        r.wait('sp', dv_s, g_done)
        r.wait('sp', MS, MS.n)
        r.sp(lambda e: e.dma_start(out=gb_d, in_=grow[:]), inc=MS, k=16)
        gb_out = MS.n
        r.wait('sp', MS, gb_out)
        r.wait('sp', MS, MS.n)
        r.sp(lambda e: e.dma_start(out=gWb[:],
                                   in_=gb_d[0:1, 0:16].partition_broadcast(128)),
             inc=MS, k=16)
        r.wait('sp', MS, MS.n)
        r.sp(lambda e: e.dma_start(out=gLb[:],
                                   in_=gb_d[0:1, 16:32].partition_broadcast(128)),
             inc=MS, k=16)
        gl_rep = MS.n

        pa_marks = []
        for t in range(NCHUNK):
            buf = xst if t % 2 == 0 else s1
            if t >= 2:
                r.wait('sp', ac_s, pa_marks[t - 2][0])
                r.wait('sp', dv_s, pa_marks[t - 2][1])
            xp = XP[t % 2]
            r.sp(lambda e, t=t, b=buf: e.dma_start(
                out=b[:], in_=x_tm[t * 128:(t + 1) * 128, :]), inc=xp, k=16)
            ld = xp.n
            r.wait('dve', xp, ld)
            r.dve(lambda e, b=buf: e.tensor_reduce(T_SUM, b[:], AX.X, ALU.add), inc=dv_s)
            red_mark = dv_s.n
            r.wait('act', xp, ld)
            r.act(lambda e, b=buf: e.activation(s2[:, 0:2048], b[:], AF.Square,
                                                accum_out=T_SSQ), inc=ac_s)
            sq_mark = ac_s.n
            r.dve(lambda e: e.tensor_scalar(T_NEGM, T_SUM, -1.0 / 2048.0, None,
                                            ALU.mult), inc=dv_s)
            r.wait('dve', ac_s, sq_mark)
            r.dve(lambda e: e.tensor_scalar(T_MSQ, T_NEGM, T_NEGM, None, ALU.mult), inc=dv_s)
            r.dve(lambda e: e.tensor_scalar(T_T0, T_SSQ, 1.0 / 2048.0, None, ALU.mult), inc=dv_s)
            r.dve(lambda e: e.tensor_tensor(T_T0, T_T0, T_MSQ, ALU.subtract), inc=dv_s)
            var_mark = dv_s.n
            r.wait('act', dv_s, var_mark)
            r.act(lambda e: e.activation(T_T0, T_T0, AF.Ln, bias=C_EPSLN, scale=1.0), inc=ac_s)
            r.act(lambda e: e.activation(T_RSTD, T_T0, AF.Exp, bias=0.0,
                                         scale=C_NEGHALF), inc=ac_s)
            rstd_mark = ac_s.n
            r.dve(lambda e, t=t: e.tensor_copy(stats_m[:, t:t + 1], T_NEGM), inc=dv_s)
            r.wait('dve', ac_s, rstd_mark)
            r.dve(lambda e, t=t: e.tensor_copy(stats_r[:, t:t + 1], T_RSTD), inc=dv_s)
            pa_marks.append((sq_mark, dv_s.n))
        pa_dve = dv_s.n
        pa_act = ac_s.n

        r.wait('pe', dv_s, pa_dve)
        r.wait('pe', MS, con_loads)
        r.pe(lambda e: e.transpose(pA0[0:16, 0:128], stats_m[:], idf[:]), inc=pe_s)
        tr1 = pe_s.n
        r.wait('dve', pe_s, tr1)
        r.dve(lambda e: e.tensor_copy(stat_t[:], pA0[0:16, 0:128]), inc=dv_s)
        ev1 = dv_s.n
        r.wait('sp', dv_s, ev1)
        r.wait('sp', MS, MS.n)
        r.sp(lambda e: e.dma_start(
            out=statb_m.rearrange("a (p c) -> (a p) c", c=128), in_=stat_t[:]),
            inc=MS, k=16)
        st1 = MS.n
        r.wait('pe', dv_s, ev1)
        r.pe(lambda e: e.transpose(pA0[0:16, 0:128], stats_r[:], idf[:]), inc=pe_s)
        tr2 = pe_s.n
        r.wait('dve', pe_s, tr2)
        r.wait('dve', MS, st1)
        r.dve(lambda e: e.tensor_copy(stat_t[:], pA0[0:16, 0:128]), inc=dv_s)
        ev2 = dv_s.n
        r.wait('sp', dv_s, ev2)
        r.wait('sp', MS, MS.n)
        r.sp(lambda e: e.dma_start(
            out=statb_r.rearrange("a (p c) -> (a p) c", c=128), in_=stat_t[:]),
            inc=MS, k=16)
        st2 = MS.n
        r.wait('sp', MS, st2)
        r.wait('sp', dv_s, pa_dve)
        r.wait('sp', ac_s, pa_act)
        r.wait('sp', MS, MS.n)
        r.sp(lambda e: e.dma_start(out=s1[:],
                                   in_=statb_m[0:1, :].partition_broadcast(128)),
             inc=MS, k=16)
        r.wait('sp', MS, MS.n)
        r.sp(lambda e: e.dma_start(out=s2[:],
                                   in_=statb_r[0:1, :].partition_broadcast(128)),
             inc=MS, k=16)
        bc_done = MS.n

        pb_marks = []
        for ec in range(NEC):
            half = xfm2[:, (ec % 2) * 2048:((ec % 2) + 1) * 2048]
            if ec >= 2:
                r.wait('sp', dv_s, pb_marks[ec - 2])
            wp_ = WP[ec % 2]
            r.sp(lambda e, ec=ec, b=half: e.dma_start(
                out=b, in_=x_fm[ec * 128:(ec + 1) * 128, :]), inc=wp_, k=16)
            ld = wp_.n
            r.wait('dve', wp_, ld)
            r.wait('dve', MS, bc_done)
            r.dve(lambda e, b=half: e.tensor_tensor(s3[:, 0:2048], b, s1[:], ALU.add),
                  inc=dv_s)
            acc_col = vec[:, 28 + (ec % 2):29 + (ec % 2)]
            r.dve(lambda e, ec=ec, ac=acc_col: e.scalar_tensor_tensor(
                pool_a[:, ec * 2048:(ec + 1) * 2048], s3[:, 0:2048], 1.0, s2[:],
                ALU.mult, ALU.mult, accum_out=ac), inc=dv_s)
            r.dve(lambda e, ec=ec, ac=acc_col: e.tensor_copy(sxln_b[:, ec:ec + 1], ac),
                  inc=dv_s)
            pb_marks.append(dv_s.n)
        pb_dve = dv_s.n

        for h in range(H):
            r.pool(lambda e, h=h: e.dma_start(
                out=pool_b[:, h * 2048:(h + 1) * 2048], in_=wq_r[h]), inc=GWQ, k=16)
        r.pool(lambda e: e.dma_start(out=mnT_b[:], in_=mnT_r), inc=GWQ, k=16)
        wq_all_ld = GWQ.n

        r.wait('pe', dv_s, pb_dve)
        r.wait('pe', GWQ, wq_all_ld)
        ph_ev = 0
        for h in range(H):
            if ph_ev:
                r.wait('pe', dv_s, ph_ev)
            for ec in range(NEC):
                r.pe(lambda e, h=h, ec=ec: e.matmul(
                    pA0[0:128, 0:1],
                    pool_b[:, h * 2048 + ec * 128: h * 2048 + (ec + 1) * 128],
                    sxln_b[:, ec:ec + 1],
                    start=(ec == 0), stop=(ec == NEC - 1)),
                    inc=(pe_s if ec == NEC - 1 else None))
            ph_mm = pe_s.n
            r.wait('dve', pe_s, ph_mm)
            r.dve(lambda e: e.tensor_copy(ph_b[:], pA0[0:128, 0:1]), inc=dv_s)
            ph_ev = dv_s.n
            r.wait('pe', dv_s, ph_ev)
            r.pe(lambda e, h=h: e.matmul(pA1[0:4, h:h + 1],
                                         mnT_b[:, h * 4:(h + 1) * 4], ph_b[:],
                                         start=True, stop=True), inc=pe_s)
        rel_mm = pe_s.n
        r.wait('dve', pe_s, rel_mm)
        r.dve(lambda e: e.tensor_copy(relp[:], pA1[0:4, 0:16]), inc=dv_s)
        relp_ev = dv_s.n
        r.wait('pool', dv_s, relp_ev)
        r.pool(lambda e: e.dma_start(out=rel_in, in_=relp[:]), inc=GMC, k=16)
        r.wait('pool', GMC, GMC.n)
        r.pool(lambda e: e.collective_compute(
            "AllGather", ALU.bypass, replica_groups=PAIRS,
            ins=[rel_in.opt()], outs=[rel_out.opt()]), inc=cc_s, k=1)
        r.wait('pool', cc_s, cc_s.n)
        r.pool(lambda e: e.dma_start(out=relp[:], in_=rel_out[0:4, :]), inc=GMC, k=16)
        r.pool(lambda e: e.dma_start(out=relB[:], in_=rel_out[4:8, :]), inc=GMC, k=16)
        rel_back = GMC.n
        r.wait('dve', GMC, rel_back)
        r.dve(lambda e: e.tensor_tensor(relp[:], relp[:], relB[:], ALU.add), inc=dv_s)
        rel_sum = dv_s.n

        r.wait('pe', dv_s, relp_ev)
        for h in range(H):
            r.pe(lambda e, h=h: e.matmul(pB0[0:4, 0:1], mnT_b[:, h * 4:(h + 1) * 4],
                                         onesb[:, 0:1], start=(h == 0),
                                         stop=(h == H - 1)),
                 inc=(pe_s if h == H - 1 else None))
        mn_mm = pe_s.n
        r.wait('act', pe_s, mn_mm)
        r.act(lambda e: e.activation(vec[0:4, 30:31], pB0[0:4, 0:1], AF.Sign,
                                     bias=C_NEGEPSA[0:4, :], scale=1.0), inc=ac_s)
        sign_mark = ac_s.n
        r.wait('dve', ac_s, sign_mark)
        r.dve(lambda e: e.tensor_scalar(vec[0:4, 30:31], vec[0:4, 30:31], 1.0, 0.5,
                                        ALU.add, ALU.mult), inc=dv_s)
        r.dve(lambda e: e.tensor_scalar(vec[0:4, 31:32], vec[0:4, 30:31], 1.0, NEG_BIG,
                                        ALU.subtract, ALU.mult), inc=dv_s)
        r.wait('dve', dv_s, rel_sum)
        r.dve(lambda e: e.tensor_scalar(relp[:], relp[:], float(SCALE / np.float32(S)),
                                        vec[0:4, 31:32], ALU.mult, ALU.add), inc=dv_s)
        reladj = dv_s.n
        r.wait('pe', dv_s, reladj)
        r.pe(lambda e: e.transpose(pB0[0:16, 0:4], relp[:], idf[0:4, 0:4]), inc=pe_s)
        r.pe(lambda e: e.transpose(pB0[0:1, 16:20], vec[0:4, 30:31], idf[0:4, 0:4]),
             inc=pe_s)
        trel = pe_s.n
        r.wait('dve', pe_s, trel)
        r.dve(lambda e: e.tensor_copy(relT[:], pB0[0:16, 0:4]), inc=dv_s)
        r.dve(lambda e: e.tensor_copy(mrow[:], pB0[0:1, 16:20]), inc=dv_s)
        r.dve(lambda e: e.tensor_reduce(vec[0:16, 32:33], relT[:], AX.X, ALU.max), inc=dv_s)
        r.dve(lambda e: e.tensor_scalar(relT[:], relT[:], vec[0:16, 32:33], None,
                                        ALU.subtract), inc=dv_s)
        sm1 = dv_s.n
        r.wait('act', dv_s, sm1)
        r.act(lambda e: e.activation(relT2[:], relT[:], AF.Exp, bias=0.0, scale=1.0),
              inc=ac_s)
        sm2 = ac_s.n
        r.wait('dve', ac_s, sm2)
        r.dve(lambda e: e.tensor_reduce(vec[0:16, 32:33], relT2[:], AX.X, ALU.add), inc=dv_s)
        r.dve(lambda e: e.reciprocal(vec[0:16, 33:34], vec[0:16, 32:33]), inc=dv_s)
        r.dve(lambda e: e.tensor_scalar(relT2[:], relT2[:], vec[0:16, 33:34], None,
                                        ALU.mult), inc=dv_s)
        r.dve(lambda e: e.tensor_copy(mrow_b[:], mrow[:]), inc=dv_s)
        mrb = dv_s.n
        r.wait('pe', dv_s, mrb)
        r.pe(lambda e: e.matmul(pB1[0:16, 0:4], onesb[0:1, 0:16], mrow_b[:],
                                start=True, stop=True), inc=pe_s)
        mi_mm = pe_s.n
        r.wait('dve', pe_s, mi_mm)
        r.dve(lambda e: e.tensor_tensor(wgrow[:], relT2[:], pB1[0:16, 0:4], ALU.mult),
              inc=dv_s)
        wg_done = dv_s.n
        r.wait('sp', dv_s, wg_done)
        r.wait('sp', MS, MS.n)
        r.sp(lambda e: e.dma_start(
            out=wgb_d.rearrange("a (p c) -> (a p) c", c=4), in_=wgrow[:]),
            inc=MS, k=16)
        wg_out = MS.n
        r.wait('sp', MS, wg_out)
        r.wait('sp', MS, MS.n)
        r.sp(lambda e: e.dma_start(out=Wb[:],
                                   in_=wgb_d[0:1, :].partition_broadcast(128)),
             inc=MS, k=16)
        wb_rep = MS.n

        wk_marks, wv_marks, mem_marks = {}, {}, {}

        def prefetch_head(h, wait_pe=0):
            if wait_pe:
                r.wait('pool', pe_s, wait_pe)
            r.wait('pool', GWK, GWK.n)
            r.wait('pool', GWV, GWV.n)
            r.wait('pool', GME, GME.n)
            r.pool(lambda e, h=h: e.dma_start(out=WKH, in_=wk_r[h]), inc=GWK, k=16)
            wk_marks[h] = GWK.n
            r.pool(lambda e, h=h: e.dma_start(out=WVH, in_=wv_r[h]), inc=GWV, k=16)
            wv_marks[h] = GWV.n
            r.pool(lambda e, h=h: e.dma_start(out=MEMA[h % 2], in_=mem_r[h]),
                   inc=GME, k=16)
            mem_marks[h] = GME.n

        prefetch_head(0)

        last_xln_use_pe = 0
        wb4_done = 0

        def elu_evac(psum, dst_slice, n):
            r.dve(lambda e, p=psum, n=n: e.tensor_scalar(
                s3[:, 0:n], p[:, 0:n], 0.0, None, ALU.min), inc=dv_s)
            t0 = dv_s.n
            r.wait('act', dv_s, t0)
            r.act(lambda e, n=n: e.activation(s3[:, 1024:1024 + n], s3[:, 0:n], AF.Exp,
                                              bias=0.0, scale=1.0), inc=ac_s)
            ee = ac_s.n
            r.wait('dve', ac_s, ee)
            r.dve(lambda e, p=psum, d=dst_slice, n=n: e.scalar_tensor_tensor(
                d, p[:, 0:n], 0.0, s3[:, 1024:1024 + n], ALU.max, ALU.add), inc=dv_s)
            return dv_s.n

        for h in range(H):
            hs = h * 2048
            bank_mark = {0: 0, 1: 0}

            r.wait('pe', ac_s, ac_s.n)
            r.wait('dve', MS, gl_rep)
            for g in range(4):
                pp = pA0 if g % 2 == 0 else pA1
                if bank_mark[g % 2]:
                    r.wait('pe', dv_s, bank_mark[g % 2])
                for ec in range(NEC):
                    r.pe(lambda e, g=g, ec=ec, pp=pp, hs=hs: e.matmul(
                        pp[:, 0:512],
                        pool_b[:, hs + ec * 128: hs + (ec + 1) * 128],
                        pool_a[:, ec * 2048 + g * 512: ec * 2048 + (g + 1) * 512],
                        start=(ec == 0), stop=(ec == NEC - 1)),
                        inc=(pe_s if ec == NEC - 1 else None))
                r.wait('dve', pe_s, pe_s.n)
                bank_mark[g % 2] = elu_evac(pp, SQ[:, g * 512:(g + 1) * 512], 512)
            r.wait('pe', GWK, wk_marks[h])
            for g in range(4):
                pp = pA0 if g % 2 == 0 else pA1
                r.wait('pe', dv_s, bank_mark[g % 2])
                for ec in range(NEC):
                    r.pe(lambda e, g=g, ec=ec, pp=pp: e.matmul(
                        pp[:, 0:512],
                        WKH[:, ec * 128:(ec + 1) * 128],
                        pool_a[:, ec * 2048 + g * 512: ec * 2048 + (g + 1) * 512],
                        start=(ec == 0), stop=(ec == NEC - 1)),
                        inc=(pe_s if ec == NEC - 1 else None))
                r.wait('dve', pe_s, pe_s.n)
                bank_mark[g % 2] = elu_evac(pp, KPF[:, g * 512:(g + 1) * 512], 512)
            tr_mark = {0: 0, 1: 0}
            r.wait('pe', dv_s, bank_mark[0])
            r.wait('pe', dv_s, bank_mark[1])
            for c in range(NCHUNK):
                pt = pTa if c % 2 == 0 else pTb
                if tr_mark[c % 2]:
                    r.wait('pe', dv_s, tr_mark[c % 2])
                r.pe(lambda e, c=c, pt=pt: e.transpose(
                    pt[:, 0:128], KPF[:, c * 128:(c + 1) * 128], idb[:]), inc=pe_s)
                r.wait('dve', pe_s, pe_s.n)
                r.dve(lambda e, c=c, pt=pt: e.tensor_copy(
                    KPT[:, c * 128:(c + 1) * 128], pt[:, 0:128]), inc=dv_s)
                tr_mark[c % 2] = dv_s.n
            r.wait('pe', GWV, wv_marks[h])
            for c in range(NCHUNK):
                pp = pA0 if c % 2 == 0 else pA1
                r.wait('pe', dv_s, bank_mark[c % 2])
                for ec in range(NEC):
                    r.pe(lambda e, c=c, ec=ec, pp=pp: e.matmul(
                        pp[:, 0:128],
                        pool_a[:, ec * 2048 + c * 128: ec * 2048 + (c + 1) * 128],
                        WVH[:, ec * 128:(ec + 1) * 128],
                        start=(ec == 0), stop=(ec == NEC - 1)),
                        inc=(pe_s if ec == NEC - 1 else None))
                r.wait('dve', pe_s, pe_s.n)
                r.dve(lambda e, c=c, pp=pp: e.tensor_copy(
                    VAUG[:, c * 129:c * 129 + 128], pp[:, 0:128]), inc=dv_s)
                bank_mark[c % 2] = dv_s.n
            vaug_done = dv_s.n
            if h == H - 1:
                last_xln_use_pe = pe_s.n

            r.wait('pe', dv_s, vaug_done)
            for c in range(NCHUNK):
                r.pe(lambda e, c=c: e.matmul(
                    pA0[:, 256:385], KPT[:, c * 128:(c + 1) * 128],
                    VAUG[:, c * 129:(c + 1) * 129],
                    start=(c == 0), stop=(c == NCHUNK - 1)),
                    inc=(pe_s if c == NCHUNK - 1 else None))
            kvt_mm = pe_s.n
            r.wait('dve', pe_s, kvt_mm)
            r.dve(lambda e: e.tensor_copy(totf, pA0[:, 256:385]), inc=dv_s)
            tot_ev = dv_s.n
            r.wait('pool', dv_s, tot_ev)
            r.pool(lambda e, h=h: e.dma_start(out=sta_in[h], in_=totf), inc=GST, k=16)
            tot_dma = GST.n
            r.wait('pool', GST, tot_dma)
            r.pool(lambda e, h=h: e.collective_compute(
                "AllGather", ALU.bypass, replica_groups=PAIRS,
                ins=[sta_in[h].opt()], outs=[sta_out[h].opt()]), inc=cc_s, k=1)
            r.wait('pool', cc_s, cc_s.n)
            r.pool(lambda e, h=h: e.dma_start(out=recvf, in_=sta_out[h][0:128, :]),
                   inc=GRC, k=16)
            recv_done = GRC.n
            r.wait('dve', GST, tot_dma)
            r.dve(lambda e: e.memset(totf, 0.0), inc=dv_s)
            r.dve(lambda e: e.memset(state_b, 0.0), inc=dv_s)
            r.wait('dve', GRC, recv_done)
            r.dve(lambda e: e.tensor_scalar(partner_b, recvf, flagv[:, 0:1],
                                            None, ALU.mult), inc=dv_s)
            r.wait('dve', MS, wb_rep)
            if wb4_done:
                r.wait('dve', ac_s, ac_s.n)
            r.dve(lambda e, h=h: e.tensor_copy(Wb4[:], Wb[:, h * 4:(h + 1) * 4]), inc=dv_s)
            wb4_done = dv_s.n
            pre_loop_dve = dv_s.n

            if h + 1 < H:
                prefetch_head(h + 1, wait_pe=kvt_mm)

            bankA = [pA0, pA1]
            bankB = [pB0, pB1]
            bankC = [pC0, pC1]
            mk_scores = [0, 0]
            mk_B = [0, 0]
            mk_memA = [0, 0]
            mk_memD = [0, 0]
            mk_tr = [tr_mark[0], tr_mark[1]]
            mk_d5b = pre_loop_dve
            mk_d4 = 0
            s9_prev = 0
            r.wait('pe', dv_s, pre_loop_dve)
            r.wait('pe', GME, mem_marks[h])
            for c in range(NCHUNK):
                par = c % 2
                pA_, pB_, pC_ = bankA[par], bankB[par], bankC[par]
                cs = c * 128
                if mk_scores[par]:
                    r.wait('pe', dv_s, mk_scores[par])
                r.pe(lambda e, pA_=pA_, cs=cs: e.matmul(
                    pA_[:, 0:128], KPF[:, cs:cs + 128], SQ[:, cs:cs + 128],
                    start=True, stop=True), inc=pe_s)
                s1_mm = pe_s.n
                r.pe(lambda e, pA_=pA_, c=c: e.matmul(
                    pA_[:, 256:385], KPT[:, c * 128:(c + 1) * 128],
                    VAUG[:, c * 129:(c + 1) * 129],
                    start=True, stop=True), inc=pe_s)
                s10_mm = pe_s.n
                if mk_memA[par] or mk_memD[par]:
                    r.wait('pe', ac_s, mk_memA[par])
                    r.wait('pe', dv_s, mk_memD[par])
                for m in range(M):
                    r.pe(lambda e, pC_=pC_, cs=cs, m=m, h=h: e.matmul(
                        pC_[:, m * 128:(m + 1) * 128], SQ[:, cs:cs + 128],
                        MEMA[h % 2][:, m * 128:(m + 1) * 128],
                        start=True, stop=True), inc=(pe_s if m == M - 1 else None))
                s6_mm = pe_s.n
                if mk_B[par]:
                    r.wait('pe', dv_s, mk_B[par])
                r.pe(lambda e, pB_=pB_, cs=cs, h=h: e.matmul(
                    pB_[:, 256:260], SQ[:, cs:cs + 128], mnT_b[:, h * 4:(h + 1) * 4],
                    start=True, stop=True), inc=pe_s)
                r.wait('pe', dv_s, mk_d5b)
                r.pe(lambda e, pB_=pB_, cs=cs: e.matmul(
                    pB_[:, 0:129], SQ[:, cs:cs + 128], state_b,
                    start=True, stop=False), inc=pe_s)
                s3_mm = pe_s.n
                r.pe(lambda e, pB_=pB_, cs=cs: e.matmul(
                    pB_[:, 0:129], SQ[:, cs:cs + 128], partner_b,
                    start=False, stop=False), inc=pe_s)
                r.wait('dve', pe_s, s1_mm)
                r.dve(lambda e, pA_=pA_: e.tensor_tensor(scb16[:], pA_[:, 0:128],
                                                         maskT[:], ALU.mult), inc=dv_s)
                d1 = dv_s.n
                r.wait('pe', dv_s, d1)
                r.pe(lambda e, pB_=pB_, c=c: e.matmul(
                    pB_[:, 0:129], scb16[:], VAUG[:, c * 129:(c + 1) * 129],
                    start=False, stop=True), inc=pe_s)
                s4_mm = pe_s.n
                r.wait('dve', pe_s, s10_mm)
                r.dve(lambda e, pA_=pA_: e.tensor_tensor(totf, totf,
                                                         pA_[:, 256:385], ALU.add),
                      inc=dv_s)
                kv_read = dv_s.n
                mk_scores[par] = max(d1, kv_read)
                r.wait('dve', pe_s, s3_mm)
                r.dve(lambda e: e.tensor_copy(state_b, totf), inc=dv_s)
                mk_d5b = dv_s.n
                r.wait('dve', pe_s, s6_mm)
                r.dve(lambda e, pB_=pB_: e.tensor_scalar(vec[:, 34:38], pB_[:, 256:260],
                                                         EPS_A, None, ALU.max), inc=dv_s)
                r.dve(lambda e: e.reciprocal(vec[:, 38:42], vec[:, 34:38]), inc=dv_s)
                r.dve(lambda e: e.tensor_tensor(vec[:, 34:38], vec[:, 38:42], Wb4[:],
                                                ALU.mult), inc=dv_s)
                f_done = dv_s.n
                r.wait('act', dv_s, max(f_done, mk_d4))
                r.act(lambda e, pC_=pC_: e.activation(mem_tm[:], pC_[:, 0:128],
                                                      AF.Identity, bias=0.0,
                                                      scale=vec[:, 34:35]), inc=ac_s)
                mk_memA[par] = ac_s.n
                r.wait('dve', ac_s, mk_memA[par])
                for m in range(1, M):
                    r.dve(lambda e, pC_=pC_, m=m: e.scalar_tensor_tensor(
                        mem_tm[:], pC_[:, m * 128:(m + 1) * 128], vec[:, 34 + m:35 + m],
                        mem_tm[:], ALU.mult, ALU.add), inc=dv_s)
                mk_memD[par] = dv_s.n
                r.wait('dve', pe_s, s4_mm)
                r.dve(lambda e, pB_=pB_: e.tensor_scalar(T_DEN, pB_[:, 128:129],
                                                         EPS_A, None, ALU.max), inc=dv_s)
                r.dve(lambda e: e.reciprocal(T_R, T_DEN), inc=dv_s)
                r.dve(lambda e, pB_=pB_: e.tensor_scalar(loc_tm[:], pB_[:, 0:128],
                                                         T_R, None, ALU.mult), inc=dv_s)
                mk_B[par] = dv_s.n
                if s9_prev:
                    r.wait('dve', pe_s, s9_prev)
                r.dve(lambda e, h=h: e.tensor_scalar(t2_tm, mem_tm[:],
                                                     gWb[:, h:h + 1], None, ALU.mult),
                      inc=dv_s)
                r.dve(lambda e, h=h: e.scalar_tensor_tensor(
                    outc_b[:], loc_tm[:], gLb[:, h:h + 1], t2_tm,
                    ALU.mult, ALU.add), inc=dv_s)
                mk_d4 = dv_s.n
                pt = pTa if par == 0 else pTb
                r.wait('pe', dv_s, mk_d4)
                if mk_tr[par]:
                    r.wait('pe', dv_s, mk_tr[par])
                r.pe(lambda e, pt=pt: e.transpose(pt[:, 0:128], outc_b[:], idb[:]),
                     inc=pe_s)
                s9_prev = pe_s.n
                r.wait('dve', pe_s, s9_prev)
                r.dve(lambda e, pt=pt, hs=hs, cs=cs: e.tensor_copy(
                    pool_b[:, hs + cs: hs + cs + 128], pt[:, 0:128]), inc=dv_s)
                mk_tr[par] = dv_s.n
        pc_dve = dv_s.n
        pc_act = ac_s.n

        wo_all = 0
        r.wait('pool', pe_s, last_xln_use_pe)
        for hh in range(H):
            r.pool(lambda e, hh=hh: e.dma_start(
                out=pool_a[:, hh * 2048:(hh + 1) * 2048], in_=wo_r[hh]), inc=GWO, k=16)
        wo_all = GWO.n

        wf_marks = []
        for dc in range(NEC):
            half = xfm2[:, (dc % 2) * 2048:((dc % 2) + 1) * 2048]
            if dc >= 2:
                r.wait('sp', dv_s, wf_marks[dc - 2])
            else:
                r.wait('sp', dv_s, pb_dve)
            wp_ = WP[dc % 2]
            r.sp(lambda e, dc=dc, b=half: e.dma_start(out=b, in_=wf_r[dc]),
                 inc=wp_, k=16)
            ld = wp_.n
            r.wait('dve', wp_, ld)
            r.dve(lambda e, b=half: e.tensor_reduce(AW_T, b, AX.X, ALU.max,
                                                    apply_absolute_value=True), inc=dv_s)
            r.dve(lambda e: e.tensor_tensor(AW_ACC, AW_ACC, AW_T, ALU.max), inc=dv_s)
            wf_marks.append(dv_s.n)
        amaxw_done = dv_s.n

        r.wait('pe', dv_s, pc_dve)
        r.wait('pe', ac_s, pc_act)
        r.wait('pe', GWO, wo_all)
        x1_sp_marks = []
        x1_dve_marks = []
        pd_evac_mark = 0
        wo_mm = 0
        banks4 = [pA0, pA1, pB0, pB1]
        for t in range(NCHUNK):
            cs = t * 128
            xb = xst[:, 0:2048] if t % 2 == 0 else wpf
            if t >= 2:
                r.wait('sp', dv_s, x1_dve_marks[t - 2])
            xp = XP[t % 2]
            r.sp(lambda e, t=t, b=xb: e.dma_start(
                out=b[:, 0:2048], in_=x_tm[t * 128:(t + 1) * 128, :]), inc=xp, k=16)
            x_ld = xp.n
            if pd_evac_mark:
                r.wait('pe', ac_s, pd_evac_mark)
            for hh in range(H):
                for ng in range(4):
                    r.pe(lambda e, hh=hh, ng=ng, cs=cs, bk=banks4[ng]: e.matmul(
                        bk[:, 0:512],
                        pool_b[:, hh * 2048 + cs: hh * 2048 + cs + 128],
                        pool_a[:, hh * 2048 + ng * 512: hh * 2048 + (ng + 1) * 512],
                        start=(hh == 0), stop=(hh == H - 1)),
                        inc=(pe_s if (hh == H - 1 and ng == 3) else None))
            wo_mm = pe_s.n
            r.wait('act', pe_s, wo_mm)
            if t >= 1:
                r.wait('act', dv_s, x1_dve_marks[-1])
            for ng in range(4):
                r.act(lambda e, ng=ng, bk=banks4[ng]: e.activation(
                    s1[:, ng * 512:(ng + 1) * 512], bk[:, 0:512], AF.Identity,
                    accum_out=vec[:, 45 + ng:46 + ng]), inc=ac_s)
            pd_evac_mark = ac_s.n
            r.wait('dve', ac_s, pd_evac_mark)
            r.dve(lambda e: e.tensor_tensor(vec[:, 45:46], vec[:, 45:46],
                                            vec[:, 46:47], ALU.add), inc=dv_s)
            r.dve(lambda e: e.tensor_tensor(vec[:, 47:48], vec[:, 47:48],
                                            vec[:, 48:49], ALU.add), inc=dv_s)
            r.dve(lambda e: e.tensor_tensor(vec[:, 45:46], vec[:, 45:46],
                                            vec[:, 47:48], ALU.add), inc=dv_s)
            r.dve(lambda e: e.tensor_scalar(T_NEGM, vec[:, 45:46], -1.0 / 2048.0, None,
                                            ALU.mult), inc=dv_s)
            nm2 = dv_s.n
            r.wait('act', dv_s, nm2)
            r.act(lambda e: e.activation(s2[:, 0:2048], s1[:, 0:2048], AF.Square,
                                         accum_out=T_SSQ), inc=ac_s)
            ssq2 = ac_s.n
            r.wait('dve', ac_s, ssq2)
            r.dve(lambda e: e.tensor_scalar(T_MSQ, T_NEGM, T_NEGM, None, ALU.mult), inc=dv_s)
            r.dve(lambda e: e.tensor_scalar(T_T0, T_SSQ, 1.0 / 2048.0, None, ALU.mult), inc=dv_s)
            r.dve(lambda e: e.tensor_tensor(T_T0, T_T0, T_MSQ, ALU.subtract), inc=dv_s)
            v2m = dv_s.n
            r.wait('act', dv_s, v2m)
            r.act(lambda e: e.activation(T_T0, T_T0, AF.Ln, bias=C_EPSLN, scale=1.0),
                  inc=ac_s)
            r.act(lambda e: e.activation(T_RSTD, T_T0, AF.Exp, bias=0.0,
                                         scale=C_NEGHALF), inc=ac_s)
            r2m = ac_s.n
            r.wait('dve', ac_s, r2m)
            r.dve(lambda e: e.tensor_scalar(s2[:, 0:2048], s1[:, 0:2048], T_NEGM,
                                            T_RSTD, ALU.add, ALU.mult), inc=dv_s)
            r.wait('dve', XP[t % 2], x_ld)
            if t >= 1:
                r.wait('dve', XSTP[(t - 1) % 2], x1_sp_marks[-1])
            r.dve(lambda e, b=xb: e.scalar_tensor_tensor(
                s3[:, 0:2048], s2[:, 0:2048], 1.0, b[:, 0:2048], ALU.mult, ALU.add,
                accum_out=vec[:, 49:50]), inc=dv_s)
            x1_mark = dv_s.n
            r.wait('sp', dv_s, x1_mark)
            xstp = XSTP[t % 2]
            r.sp(lambda e, t=t: e.dma_start(out=x1_d[t * 128:(t + 1) * 128, :],
                                            in_=s3[:, 0:2048]), inc=xstp, k=16)
            x1_sp_marks.append(xstp.n)
            r.dve(lambda e: e.tensor_scalar(T_NEGM, vec[:, 49:50], -1.0 / 2048.0, None,
                                            ALU.mult), inc=dv_s)
            r.dve(lambda e: e.tensor_scalar(s2[:, 0:2048], s3[:, 0:2048], T_NEGM, None,
                                            ALU.add), inc=dv_s)
            d3m = dv_s.n
            r.dve(lambda e: e.tensor_reduce(T_DEV, s2[:, 0:2048], AX.X, ALU.max,
                                            apply_absolute_value=True), inc=dv_s)
            r.wait('act', dv_s, d3m)
            r.act(lambda e: e.activation(s1[:, 0:2048], s2[:, 0:2048], AF.Square,
                                         accum_out=T_SSQ), inc=ac_s)
            sq3 = ac_s.n
            r.wait('dve', ac_s, sq3)
            r.dve(lambda e: e.tensor_scalar(T_T0, T_SSQ, 1.0 / 2048.0, None, ALU.mult),
                  inc=dv_s)
            v3m = dv_s.n
            r.wait('act', dv_s, v3m)
            r.act(lambda e: e.activation(T_T0, T_T0, AF.Ln, bias=C_EPSLN, scale=1.0),
                  inc=ac_s)
            r.act(lambda e: e.activation(T_RSTD, T_T0, AF.Exp, bias=0.0,
                                         scale=C_NEGHALF), inc=ac_s)
            r3m = ac_s.n
            r.wait('dve', ac_s, r3m)
            r.dve(lambda e: e.tensor_tensor(T_DEV, T_DEV, T_RSTD, ALU.mult), inc=dv_s)
            r.dve(lambda e: e.tensor_tensor(AMX_ACC, AMX_ACC, T_DEV, ALU.max), inc=dv_s)
            x1_dve_marks.append(dv_s.n)
        pd_dve = dv_s.n
        pd_pe = pe_s.n

        r.wait('pe', dv_s, pd_dve)
        r.pe(lambda e: e.transpose(pC0[0:1, 0:128], AMX_ACC, idf[:]), inc=pe_s)
        amt = pe_s.n
        r.wait('dve', pe_s, amt)
        r.wait('dve', XSTP[0], XSTP[0].n)
        r.wait('dve', XSTP[1], XSTP[1].n)
        r.dve(lambda e: e.tensor_copy(s3[0:1, 0:128], pC0[0:1, 0:128]), inc=dv_s)
        r.dve(lambda e: e.tensor_reduce(SC_AX[0:1, :], s3[0:1, 0:128], AX.X, ALU.max),
              inc=dv_s)
        amx_mark = dv_s.n
        r.wait('pool', dv_s, amx_mark)
        r.pool(lambda e: e.dma_start(out=am_in, in_=SC_AX[0:1, :]), inc=GST, k=16)
        r.wait('pool', GST, GST.n)
        r.pool(lambda e: e.collective_compute(
            "AllReduce", ALU.max, replica_groups=ALL8,
            ins=[am_in.opt()], outs=[am_out.opt()]), inc=cc_s, k=1)
        r.wait('pool', cc_s, cc_s.n)
        r.pool(lambda e: e.dma_start(out=SC_AX[0:1, :], in_=am_out), inc=GRC, k=16)
        amax_back = GRC.n

        r.wait('pe', dv_s, max(amx_mark, amaxw_done))
        r.pe(lambda e: e.transpose(pC0[0:1, 0:128], AW_ACC, idf[:]), inc=pe_s)
        awt = pe_s.n
        r.wait('dve', pe_s, awt)
        r.dve(lambda e: e.tensor_copy(s3[0:1, 0:128], pC0[0:1, 0:128]), inc=dv_s)
        r.dve(lambda e: e.tensor_reduce(SC_AW[0:1, :], s3[0:1, 0:128], AX.X, ALU.max),
              inc=dv_s)
        r.wait('dve', GRC, amax_back)
        r.dve(lambda e: e.tensor_scalar(SC_SX[0:1, :], SC_AX[0:1, :], RFP8_C, None,
                                        ALU.mult), inc=dv_s)
        r.dve(lambda e: e.tensor_tensor(SC_SW[0:1, :], SC_AX[0:1, :], SC_AW[0:1, :],
                                        ALU.max), inc=dv_s)
        r.dve(lambda e: e.tensor_scalar(SC_SW[0:1, :], SC_SW[0:1, :], RFP8_C, None,
                                        ALU.mult), inc=dv_s)
        r.dve(lambda e: e.tensor_tensor(SC_A[0:1, :], SC_SX[0:1, :], SC_SW[0:1, :],
                                        ALU.mult), inc=dv_s)
        r.dve(lambda e: e.reciprocal(scrow[0:1, 3:4], SC_A[0:1, :]), inc=dv_s)
        r.dve(lambda e: e.reciprocal(scrow[0:1, 0:1], SC_SX[0:1, :]), inc=dv_s)
        r.dve(lambda e: e.tensor_scalar(scrow[0:1, 0:1], scrow[0:1, 0:1], 0.5, None,
                                        ALU.mult), inc=dv_s)
        r.dve(lambda e: e.reciprocal(scrow[0:1, 1:2], SC_SW[0:1, :]), inc=dv_s)
        r.dve(lambda e: e.tensor_scalar(scrow[0:1, 1:2], scrow[0:1, 1:2], 0.5, None,
                                        ALU.mult), inc=dv_s)
        r.dve(lambda e: e.tensor_copy(scrow[0:1, 2:3], SC_A[0:1, :]), inc=dv_s)
        sc_mark = dv_s.n
        r.wait('sp', dv_s, sc_mark)
        r.wait('sp', MS, MS.n)
        r.sp(lambda e: e.dma_start(out=scb_d[0:1, 0:4], in_=scrow[0:1, 0:4]), inc=MS, k=16)
        r.wait('sp', MS, MS.n)
        r.wait('sp', MS, MS.n)
        r.sp(lambda e: e.dma_start(out=vec[:, 22:26],
                                   in_=scb_d[0:1, 0:4].partition_broadcast(128)),
             inc=MS, k=16)
        sc_rep = MS.n

        def quant_tail(dst_slice):
            r.dve(lambda e: e.tensor_copy(fp8t, s2[:, 0:2048]), inc=dv_s)
            f8 = dv_s.n
            r.wait('act', dv_s, f8)
            r.act(lambda e: e.activation(s1[:, 0:2048], fp8t, AF.Identity,
                                         bias=0.0, scale=C_TWO), inc=ac_s)
            big = ac_s.n
            r.dve(lambda e: e.tensor_tensor(s3[:, 0:2048], s2[:, 0:2048],
                                            s2[:, 0:2048], ALU.mult), inc=dv_s)
            r.dve(lambda e: e.tensor_scalar(msku, s3[:, 0:2048], float(2.0 ** -12),
                                            None, ALU.is_lt), inc=dv_s)
            r.dve(lambda e: e.tensor_scalar(s3[:, 0:2048], s2[:, 0:2048], 1024.0,
                                            MAGIC, ALU.mult, ALU.add), inc=dv_s)
            r.dve(lambda e: e.tensor_scalar(s3[:, 0:2048], s3[:, 0:2048], MAGIC,
                                            1.0 / 512.0, ALU.subtract, ALU.mult), inc=dv_s)
            r.wait('dve', ac_s, big)
            r.dve(lambda e, d=dst_slice: e.select(d, msku, s3[:, 0:2048],
                                                  s1[:, 0:2048], add_drain=True), inc=dv_s)
            return ac_s.n, dv_s.n

        r.wait('dve', XSTP[0], XSTP[0].n)
        r.wait('dve', XSTP[1], XSTP[1].n)
        r.wait('dve', pe_s, pd_pe)
        qw_marks = []
        qw_act = 0
        for dc in range(NEC):
            half = xfm2[:, (dc % 2) * 2048:((dc % 2) + 1) * 2048]
            if dc >= 2:
                r.wait('sp', dv_s, qw_marks[dc - 2])
            else:
                r.wait('sp', dv_s, amaxw_done)
            wp_ = WP[dc % 2]
            r.sp(lambda e, dc=dc, b=half: e.dma_start(out=b, in_=wf_r[dc]),
                 inc=wp_, k=16)
            ld = wp_.n
            r.wait('dve', wp_, ld)
            r.wait('dve', MS, sc_rep)
            r.dve(lambda e, b=half: e.tensor_scalar(s2[:, 0:2048], b, HRSW, None,
                                                    ALU.mult), inc=dv_s)
            qw_act, mrk = quant_tail(pool_b[:, dc * 2048:(dc + 1) * 2048])
            qw_marks.append(mrk)
        qw_all = dv_s.n

        r.wait('sp', dv_s, qw_all)
        r.wait('sp', ac_s, qw_act)
        r.wait('sp', MS, MS.n)
        r.sp(lambda e: e.dma_start(out=s1[0:1, 0:2048], in_=bias_r), inc=MS, k=16)
        b_ld = MS.n
        r.wait('dve', MS, b_ld)
        r.dve(lambda e: e.tensor_scalar(s2[0:1, 0:2048], s1[0:1, 0:2048],
                                        RAVEC[0:1, :], None, ALU.mult), inc=dv_s)
        r.dve(lambda e: e.tensor_copy(BIAS_HI, s2[0:1, 0:2048]), inc=dv_s)
        r.dve(lambda e: e.tensor_copy(s3[0:1, 0:2048], BIAS_HI), inc=dv_s)
        r.dve(lambda e: e.tensor_tensor(s2[0:1, 0:2048], s2[0:1, 0:2048],
                                        s3[0:1, 0:2048], ALU.subtract), inc=dv_s)
        r.dve(lambda e: e.tensor_copy(BIAS_LO, s2[0:1, 0:2048]), inc=dv_s)
        bias_mark = dv_s.n

        e3_sp_marks = []
        e3_dve_marks = []
        e3_evac = 0
        ff_mm = 0
        for t in range(NCHUNK):
            cs = t * 128
            xb = xst[:, 0:2048] if t % 2 == 0 else wpf
            if t >= 2:
                r.wait('sp', dv_s, e3_dve_marks[t - 2])
            else:
                r.wait('sp', dv_s, pd_dve)
                r.wait('sp', XSTP[0], XSTP[0].n)
                r.wait('sp', XSTP[1], XSTP[1].n)
            xp = XP[t % 2]
            r.sp(lambda e, t=t, b=xb: e.dma_start(
                out=b[:, 0:2048], in_=x1_d[t * 128:(t + 1) * 128, :]), inc=xp, k=16)
            ld = xp.n
            r.wait('dve', xp, ld)
            if t >= 1:
                r.wait('dve', OUTP[(t - 1) % 2], e3_sp_marks[-1])
            r.dve(lambda e, b=xb: e.tensor_reduce(T_SUM, b[:, 0:2048], AX.X, ALU.add),
                  inc=dv_s)
            r.dve(lambda e: e.tensor_scalar(T_NEGM, T_SUM, -1.0 / 2048.0, None,
                                            ALU.mult), inc=dv_s)
            nm3 = dv_s.n
            r.wait('act', XP[t % 2], ld)
            r.wait('act', dv_s, nm3)
            r.act(lambda e, b=xb: e.activation(s1[:, 0:2048], b[:, 0:2048], AF.Square,
                                               accum_out=T_SSQ), inc=ac_s)
            sq3b = ac_s.n
            r.wait('dve', ac_s, sq3b)
            r.dve(lambda e: e.tensor_scalar(T_MSQ, T_NEGM, T_NEGM, None, ALU.mult), inc=dv_s)
            r.dve(lambda e: e.tensor_scalar(T_T0, T_SSQ, 1.0 / 2048.0, None, ALU.mult), inc=dv_s)
            r.dve(lambda e: e.tensor_tensor(T_T0, T_T0, T_MSQ, ALU.subtract), inc=dv_s)
            v3b = dv_s.n
            r.wait('act', dv_s, v3b)
            r.act(lambda e: e.activation(T_T0, T_T0, AF.Ln, bias=C_EPSLN, scale=1.0),
                  inc=ac_s)
            r.act(lambda e: e.activation(T_RSTD, T_T0, AF.Exp, bias=0.0,
                                         scale=C_NEGHALF), inc=ac_s)
            r3b = ac_s.n
            r.wait('dve', ac_s, r3b)
            r.dve(lambda e: e.tensor_tensor(T_T0, T_RSTD, HRSX, ALU.mult), inc=dv_s)
            r.dve(lambda e, b=xb: e.tensor_scalar(s2[:, 0:2048], b[:, 0:2048], T_NEGM,
                                                  T_T0, ALU.add, ALU.mult), inc=dv_s)
            _, qx_mark = quant_tail(SQ)
            r.wait('pe', dv_s, qx_mark)
            trm = {0: 0, 1: 0}
            for ec in range(NEC):
                pt = pTa if ec % 2 == 0 else pTb
                if trm[ec % 2]:
                    r.wait('pe', dv_s, trm[ec % 2])
                r.pe(lambda e, ec=ec, pt=pt: e.transpose(
                    pt[:, 0:128], SQ[:, ec * 128:(ec + 1) * 128], idb[:]), inc=pe_s)
                r.wait('dve', pe_s, pe_s.n)
                r.dve(lambda e, ec=ec, pt=pt, cs=cs: e.tensor_copy(
                    pool_a[:, ec * 2048 + cs: ec * 2048 + cs + 128], pt[:, 0:128]),
                    inc=dv_s)
                trm[ec % 2] = dv_s.n
            qxf_mark = dv_s.n
            r.wait('pe', dv_s, max(qxf_mark, qw_all, bias_mark))
            if e3_evac:
                r.wait('pe', ac_s, e3_evac)
            for ec in range(NEC):
                for ng in range(4):
                    r.pe(lambda e, ec=ec, ng=ng, cs=cs, bk=banks4[ng]: e.matmul(
                        bk[:, 0:512],
                        pool_a[:, ec * 2048 + cs: ec * 2048 + cs + 128],
                        pool_b[:, ec * 2048 + ng * 512: ec * 2048 + (ng + 1) * 512],
                        start=(ec == 0), stop=False), inc=None)
            for ng in range(4):
                r.pe(lambda e, ng=ng, bk=banks4[ng]: e.matmul(
                    bk[:, 0:512], onesb[0:1, 0:128],
                    BIAS_HI[0:1, ng * 512:(ng + 1) * 512],
                    start=False, stop=False), inc=None)
                r.pe(lambda e, ng=ng, bk=banks4[ng]: e.matmul(
                    bk[:, 0:512], onesb[0:1, 0:128],
                    BIAS_LO[0:1, ng * 512:(ng + 1) * 512],
                    start=False, stop=True), inc=(pe_s if ng == 3 else None))
            ff_mm = pe_s.n
            r.wait('act', pe_s, ff_mm)
            for ng in range(4):
                r.act(lambda e, ng=ng, bk=banks4[ng]: e.activation(
                    s1[:, ng * 512:(ng + 1) * 512], bk[:, 0:512], AF.Identity,
                    bias=0.0, scale=AVEC, accum_out=vec[:, 45 + ng:46 + ng]), inc=ac_s)
            e3_evac = ac_s.n
            r.wait('dve', ac_s, e3_evac)
            r.dve(lambda e: e.tensor_tensor(vec[:, 45:46], vec[:, 45:46],
                                            vec[:, 46:47], ALU.add), inc=dv_s)
            r.dve(lambda e: e.tensor_tensor(vec[:, 47:48], vec[:, 47:48],
                                            vec[:, 48:49], ALU.add), inc=dv_s)
            r.dve(lambda e: e.tensor_tensor(vec[:, 45:46], vec[:, 45:46],
                                            vec[:, 47:48], ALU.add), inc=dv_s)
            r.dve(lambda e: e.tensor_scalar(T_M4, vec[:, 45:46], -1.0 / 2048.0, None,
                                            ALU.mult), inc=dv_s)
            m4m = dv_s.n
            r.wait('act', dv_s, m4m)
            r.act(lambda e: e.activation(s2[:, 0:2048], s1[:, 0:2048], AF.Square,
                                         accum_out=T_SSQ), inc=ac_s)
            sq4 = ac_s.n
            r.wait('dve', ac_s, sq4)
            r.dve(lambda e: e.tensor_scalar(T_MSQ, T_M4, T_M4, None, ALU.mult), inc=dv_s)
            r.dve(lambda e: e.tensor_scalar(T_V4, T_SSQ, 1.0 / 2048.0, None, ALU.mult), inc=dv_s)
            r.dve(lambda e: e.tensor_tensor(T_V4, T_V4, T_MSQ, ALU.subtract), inc=dv_s)
            v4m = dv_s.n
            r.wait('act', dv_s, v4m)
            r.act(lambda e: e.activation(T_V4, T_V4, AF.Ln, bias=C_EPSLN, scale=1.0),
                  inc=ac_s)
            r.act(lambda e: e.activation(T_RSTD, T_V4, AF.Exp, bias=0.0,
                                         scale=C_NEGHALF), inc=ac_s)
            r4m = ac_s.n
            r.wait('dve', ac_s, r4m)
            r.dve(lambda e: e.tensor_scalar(s2[:, 0:2048], s1[:, 0:2048], T_M4,
                                            T_RSTD, ALU.add, ALU.mult), inc=dv_s)
            r.dve(lambda e, b=xb: e.tensor_tensor(s3[:, 0:2048], s2[:, 0:2048],
                                                  b[:, 0:2048], ALU.add), inc=dv_s)
            out_mark = dv_s.n
            e3_dve_marks.append(out_mark)
            r.wait('sp', dv_s, out_mark)
            outp = OUTP[t % 2]
            r.sp(lambda e, t=t: e.dma_start(out=out_d[t * 128:(t + 1) * 128, :],
                                            in_=s3[:, 0:2048]), inc=outp, k=16)
            e3_sp_marks.append(outp.n)

        r.wait('sp', OUTP[0], OUTP[0].n)
        r.wait('sp', OUTP[1], OUTP[1].n)
        r.wait('pool', OUTP[0], OUTP[0].n)
        r.wait('pool', OUTP[1], OUTP[1].n)

        @block.tensor
        def _(t):
            r.replay('pe', t)

        @block.vector
        def _(v):
            r.replay('dve', v)

        @block.scalar
        def _(s):
            r.replay('act', s)

        @block.gpsimd
        def _(g):
            r.replay('pool', g)

        @block.sync
        def _(s):
            r.replay('sp', s)

    return nc


def _stage_weight_qkv(w):
    t = w.reshape(H, 128, NEC, 128)
    return np.ascontiguousarray(t.transpose(0, 3, 2, 1)).reshape(H, 128, D).astype(np.float32)


def _stage_weight_out(w):
    return np.ascontiguousarray(w.T.reshape(NEC, 128, D)).astype(np.float32)


def kernel(L0_attn_hidden_states, L0_attn_memories, L0_attn_memory_norms,
           L0_attn_w_q, L0_attn_w_k, L0_attn_w_v, L0_attn_w_o, L0_attn_gate,
           L0_ffn_weight, L0_ffn_bias):
    x = np.ascontiguousarray(np.asarray(L0_attn_hidden_states, dtype=np.float32))
    mem = np.asarray(L0_attn_memories, dtype=np.float32)
    mn = np.asarray(L0_attn_memory_norms, dtype=np.float32)
    wq = np.asarray(L0_attn_w_q, dtype=np.float32)
    wk = np.asarray(L0_attn_w_k, dtype=np.float32)
    wv = np.asarray(L0_attn_w_v, dtype=np.float32)
    wo = np.asarray(L0_attn_w_o, dtype=np.float32)
    gate = np.asarray(L0_attn_gate, dtype=np.float32)
    wf = np.asarray(L0_ffn_weight, dtype=np.float32)
    bias = np.asarray(L0_ffn_bias, dtype=np.float32)

    wq_r = _stage_weight_qkv(wq)
    wk_r = _stage_weight_qkv(wk)
    wv_r = _stage_weight_qkv(wv)
    wo_r = _stage_weight_out(wo)
    wf_r = _stage_weight_out(wf)
    mem_r = np.ascontiguousarray(
        mem.transpose(1, 2, 0, 3).reshape(H, 128, M * 128)).astype(np.float32)
    mnT_r = np.ascontiguousarray(
        mn.transpose(2, 1, 0).reshape(128, H * M)).astype(np.float32)
    gate_r = gate.reshape(1, H).astype(np.float32)
    bias_r = bias.reshape(1, D).astype(np.float32)
    maskT_r = np.triu(np.ones((128, 128), dtype=ml_dtypes.bfloat16))
    idf_r = np.eye(128, dtype=np.float32)
    idb_r = np.eye(128, dtype=ml_dtypes.bfloat16)
    ones_r = np.ones((128, 128), dtype=ml_dtypes.bfloat16)

    nc = build_nc()
    in_maps = []
    for core in range(8):
        b, half = core // 2, core % 2
        xs = np.ascontiguousarray(x[b, half * TOK:(half + 1) * TOK, :])
        in_maps.append({
            "x_tm": xs,
            "x_fm": np.ascontiguousarray(xs.T),
            "wq_r": wq_r, "wk_r": wk_r, "wv_r": wv_r,
            "wo_r": wo_r, "wf_r": wf_r,
            "mem_r": mem_r, "mnT_r": mnT_r,
            "gate_r": gate_r, "bias_r": bias_r,
            "maskT_r": maskT_r, "idf_r": idf_r,
            "idb_r": idb_r, "ones_r": ones_r,
            "flag_r": np.full((1, 1), float(half), dtype=np.float32),
        })

    res = run_bass_kernel_spmd(nc, in_maps, core_ids=list(range(8)))
    out = np.empty((B, S, D), dtype=np.float32)
    for core in range(8):
        b, half = core // 2, core % 2
        out[b, half * TOK:(half + 1) * TOK, :] = res.results[core]["out"]
    return out


if __name__ == "__main__":
    rng = np.random.default_rng(0)
    ins = {
        "L0_attn_hidden_states": rng.standard_normal((B, S, D)).astype(np.float32),
        "L0_attn_memories": rng.standard_normal((M, H, HD, HD)).astype(np.float32),
        "L0_attn_memory_norms": rng.random((M, H, HD)).astype(np.float32),
        "L0_attn_w_q": (rng.standard_normal((D, D)) * 0.02).astype(np.float32),
        "L0_attn_w_k": (rng.standard_normal((D, D)) * 0.02).astype(np.float32),
        "L0_attn_w_v": (rng.standard_normal((D, D)) * 0.02).astype(np.float32),
        "L0_attn_w_o": (rng.standard_normal((D, D)) * 0.02).astype(np.float32),
        "L0_attn_gate": rng.standard_normal((H,)).astype(np.float32),
        "L0_ffn_weight": rng.standard_normal((D, D)).astype(np.float32),
        "L0_ffn_bias": rng.standard_normal((D,)).astype(np.float32),
    }
    y = kernel(**ins)
    print("ok", y.shape, y.dtype, float(np.abs(y).max()))


# version 07
# speedup vs baseline: 1.0430x; 1.0430x over previous
import contextlib

import numpy as np
import ml_dtypes

import concourse.bass as bass
import concourse.mybir as mybir
from concourse.bass_utils import run_bass_kernel_spmd

F32 = mybir.dt.float32
BF16 = mybir.dt.bfloat16
FP8 = mybir.dt.float8e4
U8 = mybir.dt.uint8
AF = mybir.ActivationFunctionType
ALU = mybir.AluOpType
AX = mybir.AxisListType

B, S, D = 4, 4096, 2048
H, HD = 16, 128
M = 4
NCHUNK = 16
NEC = 16
TOK = 2048
SCALE = np.float32(np.float32(HD) ** np.float32(-0.5))
EPS_A = 1e-6
EPS_LN = 1e-5
FP8_C = float(np.float32(448.0) * np.float32(0.8))
MAGIC = float(np.float32(1.5 * 2.0 ** 23))
RFP8_C = float(np.float32(1.0) / np.float32(448.0 * np.float32(0.8)))
NEG_BIG = -1e30
_SIM_ALL8 = None
_SIM_LOCAL_AMOUT = False


class Sem:
    def __init__(self, handle):
        self.h = handle
        self.n = 0


class Rec:

    def __init__(self):
        self.ops = []
        self.own = {}

    def _em(self, key, fn, inc=None, k=1):
        own = self.own.get(key)
        if own is not None and own.n > 0:
            self.ops.append((key, lambda e, h=own.h, v=own.n: e.wait_ge(h, v)))
        if inc is not None:
            inc.n += k
            h, kk = inc.h, k
            self.ops.append((key, lambda e, f=fn, h=h, kk=kk: f(e).then_inc(h, kk)))
        else:
            self.ops.append((key, fn))

    def pe(self, fn, inc=None, k=1):
        self._em('pe', fn, inc, k)

    def dve(self, fn, inc=None, k=1):
        self._em('dve', fn, inc, k)

    def act(self, fn, inc=None, k=1):
        self._em('act', fn, inc, k)

    def pool(self, fn, inc=None, k=1):
        self._em('pool', fn, inc, k)

    def sp(self, fn, inc=None, k=1):
        self._em('sp', fn, inc, k)

    def wait(self, key, sem, val=None):
        v = sem.n if val is None else val
        if v <= 0:
            return
        self.ops.append((key, lambda e, h=sem.h, v=v: e.wait_ge(h, v)))

    def replay(self, key, eng):
        for k, fn in self.ops:
            if k == key:
                fn(eng)


def build_nc():
    nc = bass.Bass("TRN2", target_bir_lowering=False, debug=False)

    def din(name, shape, dt=F32):
        return nc.dram_tensor(name, shape, dt, kind="ExternalInput").ap()

    x_tm = din("x_tm", [TOK, D])
    x_fm = din("x_fm", [D, TOK])
    wq_r = din("wq_r", [H, 128, D])
    wk_r = din("wk_r", [H, 128, D])
    wv_r = din("wv_r", [H, 128, D])
    wo_r = din("wo_r", [H, 128, D])
    wf_r = din("wf_r", [NEC, 128, D])
    mem_r = din("mem_r", [H, 128, M * 128])
    mnT_r = din("mnT_r", [128, H * M])
    gate_r = din("gate_r", [1, H])
    bias_r = din("bias_r", [1, D])
    maskT_r = din("maskT_r", [128, 128], BF16)
    idf_r = din("idf_r", [128, 128])
    flag_r = din("flag_r", [1, 1])
    idb_r = din("idb_r", [128, 128], BF16)
    ones_r = din("ones_r", [128, 128], BF16)

    out_d = nc.dram_tensor("out", [TOK, D], F32, kind="ExternalOutput").ap()

    x1_d = nc.dram_tensor("x1_d", [TOK, D], F32, kind="Internal").ap()
    statb_m = nc.dram_tensor("statb_m", [1, TOK], F32, kind="Internal").ap()
    statb_r = nc.dram_tensor("statb_r", [1, TOK], F32, kind="Internal").ap()
    gb_d = nc.dram_tensor("gb_d", [1, 2 * H], F32, kind="Internal").ap()
    wgb_d = nc.dram_tensor("wgb_d", [1, H * M], F32, kind="Internal").ap()
    scb_d = nc.dram_tensor("scb_d", [1, 8], F32, kind="Internal").ap()
    sta_in = [nc.dram_tensor(f"sta_in{h}", [128, 129], F32, kind="Internal").ap()
              for h in range(H)]
    sta_out = [nc.dram_tensor(f"sta_out{h}", [256, 129], F32, kind="Internal").ap()
               for h in range(H)]
    rel_in = nc.dram_tensor("rel_in", [4, 16], F32, kind="Internal").ap()
    rel_out = nc.dram_tensor("rel_out", [8, 16], F32, kind="Internal").ap()
    am_in = nc.dram_tensor("am_in", [1, 1], F32, kind="Internal").ap()
    am_out = nc.dram_tensor("am_out", [1, 1], F32, kind="Internal",
                            **({} if _SIM_LOCAL_AMOUT else {"addr_space": "Shared"})).ap()

    PAIRS = [[0, 1], [2, 3], [4, 5], [6, 7]]
    ALL8 = _SIM_ALL8 or [[0, 1, 2, 3, 4, 5, 6, 7]]

    with contextlib.ExitStack() as _st:
        pool_a = _st.enter_context(nc.sbuf_tensor("pool_a", [128, 32768], BF16))
        pool_b = _st.enter_context(nc.sbuf_tensor("pool_b", [128, 32768], BF16))
        wp = _st.enter_context(nc.sbuf_tensor("wp", [128, 13376], BF16))
        xst = _st.enter_context(nc.sbuf_tensor("xst", [128, 2048], F32))
        xfm2 = _st.enter_context(nc.sbuf_tensor("xfm2", [128, 4096], F32))
        s1 = _st.enter_context(nc.sbuf_tensor("s1", [128, 2048], F32))
        s2 = _st.enter_context(nc.sbuf_tensor("s2", [128, 2048], F32))
        s3 = _st.enter_context(nc.sbuf_tensor("s3", [128, 2048], F32))
        vec = _st.enter_context(nc.sbuf_tensor("vec", [128, 56], F32))
        stats_m = _st.enter_context(nc.sbuf_tensor("stats_m", [128, 16], F32))
        stats_r = _st.enter_context(nc.sbuf_tensor("stats_r", [128, 16], F32))
        stat_t = _st.enter_context(nc.sbuf_tensor("stat_t", [16, 128], F32))
        mem_tm = _st.enter_context(nc.sbuf_tensor("mem_tm", [128, 128], F32))
        loc_tm = _st.enter_context(nc.sbuf_tensor("loc_tm", [128, 128], F32))
        maskT = _st.enter_context(nc.sbuf_tensor("maskT", [128, 128], BF16))
        idf = _st.enter_context(nc.sbuf_tensor("idf", [128, 128], F32))
        Wb = _st.enter_context(nc.sbuf_tensor("Wb", [128, 64], F32))
        Wb4 = _st.enter_context(nc.sbuf_tensor("Wb4", [128, 4], F32))
        gWb = _st.enter_context(nc.sbuf_tensor("gWb", [128, 16], F32))
        gLb = _st.enter_context(nc.sbuf_tensor("gLb", [128, 16], F32))
        flagv = _st.enter_context(nc.sbuf_tensor("flagv", [128, 1], F32))
        relp = _st.enter_context(nc.sbuf_tensor("relp", [4, 16], F32))
        relB = _st.enter_context(nc.sbuf_tensor("relB", [4, 16], F32))
        relT = _st.enter_context(nc.sbuf_tensor("relT", [16, 4], F32))
        relT2 = _st.enter_context(nc.sbuf_tensor("relT2", [16, 4], F32))
        grow = _st.enter_context(nc.sbuf_tensor("grow", [1, 2 * H], F32))
        wgrow = _st.enter_context(nc.sbuf_tensor("wgrow", [16, 4], F32))
        mrow = _st.enter_context(nc.sbuf_tensor("mrow", [1, 4], F32))
        mrow_b = _st.enter_context(nc.sbuf_tensor("mrow_b", [1, 4], BF16))
        scrow = _st.enter_context(nc.sbuf_tensor("scrow", [1, 8], F32))
        idb = _st.enter_context(nc.sbuf_tensor("idb", [128, 128], BF16))
        onesb = _st.enter_context(nc.sbuf_tensor("onesb", [128, 128], BF16))
        outc_b = _st.enter_context(nc.sbuf_tensor("outc_b", [128, 128], BF16))
        scb16 = _st.enter_context(nc.sbuf_tensor("scb16", [128, 128], BF16))
        sxln_b = _st.enter_context(nc.sbuf_tensor("sxln_b", [128, 16], BF16))
        ph_b = _st.enter_context(nc.sbuf_tensor("ph_b", [128, 1], BF16))
        mnT_b = _st.enter_context(nc.sbuf_tensor("mnT_b", [128, 64], BF16))
        pA0 = _st.enter_context(nc.psum_tensor("pA0", [128, 512], F32))
        pA1 = _st.enter_context(nc.psum_tensor("pA1", [128, 512], F32))
        pB0 = _st.enter_context(nc.psum_tensor("pB0", [128, 512], F32))
        pB1 = _st.enter_context(nc.psum_tensor("pB1", [128, 512], F32))
        pC0 = _st.enter_context(nc.psum_tensor("pC0", [128, 512], F32))
        pC1 = _st.enter_context(nc.psum_tensor("pC1", [128, 512], F32))
        pTa = _st.enter_context(nc.psum_tensor("pTa", [128, 1024], BF16))
        pTb = _st.enter_context(nc.psum_tensor("pTb", [128, 1024], BF16))
        _sp = _st.enter_context(nc.semaphore("m_sp"))
        _gq = _st.enter_context(nc.semaphore("m_gq"))
        _cc = _st.enter_context(nc.semaphore("m_cc"))
        _pe = _st.enter_context(nc.semaphore("m_pe"))
        _dve = _st.enter_context(nc.semaphore("m_dve"))
        _act = _st.enter_context(nc.semaphore("m_act"))
        _pool = _st.enter_context(nc.semaphore("m_pool"))
        block = _st.enter_context(nc.Block())

        r = Rec()
        cc_s = Sem(_cc)
        pe_s, dv_s, ac_s, po_s = Sem(_pe), Sem(_dve), Sem(_act), Sem(_pool)
        r.own = {'dve': dv_s, 'act': ac_s, 'pool': po_s}
        MS = Sem(_st.enter_context(nc.semaphore("m_ms")))
        XP = [Sem(_st.enter_context(nc.semaphore("m_x0"))),
              Sem(_st.enter_context(nc.semaphore("m_x1")))]
        WP = [Sem(_st.enter_context(nc.semaphore("m_w0"))),
              Sem(_st.enter_context(nc.semaphore("m_w1")))]
        XSTP = [Sem(_st.enter_context(nc.semaphore("m_xs0"))),
                Sem(_st.enter_context(nc.semaphore("m_xs1")))]
        OUTP = [Sem(_st.enter_context(nc.semaphore("m_o0"))),
                Sem(_st.enter_context(nc.semaphore("m_o1")))]
        GWQ = Sem(_st.enter_context(nc.semaphore("m_gwq")))
        GWK = Sem(_st.enter_context(nc.semaphore("m_gwk")))
        GWV = Sem(_st.enter_context(nc.semaphore("m_gwv")))
        GME = Sem(_st.enter_context(nc.semaphore("m_gme")))
        GWO = Sem(_st.enter_context(nc.semaphore("m_gwo")))
        GST = Sem(_st.enter_context(nc.semaphore("m_gst")))
        GRC = Sem(_st.enter_context(nc.semaphore("m_grc")))
        GMC = Sem(_st.enter_context(nc.semaphore("m_gmc")))

        def V(i):
            return vec[:, i:i + 1]
        C_EPSLN, C_NEGHALF, C_TWO, C_NEGEPSA, C_NEGONE = V(0), V(1), V(2), V(3), V(4)
        T_SUM, T_SSQ, T_NEGM, T_RSTD, T_MSQ, T_T0 = V(5), V(6), V(7), V(8), V(9), V(10)
        T_R, T_DEN = V(11), V(12)
        AMX_ACC, T_DEV = V(13), V(14)
        AW_ACC, AW_T = V(15), V(16)
        SC_AX, SC_AW, SC_SX, SC_SW, SC_A = V(17), V(18), V(19), V(20), V(21)
        HRSX, HRSW, AVEC, RAVEC = V(22), V(23), V(24), V(25)
        T_M4, T_V4 = V(26), V(27)

        SQ = wp[:, 0:2048]
        KPF = wp[:, 2048:4096]
        KPT = wp[:, 4096:6144]
        VAUG = wp[:, 6144:8208]
        WKH = wp[:, 8208:10256]
        WVH = wp[:, 10256:12304]
        MEMA = [wp[:, 12304:12816], wp[:, 12816:13328]]
        BIAS_HI = wp[0:1, 2048:4096]
        BIAS_LO = wp[0:1, 4096:6144]
        wpf = wp[:, 8208:12304].bitcast(F32)
        msku = wp[:, 12304:13328].bitcast(U8)
        t2_tm = s3[:, 1664:1792]
        recvf = s2[:, 0:129]
        totf = s2[:, 256:385]
        state_b = s2[:, 512:577].bitcast(BF16)[:, 0:129]
        partner_b = s2[:, 640:705].bitcast(BF16)[:, 0:129]
        fp8t = wp[:, 6144:7168].bitcast(FP8)

        r.wait('sp', MS, MS.n)
        r.sp(lambda e: e.dma_start(out=maskT[:], in_=maskT_r), inc=MS, k=16)
        r.wait('sp', MS, MS.n)
        r.sp(lambda e: e.dma_start(out=idf[:], in_=idf_r), inc=MS, k=16)
        r.wait('sp', MS, MS.n)
        r.sp(lambda e: e.dma_start(out=idb[:], in_=idb_r), inc=MS, k=16)
        r.wait('sp', MS, MS.n)
        r.sp(lambda e: e.dma_start(out=onesb[:], in_=ones_r), inc=MS, k=16)
        r.wait('sp', MS, MS.n)
        r.sp(lambda e: e.dma_start(out=grow[:, 0:16], in_=gate_r), inc=MS, k=16)
        r.wait('sp', MS, MS.n)
        r.sp(lambda e: e.dma_start(out=flagv[:],
                                   in_=flag_r[0:1, 0:1].partition_broadcast(128)),
             inc=MS, k=16)
        con_loads = MS.n

        for val, ap in [(EPS_LN, C_EPSLN), (-0.5, C_NEGHALF), (2.0, C_TWO),
                        (-EPS_A, C_NEGEPSA), (-1.0, C_NEGONE), (0.0, AMX_ACC),
                        (0.0, AW_ACC)]:
            r.dve(lambda e, v=val, a=ap: e.memset(a, v), inc=dv_s)
        r.dve(lambda e: e.memset(VAUG, 1.0), inc=dv_s)
        r.dve(lambda e: e.memset(state_b, 0.0), inc=dv_s)
        memset_done = dv_s.n

        r.wait('act', MS, con_loads)
        r.wait('act', dv_s, memset_done)
        r.act(lambda e: e.activation(grow[0:1, 16:32], grow[0:1, 0:16], AF.Exp,
                                     bias=0.0, scale=C_NEGONE[0:1, :]), inc=ac_s)
        g_exp = ac_s.n
        r.wait('dve', ac_s, g_exp)
        r.dve(lambda e: e.tensor_scalar(grow[0:1, 16:32], grow[0:1, 16:32], 1.0, None,
                                        ALU.add), inc=dv_s)
        r.dve(lambda e: e.reciprocal(grow[0:1, 0:16], grow[0:1, 16:32]), inc=dv_s)
        r.dve(lambda e: e.tensor_scalar(grow[0:1, 16:32], grow[0:1, 0:16], 1.0, -1.0,
                                        ALU.subtract, ALU.mult), inc=dv_s)
        g_done = dv_s.n
        r.wait('sp', dv_s, g_done)
        r.wait('sp', MS, MS.n)
        r.sp(lambda e: e.dma_start(out=gb_d, in_=grow[:]), inc=MS, k=16)
        gb_out = MS.n
        r.wait('sp', MS, gb_out)
        r.wait('sp', MS, MS.n)
        r.sp(lambda e: e.dma_start(out=gWb[:],
                                   in_=gb_d[0:1, 0:16].partition_broadcast(128)),
             inc=MS, k=16)
        r.wait('sp', MS, MS.n)
        r.sp(lambda e: e.dma_start(out=gLb[:],
                                   in_=gb_d[0:1, 16:32].partition_broadcast(128)),
             inc=MS, k=16)
        gl_rep = MS.n

        pa_marks = []
        for t in range(NCHUNK):
            buf = xst if t % 2 == 0 else s1
            if t >= 2:
                r.wait('sp', ac_s, pa_marks[t - 2][0])
                r.wait('sp', dv_s, pa_marks[t - 2][1])
            xp = XP[t % 2]
            r.sp(lambda e, t=t, b=buf: e.dma_start(
                out=b[:], in_=x_tm[t * 128:(t + 1) * 128, :]), inc=xp, k=16)
            ld = xp.n
            r.wait('dve', xp, ld)
            r.dve(lambda e, b=buf: e.tensor_reduce(T_SUM, b[:], AX.X, ALU.add), inc=dv_s)
            red_mark = dv_s.n
            r.wait('act', xp, ld)
            r.act(lambda e, b=buf: e.activation(s2[:, 0:2048], b[:], AF.Square,
                                                accum_out=T_SSQ), inc=ac_s)
            sq_mark = ac_s.n
            r.dve(lambda e: e.tensor_scalar(T_NEGM, T_SUM, -1.0 / 2048.0, None,
                                            ALU.mult), inc=dv_s)
            r.wait('dve', ac_s, sq_mark)
            r.dve(lambda e: e.tensor_scalar(T_MSQ, T_NEGM, T_NEGM, None, ALU.mult), inc=dv_s)
            r.dve(lambda e: e.tensor_scalar(T_T0, T_SSQ, 1.0 / 2048.0, None, ALU.mult), inc=dv_s)
            r.dve(lambda e: e.tensor_tensor(T_T0, T_T0, T_MSQ, ALU.subtract), inc=dv_s)
            var_mark = dv_s.n
            r.wait('act', dv_s, var_mark)
            r.act(lambda e: e.activation(T_T0, T_T0, AF.Ln, bias=C_EPSLN, scale=1.0), inc=ac_s)
            r.act(lambda e: e.activation(T_RSTD, T_T0, AF.Exp, bias=0.0,
                                         scale=C_NEGHALF), inc=ac_s)
            rstd_mark = ac_s.n
            r.dve(lambda e, t=t: e.tensor_copy(stats_m[:, t:t + 1], T_NEGM), inc=dv_s)
            r.wait('dve', ac_s, rstd_mark)
            r.dve(lambda e, t=t: e.tensor_copy(stats_r[:, t:t + 1], T_RSTD), inc=dv_s)
            pa_marks.append((sq_mark, dv_s.n))
        pa_dve = dv_s.n
        pa_act = ac_s.n

        r.wait('pe', dv_s, pa_dve)
        r.wait('pe', MS, con_loads)
        r.pe(lambda e: e.transpose(pA0[0:16, 0:128], stats_m[:], idf[:]), inc=pe_s)
        tr1 = pe_s.n
        r.wait('dve', pe_s, tr1)
        r.dve(lambda e: e.tensor_copy(stat_t[:], pA0[0:16, 0:128]), inc=dv_s)
        ev1 = dv_s.n
        r.wait('sp', dv_s, ev1)
        r.wait('sp', MS, MS.n)
        r.sp(lambda e: e.dma_start(
            out=statb_m.rearrange("a (p c) -> (a p) c", c=128), in_=stat_t[:]),
            inc=MS, k=16)
        st1 = MS.n
        r.wait('pe', dv_s, ev1)
        r.pe(lambda e: e.transpose(pA0[0:16, 0:128], stats_r[:], idf[:]), inc=pe_s)
        tr2 = pe_s.n
        r.wait('dve', pe_s, tr2)
        r.wait('dve', MS, st1)
        r.dve(lambda e: e.tensor_copy(stat_t[:], pA0[0:16, 0:128]), inc=dv_s)
        ev2 = dv_s.n
        r.wait('sp', dv_s, ev2)
        r.wait('sp', MS, MS.n)
        r.sp(lambda e: e.dma_start(
            out=statb_r.rearrange("a (p c) -> (a p) c", c=128), in_=stat_t[:]),
            inc=MS, k=16)
        st2 = MS.n
        r.wait('sp', MS, st2)
        r.wait('sp', dv_s, pa_dve)
        r.wait('sp', ac_s, pa_act)
        r.wait('sp', MS, MS.n)
        r.sp(lambda e: e.dma_start(out=s1[:],
                                   in_=statb_m[0:1, :].partition_broadcast(128)),
             inc=MS, k=16)
        r.wait('sp', MS, MS.n)
        r.sp(lambda e: e.dma_start(out=s2[:],
                                   in_=statb_r[0:1, :].partition_broadcast(128)),
             inc=MS, k=16)
        bc_done = MS.n

        pb_marks = []
        for ec in range(NEC):
            half = xfm2[:, (ec % 2) * 2048:((ec % 2) + 1) * 2048]
            if ec >= 2:
                r.wait('sp', dv_s, pb_marks[ec - 2])
            wp_ = WP[ec % 2]
            r.sp(lambda e, ec=ec, b=half: e.dma_start(
                out=b, in_=x_fm[ec * 128:(ec + 1) * 128, :]), inc=wp_, k=16)
            ld = wp_.n
            r.wait('dve', wp_, ld)
            r.wait('dve', MS, bc_done)
            r.dve(lambda e, b=half: e.tensor_tensor(s3[:, 0:2048], b, s1[:], ALU.add),
                  inc=dv_s)
            acc_col = vec[:, 28 + (ec % 2):29 + (ec % 2)]
            r.dve(lambda e, ec=ec, ac=acc_col: e.scalar_tensor_tensor(
                pool_a[:, ec * 2048:(ec + 1) * 2048], s3[:, 0:2048], 1.0, s2[:],
                ALU.mult, ALU.mult, accum_out=ac), inc=dv_s)
            r.dve(lambda e, ec=ec, ac=acc_col: e.tensor_copy(sxln_b[:, ec:ec + 1], ac),
                  inc=dv_s)
            pb_marks.append(dv_s.n)
        pb_dve = dv_s.n

        for h in range(H):
            r.pool(lambda e, h=h: e.dma_start(
                out=pool_b[:, h * 2048:(h + 1) * 2048], in_=wq_r[h]), inc=GWQ, k=16)
        r.pool(lambda e: e.dma_start(out=mnT_b[:], in_=mnT_r), inc=GWQ, k=16)
        wq_all_ld = GWQ.n

        r.wait('pe', dv_s, pb_dve)
        r.wait('pe', GWQ, wq_all_ld)
        ph_ev = 0
        for h in range(H):
            if ph_ev:
                r.wait('pe', dv_s, ph_ev)
            for ec in range(NEC):
                r.pe(lambda e, h=h, ec=ec: e.matmul(
                    pA0[0:128, 0:1],
                    pool_b[:, h * 2048 + ec * 128: h * 2048 + (ec + 1) * 128],
                    sxln_b[:, ec:ec + 1],
                    start=(ec == 0), stop=(ec == NEC - 1)),
                    inc=(pe_s if ec == NEC - 1 else None))
            ph_mm = pe_s.n
            r.wait('dve', pe_s, ph_mm)
            r.dve(lambda e: e.tensor_copy(ph_b[:], pA0[0:128, 0:1]), inc=dv_s)
            ph_ev = dv_s.n
            r.wait('pe', dv_s, ph_ev)
            r.pe(lambda e, h=h: e.matmul(pA1[0:4, h:h + 1],
                                         mnT_b[:, h * 4:(h + 1) * 4], ph_b[:],
                                         start=True, stop=True), inc=pe_s)
        rel_mm = pe_s.n
        r.wait('dve', pe_s, rel_mm)
        r.dve(lambda e: e.tensor_copy(relp[:], pA1[0:4, 0:16]), inc=dv_s)
        relp_ev = dv_s.n
        r.wait('pool', dv_s, relp_ev)
        r.pool(lambda e: e.dma_start(out=rel_in, in_=relp[:]), inc=GMC, k=16)
        r.wait('pool', GMC, GMC.n)
        r.pool(lambda e: e.collective_compute(
            "AllGather", ALU.bypass, replica_groups=PAIRS,
            ins=[rel_in.opt()], outs=[rel_out.opt()]), inc=cc_s, k=1)
        r.wait('pool', cc_s, cc_s.n)
        r.pool(lambda e: e.dma_start(out=relp[:], in_=rel_out[0:4, :]), inc=GMC, k=16)
        r.pool(lambda e: e.dma_start(out=relB[:], in_=rel_out[4:8, :]), inc=GMC, k=16)
        rel_back = GMC.n
        r.wait('dve', GMC, rel_back)
        r.dve(lambda e: e.tensor_tensor(relp[:], relp[:], relB[:], ALU.add), inc=dv_s)
        rel_sum = dv_s.n

        r.wait('pe', dv_s, relp_ev)
        for h in range(H):
            r.pe(lambda e, h=h: e.matmul(pB0[0:4, 0:1], mnT_b[:, h * 4:(h + 1) * 4],
                                         onesb[:, 0:1], start=(h == 0),
                                         stop=(h == H - 1)),
                 inc=(pe_s if h == H - 1 else None))
        mn_mm = pe_s.n
        r.wait('act', pe_s, mn_mm)
        r.act(lambda e: e.activation(vec[0:4, 30:31], pB0[0:4, 0:1], AF.Sign,
                                     bias=C_NEGEPSA[0:4, :], scale=1.0), inc=ac_s)
        sign_mark = ac_s.n
        r.wait('dve', ac_s, sign_mark)
        r.dve(lambda e: e.tensor_scalar(vec[0:4, 30:31], vec[0:4, 30:31], 1.0, 0.5,
                                        ALU.add, ALU.mult), inc=dv_s)
        r.dve(lambda e: e.tensor_scalar(vec[0:4, 31:32], vec[0:4, 30:31], 1.0, NEG_BIG,
                                        ALU.subtract, ALU.mult), inc=dv_s)
        r.wait('dve', dv_s, rel_sum)
        r.dve(lambda e: e.tensor_scalar(relp[:], relp[:], float(SCALE / np.float32(S)),
                                        vec[0:4, 31:32], ALU.mult, ALU.add), inc=dv_s)
        reladj = dv_s.n
        r.wait('pe', dv_s, reladj)
        r.pe(lambda e: e.transpose(pB0[0:16, 0:4], relp[:], idf[0:4, 0:4]), inc=pe_s)
        r.pe(lambda e: e.transpose(pB0[0:1, 16:20], vec[0:4, 30:31], idf[0:4, 0:4]),
             inc=pe_s)
        trel = pe_s.n
        r.wait('dve', pe_s, trel)
        r.dve(lambda e: e.tensor_copy(relT[:], pB0[0:16, 0:4]), inc=dv_s)
        r.dve(lambda e: e.tensor_copy(mrow[:], pB0[0:1, 16:20]), inc=dv_s)
        r.dve(lambda e: e.tensor_reduce(vec[0:16, 32:33], relT[:], AX.X, ALU.max), inc=dv_s)
        r.dve(lambda e: e.tensor_scalar(relT[:], relT[:], vec[0:16, 32:33], None,
                                        ALU.subtract), inc=dv_s)
        sm1 = dv_s.n
        r.wait('act', dv_s, sm1)
        r.act(lambda e: e.activation(relT2[:], relT[:], AF.Exp, bias=0.0, scale=1.0),
              inc=ac_s)
        sm2 = ac_s.n
        r.wait('dve', ac_s, sm2)
        r.dve(lambda e: e.tensor_reduce(vec[0:16, 32:33], relT2[:], AX.X, ALU.add), inc=dv_s)
        r.dve(lambda e: e.reciprocal(vec[0:16, 33:34], vec[0:16, 32:33]), inc=dv_s)
        r.dve(lambda e: e.tensor_scalar(relT2[:], relT2[:], vec[0:16, 33:34], None,
                                        ALU.mult), inc=dv_s)
        r.dve(lambda e: e.tensor_copy(mrow_b[:], mrow[:]), inc=dv_s)
        mrb = dv_s.n
        r.wait('pe', dv_s, mrb)
        r.pe(lambda e: e.matmul(pB1[0:16, 0:4], onesb[0:1, 0:16], mrow_b[:],
                                start=True, stop=True), inc=pe_s)
        mi_mm = pe_s.n
        r.wait('dve', pe_s, mi_mm)
        r.dve(lambda e: e.tensor_tensor(wgrow[:], relT2[:], pB1[0:16, 0:4], ALU.mult),
              inc=dv_s)
        wg_done = dv_s.n
        r.wait('sp', dv_s, wg_done)
        r.wait('sp', MS, MS.n)
        r.sp(lambda e: e.dma_start(
            out=wgb_d.rearrange("a (p c) -> (a p) c", c=4), in_=wgrow[:]),
            inc=MS, k=16)
        wg_out = MS.n
        r.wait('sp', MS, wg_out)
        r.wait('sp', MS, MS.n)
        r.sp(lambda e: e.dma_start(out=Wb[:],
                                   in_=wgb_d[0:1, :].partition_broadcast(128)),
             inc=MS, k=16)
        wb_rep = MS.n

        wk_marks, wv_marks, mem_marks = {}, {}, {}

        def prefetch_head(h, wait_pe=0):
            if wait_pe:
                r.wait('pool', pe_s, wait_pe)
            r.wait('pool', GWK, GWK.n)
            r.wait('pool', GWV, GWV.n)
            r.wait('pool', GME, GME.n)
            r.pool(lambda e, h=h: e.dma_start(out=WKH, in_=wk_r[h]), inc=GWK, k=16)
            wk_marks[h] = GWK.n
            r.pool(lambda e, h=h: e.dma_start(out=WVH, in_=wv_r[h]), inc=GWV, k=16)
            wv_marks[h] = GWV.n
            r.pool(lambda e, h=h: e.dma_start(out=MEMA[h % 2], in_=mem_r[h]),
                   inc=GME, k=16)
            mem_marks[h] = GME.n

        prefetch_head(0)

        last_xln_use_pe = 0
        wb4_done = 0

        def elu_evac(psum, dst_slice, n):
            r.dve(lambda e, p=psum, n=n: e.tensor_scalar(
                s3[:, 0:n], p[:, 0:n], 0.0, None, ALU.min), inc=dv_s)
            t0 = dv_s.n
            r.wait('act', dv_s, t0)
            r.act(lambda e, n=n: e.activation(s3[:, 1024:1024 + n], s3[:, 0:n], AF.Exp,
                                              bias=0.0, scale=1.0), inc=ac_s)
            ee = ac_s.n
            r.wait('dve', ac_s, ee)
            r.dve(lambda e, p=psum, d=dst_slice, n=n: e.scalar_tensor_tensor(
                d, p[:, 0:n], 0.0, s3[:, 1024:1024 + n], ALU.max, ALU.add), inc=dv_s)
            return dv_s.n

        for h in range(H):
            hs = h * 2048
            bank_mark = {0: 0, 1: 0}

            r.wait('pe', ac_s, ac_s.n)
            r.wait('dve', MS, gl_rep)
            for g in range(4):
                pp = pA0 if g % 2 == 0 else pA1
                if bank_mark[g % 2]:
                    r.wait('pe', dv_s, bank_mark[g % 2])
                for ec in range(NEC):
                    r.pe(lambda e, g=g, ec=ec, pp=pp, hs=hs: e.matmul(
                        pp[:, 0:512],
                        pool_b[:, hs + ec * 128: hs + (ec + 1) * 128],
                        pool_a[:, ec * 2048 + g * 512: ec * 2048 + (g + 1) * 512],
                        start=(ec == 0), stop=(ec == NEC - 1)),
                        inc=(pe_s if ec == NEC - 1 else None))
                r.wait('dve', pe_s, pe_s.n)
                bank_mark[g % 2] = elu_evac(pp, SQ[:, g * 512:(g + 1) * 512], 512)
            r.wait('pe', GWK, wk_marks[h])
            for g in range(4):
                pp = pA0 if g % 2 == 0 else pA1
                r.wait('pe', dv_s, bank_mark[g % 2])
                for ec in range(NEC):
                    r.pe(lambda e, g=g, ec=ec, pp=pp: e.matmul(
                        pp[:, 0:512],
                        WKH[:, ec * 128:(ec + 1) * 128],
                        pool_a[:, ec * 2048 + g * 512: ec * 2048 + (g + 1) * 512],
                        start=(ec == 0), stop=(ec == NEC - 1)),
                        inc=(pe_s if ec == NEC - 1 else None))
                r.wait('dve', pe_s, pe_s.n)
                bank_mark[g % 2] = elu_evac(pp, KPF[:, g * 512:(g + 1) * 512], 512)
            tr_mark = {0: 0, 1: 0}
            r.wait('pe', dv_s, bank_mark[0])
            r.wait('pe', dv_s, bank_mark[1])
            for c in range(NCHUNK):
                pt = pTa if c % 2 == 0 else pTb
                if tr_mark[c % 2]:
                    r.wait('pe', dv_s, tr_mark[c % 2])
                r.pe(lambda e, c=c, pt=pt: e.transpose(
                    pt[:, 0:128], KPF[:, c * 128:(c + 1) * 128], idb[:]), inc=pe_s)
                r.wait('dve', pe_s, pe_s.n)
                r.dve(lambda e, c=c, pt=pt: e.tensor_copy(
                    KPT[:, c * 128:(c + 1) * 128], pt[:, 0:128]), inc=dv_s)
                tr_mark[c % 2] = dv_s.n
            r.wait('pe', GWV, wv_marks[h])
            for c in range(NCHUNK):
                pp = pA0 if c % 2 == 0 else pA1
                r.wait('pe', dv_s, bank_mark[c % 2])
                for ec in range(NEC):
                    r.pe(lambda e, c=c, ec=ec, pp=pp: e.matmul(
                        pp[:, 0:128],
                        pool_a[:, ec * 2048 + c * 128: ec * 2048 + (c + 1) * 128],
                        WVH[:, ec * 128:(ec + 1) * 128],
                        start=(ec == 0), stop=(ec == NEC - 1)),
                        inc=(pe_s if ec == NEC - 1 else None))
                r.wait('dve', pe_s, pe_s.n)
                r.dve(lambda e, c=c, pp=pp: e.tensor_copy(
                    VAUG[:, c * 129:c * 129 + 128], pp[:, 0:128]), inc=dv_s)
                bank_mark[c % 2] = dv_s.n
            vaug_done = dv_s.n
            if h == H - 1:
                last_xln_use_pe = pe_s.n

            r.wait('pe', dv_s, vaug_done)
            for c in range(NCHUNK):
                r.pe(lambda e, c=c: e.matmul(
                    pA0[:, 256:385], KPT[:, c * 128:(c + 1) * 128],
                    VAUG[:, c * 129:(c + 1) * 129],
                    start=(c == 0), stop=(c == NCHUNK - 1)),
                    inc=(pe_s if c == NCHUNK - 1 else None))
            kvt_mm = pe_s.n
            r.wait('dve', pe_s, kvt_mm)
            r.dve(lambda e: e.tensor_copy(totf, pA0[:, 256:385]), inc=dv_s)
            tot_ev = dv_s.n
            r.wait('pool', dv_s, tot_ev)
            r.pool(lambda e, h=h: e.dma_start(out=sta_in[h], in_=totf), inc=GST, k=16)
            tot_dma = GST.n
            r.wait('pool', GST, tot_dma)
            r.pool(lambda e, h=h: e.collective_compute(
                "AllGather", ALU.bypass, replica_groups=PAIRS,
                ins=[sta_in[h].opt()], outs=[sta_out[h].opt()]), inc=cc_s, k=1)
            r.wait('pool', cc_s, cc_s.n)
            r.pool(lambda e, h=h: e.dma_start(out=recvf, in_=sta_out[h][0:128, :]),
                   inc=GRC, k=16)
            recv_done = GRC.n
            r.wait('dve', GST, tot_dma)
            r.dve(lambda e: e.memset(totf, 0.0), inc=dv_s)
            r.dve(lambda e: e.memset(state_b, 0.0), inc=dv_s)
            r.wait('dve', GRC, recv_done)
            r.dve(lambda e: e.tensor_scalar(partner_b, recvf, flagv[:, 0:1],
                                            None, ALU.mult), inc=dv_s)
            r.wait('dve', MS, wb_rep)
            if wb4_done:
                r.wait('dve', ac_s, ac_s.n)
            r.dve(lambda e, h=h: e.tensor_copy(Wb4[:], Wb[:, h * 4:(h + 1) * 4]), inc=dv_s)
            wb4_done = dv_s.n
            pre_loop_dve = dv_s.n

            if h + 1 < H:
                prefetch_head(h + 1, wait_pe=kvt_mm)

            bankA = [pA0, pA1]
            bankB = [pB0, pB1]
            bankC = [pC0, pC1]
            mk_scores = [0, 0]
            mk_B = [0, 0]
            mk_memA = [0, 0]
            mk_memD = [0, 0]
            mk_tr = [tr_mark[0], tr_mark[1]]
            mk_d5b = pre_loop_dve
            mk_d4 = 0
            s9_prev = 0
            r.wait('pe', dv_s, pre_loop_dve)
            r.wait('pe', GME, mem_marks[h])
            for c in range(NCHUNK):
                par = c % 2
                pA_, pB_, pC_ = bankA[par], bankB[par], bankC[par]
                cs = c * 128
                if mk_scores[par]:
                    r.wait('pe', dv_s, mk_scores[par])
                r.pe(lambda e, pA_=pA_, cs=cs: e.matmul(
                    pA_[:, 0:128], KPF[:, cs:cs + 128], SQ[:, cs:cs + 128],
                    start=True, stop=True), inc=pe_s)
                s1_mm = pe_s.n
                r.pe(lambda e, pA_=pA_, c=c: e.matmul(
                    pA_[:, 256:385], KPT[:, c * 128:(c + 1) * 128],
                    VAUG[:, c * 129:(c + 1) * 129],
                    start=True, stop=True), inc=pe_s)
                s10_mm = pe_s.n
                if mk_memA[par] or mk_memD[par]:
                    r.wait('pe', ac_s, mk_memA[par])
                    r.wait('pe', dv_s, mk_memD[par])
                for m in range(M):
                    r.pe(lambda e, pC_=pC_, cs=cs, m=m, h=h: e.matmul(
                        pC_[:, m * 128:(m + 1) * 128], SQ[:, cs:cs + 128],
                        MEMA[h % 2][:, m * 128:(m + 1) * 128],
                        start=True, stop=True), inc=(pe_s if m == M - 1 else None))
                s6_mm = pe_s.n
                if mk_B[par]:
                    r.wait('pe', dv_s, mk_B[par])
                r.pe(lambda e, pB_=pB_, cs=cs, h=h: e.matmul(
                    pB_[:, 256:260], SQ[:, cs:cs + 128], mnT_b[:, h * 4:(h + 1) * 4],
                    start=True, stop=True), inc=pe_s)
                r.wait('pe', dv_s, mk_d5b)
                r.pe(lambda e, pB_=pB_, cs=cs: e.matmul(
                    pB_[:, 0:129], SQ[:, cs:cs + 128], state_b,
                    start=True, stop=False), inc=pe_s)
                s3_mm = pe_s.n
                r.pe(lambda e, pB_=pB_, cs=cs: e.matmul(
                    pB_[:, 0:129], SQ[:, cs:cs + 128], partner_b,
                    start=False, stop=False), inc=pe_s)
                r.wait('dve', pe_s, s1_mm)
                r.dve(lambda e, pA_=pA_: e.tensor_tensor(scb16[:], pA_[:, 0:128],
                                                         maskT[:], ALU.mult), inc=dv_s)
                d1 = dv_s.n
                r.wait('pe', dv_s, d1)
                r.pe(lambda e, pB_=pB_, c=c: e.matmul(
                    pB_[:, 0:129], scb16[:], VAUG[:, c * 129:(c + 1) * 129],
                    start=False, stop=True), inc=pe_s)
                s4_mm = pe_s.n
                r.wait('dve', pe_s, s10_mm)
                r.dve(lambda e, pA_=pA_: e.tensor_tensor(totf, totf,
                                                         pA_[:, 256:385], ALU.add),
                      inc=dv_s)
                kv_read = dv_s.n
                mk_scores[par] = max(d1, kv_read)
                r.wait('dve', pe_s, s3_mm)
                r.dve(lambda e: e.tensor_copy(state_b, totf), inc=dv_s)
                mk_d5b = dv_s.n
                r.wait('dve', pe_s, s6_mm)
                r.dve(lambda e, pB_=pB_: e.tensor_scalar(vec[:, 34:38], pB_[:, 256:260],
                                                         EPS_A, None, ALU.max), inc=dv_s)
                r.dve(lambda e: e.reciprocal(vec[:, 38:42], vec[:, 34:38]), inc=dv_s)
                r.dve(lambda e: e.tensor_tensor(vec[:, 34:38], vec[:, 38:42], Wb4[:],
                                                ALU.mult), inc=dv_s)
                f_done = dv_s.n
                r.wait('act', dv_s, max(f_done, mk_d4))
                r.act(lambda e, pC_=pC_: e.activation(mem_tm[:], pC_[:, 0:128],
                                                      AF.Identity, bias=0.0,
                                                      scale=vec[:, 34:35]), inc=ac_s)
                mk_memA[par] = ac_s.n
                r.wait('dve', ac_s, mk_memA[par])
                for m in range(1, M):
                    r.dve(lambda e, pC_=pC_, m=m: e.scalar_tensor_tensor(
                        mem_tm[:], pC_[:, m * 128:(m + 1) * 128], vec[:, 34 + m:35 + m],
                        mem_tm[:], ALU.mult, ALU.add), inc=dv_s)
                mk_memD[par] = dv_s.n
                r.wait('dve', pe_s, s4_mm)
                r.dve(lambda e, pB_=pB_: e.tensor_scalar(T_DEN, pB_[:, 128:129],
                                                         EPS_A, None, ALU.max), inc=dv_s)
                r.dve(lambda e: e.reciprocal(T_R, T_DEN), inc=dv_s)
                r.dve(lambda e, pB_=pB_: e.tensor_scalar(loc_tm[:], pB_[:, 0:128],
                                                         T_R, None, ALU.mult), inc=dv_s)
                mk_B[par] = dv_s.n
                if s9_prev:
                    r.wait('dve', pe_s, s9_prev)
                r.dve(lambda e, h=h: e.tensor_scalar(t2_tm, mem_tm[:],
                                                     gWb[:, h:h + 1], None, ALU.mult),
                      inc=dv_s)
                r.dve(lambda e, h=h: e.scalar_tensor_tensor(
                    outc_b[:], loc_tm[:], gLb[:, h:h + 1], t2_tm,
                    ALU.mult, ALU.add), inc=dv_s)
                mk_d4 = dv_s.n
                pt = pTa if par == 0 else pTb
                r.wait('pe', dv_s, mk_d4)
                if mk_tr[par]:
                    r.wait('pe', dv_s, mk_tr[par])
                r.pe(lambda e, pt=pt: e.transpose(pt[:, 0:128], outc_b[:], idb[:]),
                     inc=pe_s)
                s9_prev = pe_s.n
                r.wait('dve', pe_s, s9_prev)
                r.dve(lambda e, pt=pt, hs=hs, cs=cs: e.tensor_copy(
                    pool_b[:, hs + cs: hs + cs + 128], pt[:, 0:128]), inc=dv_s)
                mk_tr[par] = dv_s.n
        pc_dve = dv_s.n
        pc_act = ac_s.n

        wo_all = 0
        r.wait('pool', pe_s, last_xln_use_pe)
        for hh in range(H):
            r.pool(lambda e, hh=hh: e.dma_start(
                out=pool_a[:, hh * 2048:(hh + 1) * 2048], in_=wo_r[hh]), inc=GWO, k=16)
        wo_all = GWO.n

        wf_marks = []
        for dc in range(NEC):
            half = xfm2[:, (dc % 2) * 2048:((dc % 2) + 1) * 2048]
            if dc >= 2:
                r.wait('sp', dv_s, wf_marks[dc - 2])
            else:
                r.wait('sp', dv_s, pb_dve)
            wp_ = WP[dc % 2]
            r.sp(lambda e, dc=dc, b=half: e.dma_start(out=b, in_=wf_r[dc]),
                 inc=wp_, k=16)
            ld = wp_.n
            r.wait('dve', wp_, ld)
            r.dve(lambda e, b=half: e.tensor_reduce(AW_T, b, AX.X, ALU.max,
                                                    apply_absolute_value=True), inc=dv_s)
            r.dve(lambda e: e.tensor_tensor(AW_ACC, AW_ACC, AW_T, ALU.max), inc=dv_s)
            wf_marks.append(dv_s.n)
        amaxw_done = dv_s.n

        r.wait('pe', dv_s, pc_dve)
        r.wait('pe', ac_s, pc_act)
        r.wait('pe', GWO, wo_all)
        x1_sp_marks = []
        x1_dve_marks = []
        pd_evac_mark = 0
        pd_bankfree = [0, 0]
        banks4 = [pA0, pA1, pB0, pB1]
        for t in range(NCHUNK):
            cs = t * 128
            xb = xst[:, 0:2048] if t % 2 == 0 else wpf
            if t >= 2:
                r.wait('sp', dv_s, x1_dve_marks[t - 2])
            xp = XP[t % 2]
            r.sp(lambda e, t=t, b=xb: e.dma_start(
                out=b[:, 0:2048], in_=x_tm[t * 128:(t + 1) * 128, :]), inc=xp, k=16)
            x_ld = xp.n
            for ng in range(4):
                bk = banks4[ng % 2]
                if pd_bankfree[ng % 2]:
                    r.wait('pe', ac_s, pd_bankfree[ng % 2])
                for hh in range(H):
                    r.pe(lambda e, hh=hh, ng=ng, cs=cs, bk=bk: e.matmul(
                        bk[:, 0:512],
                        pool_b[:, hh * 2048 + cs: hh * 2048 + cs + 128],
                        pool_a[:, hh * 2048 + ng * 512: hh * 2048 + (ng + 1) * 512],
                        start=(hh == 0), stop=(hh == H - 1)),
                        inc=(pe_s if hh == H - 1 else None))
                g_mm = pe_s.n
                r.wait('act', pe_s, g_mm)
                if ng == 0 and t >= 1:
                    r.wait('act', dv_s, x1_dve_marks[-1])
                r.act(lambda e, ng=ng, bk=bk: e.activation(
                    s1[:, ng * 512:(ng + 1) * 512], bk[:, 0:512], AF.Identity,
                    accum_out=vec[:, 45 + ng:46 + ng]), inc=ac_s)
                pd_bankfree[ng % 2] = ac_s.n
            pd_evac_mark = ac_s.n
            r.wait('dve', ac_s, pd_evac_mark)
            r.dve(lambda e: e.tensor_tensor(vec[:, 45:46], vec[:, 45:46],
                                            vec[:, 46:47], ALU.add), inc=dv_s)
            r.dve(lambda e: e.tensor_tensor(vec[:, 47:48], vec[:, 47:48],
                                            vec[:, 48:49], ALU.add), inc=dv_s)
            r.dve(lambda e: e.tensor_tensor(vec[:, 45:46], vec[:, 45:46],
                                            vec[:, 47:48], ALU.add), inc=dv_s)
            r.dve(lambda e: e.tensor_scalar(T_NEGM, vec[:, 45:46], -1.0 / 2048.0, None,
                                            ALU.mult), inc=dv_s)
            nm2 = dv_s.n
            r.wait('act', dv_s, nm2)
            r.act(lambda e: e.activation(s2[:, 0:2048], s1[:, 0:2048], AF.Square,
                                         accum_out=T_SSQ), inc=ac_s)
            ssq2 = ac_s.n
            r.wait('dve', ac_s, ssq2)
            r.dve(lambda e: e.tensor_scalar(T_MSQ, T_NEGM, T_NEGM, None, ALU.mult), inc=dv_s)
            r.dve(lambda e: e.tensor_scalar(T_T0, T_SSQ, 1.0 / 2048.0, None, ALU.mult), inc=dv_s)
            r.dve(lambda e: e.tensor_tensor(T_T0, T_T0, T_MSQ, ALU.subtract), inc=dv_s)
            v2m = dv_s.n
            r.wait('act', dv_s, v2m)
            r.act(lambda e: e.activation(T_T0, T_T0, AF.Ln, bias=C_EPSLN, scale=1.0),
                  inc=ac_s)
            r.act(lambda e: e.activation(T_RSTD, T_T0, AF.Exp, bias=0.0,
                                         scale=C_NEGHALF), inc=ac_s)
            r2m = ac_s.n
            r.wait('dve', ac_s, r2m)
            r.dve(lambda e: e.tensor_scalar(s2[:, 0:2048], s1[:, 0:2048], T_NEGM,
                                            T_RSTD, ALU.add, ALU.mult), inc=dv_s)
            r.wait('dve', XP[t % 2], x_ld)
            if t >= 1:
                r.wait('dve', XSTP[(t - 1) % 2], x1_sp_marks[-1])
            r.dve(lambda e, b=xb: e.scalar_tensor_tensor(
                s3[:, 0:2048], s2[:, 0:2048], 1.0, b[:, 0:2048], ALU.mult, ALU.add,
                accum_out=vec[:, 49:50]), inc=dv_s)
            x1_mark = dv_s.n
            r.wait('sp', dv_s, x1_mark)
            xstp = XSTP[t % 2]
            r.sp(lambda e, t=t: e.dma_start(out=x1_d[t * 128:(t + 1) * 128, :],
                                            in_=s3[:, 0:2048]), inc=xstp, k=16)
            x1_sp_marks.append(xstp.n)
            r.dve(lambda e: e.tensor_scalar(T_NEGM, vec[:, 49:50], -1.0 / 2048.0, None,
                                            ALU.mult), inc=dv_s)
            r.dve(lambda e: e.tensor_scalar(s2[:, 0:2048], s3[:, 0:2048], T_NEGM, None,
                                            ALU.add), inc=dv_s)
            d3m = dv_s.n
            r.dve(lambda e: e.tensor_reduce(T_DEV, s2[:, 0:2048], AX.X, ALU.max,
                                            apply_absolute_value=True), inc=dv_s)
            r.wait('act', dv_s, d3m)
            r.act(lambda e: e.activation(s1[:, 0:2048], s2[:, 0:2048], AF.Square,
                                         accum_out=T_SSQ), inc=ac_s)
            sq3 = ac_s.n
            r.wait('dve', ac_s, sq3)
            r.dve(lambda e: e.tensor_scalar(T_T0, T_SSQ, 1.0 / 2048.0, None, ALU.mult),
                  inc=dv_s)
            v3m = dv_s.n
            r.wait('act', dv_s, v3m)
            r.act(lambda e: e.activation(T_T0, T_T0, AF.Ln, bias=C_EPSLN, scale=1.0),
                  inc=ac_s)
            r.act(lambda e: e.activation(T_RSTD, T_T0, AF.Exp, bias=0.0,
                                         scale=C_NEGHALF), inc=ac_s)
            r3m = ac_s.n
            r.wait('dve', ac_s, r3m)
            r.dve(lambda e: e.tensor_tensor(T_DEV, T_DEV, T_RSTD, ALU.mult), inc=dv_s)
            r.dve(lambda e: e.tensor_tensor(AMX_ACC, AMX_ACC, T_DEV, ALU.max), inc=dv_s)
            x1_dve_marks.append(dv_s.n)
        pd_dve = dv_s.n
        pd_pe = pe_s.n

        r.wait('pe', dv_s, pd_dve)
        r.pe(lambda e: e.transpose(pC0[0:1, 0:128], AMX_ACC, idf[:]), inc=pe_s)
        amt = pe_s.n
        r.wait('dve', pe_s, amt)
        r.wait('dve', XSTP[0], XSTP[0].n)
        r.wait('dve', XSTP[1], XSTP[1].n)
        r.dve(lambda e: e.tensor_copy(s3[0:1, 0:128], pC0[0:1, 0:128]), inc=dv_s)
        r.dve(lambda e: e.tensor_reduce(SC_AX[0:1, :], s3[0:1, 0:128], AX.X, ALU.max),
              inc=dv_s)
        amx_mark = dv_s.n
        r.wait('pool', dv_s, amx_mark)
        r.pool(lambda e: e.dma_start(out=am_in, in_=SC_AX[0:1, :]), inc=GST, k=16)
        r.wait('pool', GST, GST.n)
        r.pool(lambda e: e.collective_compute(
            "AllReduce", ALU.max, replica_groups=ALL8,
            ins=[am_in.opt()], outs=[am_out.opt()]), inc=cc_s, k=1)
        r.wait('pool', cc_s, cc_s.n)
        r.pool(lambda e: e.dma_start(out=SC_AX[0:1, :], in_=am_out), inc=GRC, k=16)
        amax_back = GRC.n

        r.wait('pe', dv_s, max(amx_mark, amaxw_done))
        r.pe(lambda e: e.transpose(pC0[0:1, 0:128], AW_ACC, idf[:]), inc=pe_s)
        awt = pe_s.n
        r.wait('dve', pe_s, awt)
        r.dve(lambda e: e.tensor_copy(s3[0:1, 0:128], pC0[0:1, 0:128]), inc=dv_s)
        r.dve(lambda e: e.tensor_reduce(SC_AW[0:1, :], s3[0:1, 0:128], AX.X, ALU.max),
              inc=dv_s)
        r.wait('dve', GRC, amax_back)
        r.dve(lambda e: e.tensor_scalar(SC_SX[0:1, :], SC_AX[0:1, :], RFP8_C, None,
                                        ALU.mult), inc=dv_s)
        r.dve(lambda e: e.tensor_tensor(SC_SW[0:1, :], SC_AX[0:1, :], SC_AW[0:1, :],
                                        ALU.max), inc=dv_s)
        r.dve(lambda e: e.tensor_scalar(SC_SW[0:1, :], SC_SW[0:1, :], RFP8_C, None,
                                        ALU.mult), inc=dv_s)
        r.dve(lambda e: e.tensor_tensor(SC_A[0:1, :], SC_SX[0:1, :], SC_SW[0:1, :],
                                        ALU.mult), inc=dv_s)
        r.dve(lambda e: e.reciprocal(scrow[0:1, 3:4], SC_A[0:1, :]), inc=dv_s)
        r.dve(lambda e: e.reciprocal(scrow[0:1, 0:1], SC_SX[0:1, :]), inc=dv_s)
        r.dve(lambda e: e.tensor_scalar(scrow[0:1, 0:1], scrow[0:1, 0:1], 0.5, None,
                                        ALU.mult), inc=dv_s)
        r.dve(lambda e: e.reciprocal(scrow[0:1, 1:2], SC_SW[0:1, :]), inc=dv_s)
        r.dve(lambda e: e.tensor_scalar(scrow[0:1, 1:2], scrow[0:1, 1:2], 0.5, None,
                                        ALU.mult), inc=dv_s)
        r.dve(lambda e: e.tensor_copy(scrow[0:1, 2:3], SC_A[0:1, :]), inc=dv_s)
        sc_mark = dv_s.n
        r.wait('sp', dv_s, sc_mark)
        r.wait('sp', MS, MS.n)
        r.sp(lambda e: e.dma_start(out=scb_d[0:1, 0:4], in_=scrow[0:1, 0:4]), inc=MS, k=16)
        r.wait('sp', MS, MS.n)
        r.wait('sp', MS, MS.n)
        r.sp(lambda e: e.dma_start(out=vec[:, 22:26],
                                   in_=scb_d[0:1, 0:4].partition_broadcast(128)),
             inc=MS, k=16)
        sc_rep = MS.n

        def quant_tail(dst_slice):
            r.dve(lambda e: e.tensor_copy(fp8t, s2[:, 0:2048]), inc=dv_s)
            f8 = dv_s.n
            r.wait('act', dv_s, f8)
            r.act(lambda e: e.activation(s1[:, 0:2048], fp8t, AF.Identity,
                                         bias=0.0, scale=C_TWO), inc=ac_s)
            big = ac_s.n
            r.dve(lambda e: e.tensor_tensor(s3[:, 0:2048], s2[:, 0:2048],
                                            s2[:, 0:2048], ALU.mult), inc=dv_s)
            r.dve(lambda e: e.tensor_scalar(msku, s3[:, 0:2048], float(2.0 ** -12),
                                            None, ALU.is_lt), inc=dv_s)
            r.dve(lambda e: e.tensor_scalar(s3[:, 0:2048], s2[:, 0:2048], 1024.0,
                                            MAGIC, ALU.mult, ALU.add), inc=dv_s)
            r.dve(lambda e: e.tensor_scalar(s3[:, 0:2048], s3[:, 0:2048], MAGIC,
                                            1.0 / 512.0, ALU.subtract, ALU.mult), inc=dv_s)
            r.wait('dve', ac_s, big)
            r.dve(lambda e, d=dst_slice: e.select(d, msku, s3[:, 0:2048],
                                                  s1[:, 0:2048], add_drain=True), inc=dv_s)
            return ac_s.n, dv_s.n

        r.wait('dve', XSTP[0], XSTP[0].n)
        r.wait('dve', XSTP[1], XSTP[1].n)
        r.wait('dve', pe_s, pd_pe)
        qw_marks = []
        qw_act = 0
        for dc in range(NEC):
            half = xfm2[:, (dc % 2) * 2048:((dc % 2) + 1) * 2048]
            if dc >= 2:
                r.wait('sp', dv_s, qw_marks[dc - 2])
            else:
                r.wait('sp', dv_s, amaxw_done)
            wp_ = WP[dc % 2]
            r.sp(lambda e, dc=dc, b=half: e.dma_start(out=b, in_=wf_r[dc]),
                 inc=wp_, k=16)
            ld = wp_.n
            r.wait('dve', wp_, ld)
            r.wait('dve', MS, sc_rep)
            r.dve(lambda e, b=half: e.tensor_scalar(s2[:, 0:2048], b, HRSW, None,
                                                    ALU.mult), inc=dv_s)
            qw_act, mrk = quant_tail(pool_b[:, dc * 2048:(dc + 1) * 2048])
            qw_marks.append(mrk)
        qw_all = dv_s.n

        r.wait('sp', dv_s, qw_all)
        r.wait('sp', ac_s, qw_act)
        r.wait('sp', MS, MS.n)
        r.sp(lambda e: e.dma_start(out=s1[0:1, 0:2048], in_=bias_r), inc=MS, k=16)
        b_ld = MS.n
        r.wait('dve', MS, b_ld)
        r.dve(lambda e: e.tensor_scalar(s2[0:1, 0:2048], s1[0:1, 0:2048],
                                        RAVEC[0:1, :], None, ALU.mult), inc=dv_s)
        r.dve(lambda e: e.tensor_copy(BIAS_HI, s2[0:1, 0:2048]), inc=dv_s)
        r.dve(lambda e: e.tensor_copy(s3[0:1, 0:2048], BIAS_HI), inc=dv_s)
        r.dve(lambda e: e.tensor_tensor(s2[0:1, 0:2048], s2[0:1, 0:2048],
                                        s3[0:1, 0:2048], ALU.subtract), inc=dv_s)
        r.dve(lambda e: e.tensor_copy(BIAS_LO, s2[0:1, 0:2048]), inc=dv_s)
        bias_mark = dv_s.n

        e3_sp_marks = []
        eq_marks = []
        r.wait('dve', MS, sc_rep)
        for t in range(NCHUNK):
            cs = t * 128
            xb = xst[:, 0:2048] if t % 2 == 0 else wpf
            xp = XP[t % 2]
            if t >= 2:
                r.wait('sp', dv_s, eq_marks[t - 2])
            else:
                r.wait('sp', dv_s, pd_dve)
                r.wait('sp', XSTP[0], XSTP[0].n)
                r.wait('sp', XSTP[1], XSTP[1].n)
            r.sp(lambda e, t=t, b=xb: e.dma_start(
                out=b[:, 0:2048], in_=x1_d[t * 128:(t + 1) * 128, :]), inc=xp, k=16)
            ld = xp.n
            r.wait('dve', xp, ld)
            r.dve(lambda e, b=xb: e.tensor_reduce(T_SUM, b[:, 0:2048], AX.X, ALU.add),
                  inc=dv_s)
            r.dve(lambda e: e.tensor_scalar(T_NEGM, T_SUM, -1.0 / 2048.0, None,
                                            ALU.mult), inc=dv_s)
            nm3 = dv_s.n
            r.wait('act', xp, ld)
            r.wait('act', dv_s, nm3)
            r.act(lambda e, b=xb: e.activation(s1[:, 0:2048], b[:, 0:2048], AF.Square,
                                               accum_out=T_SSQ), inc=ac_s)
            sq3b = ac_s.n
            r.wait('dve', ac_s, sq3b)
            r.dve(lambda e: e.tensor_scalar(T_MSQ, T_NEGM, T_NEGM, None, ALU.mult), inc=dv_s)
            r.dve(lambda e: e.tensor_scalar(T_T0, T_SSQ, 1.0 / 2048.0, None, ALU.mult), inc=dv_s)
            r.dve(lambda e: e.tensor_tensor(T_T0, T_T0, T_MSQ, ALU.subtract), inc=dv_s)
            v3b = dv_s.n
            r.wait('act', dv_s, v3b)
            r.act(lambda e: e.activation(T_T0, T_T0, AF.Ln, bias=C_EPSLN, scale=1.0),
                  inc=ac_s)
            r.act(lambda e: e.activation(T_RSTD, T_T0, AF.Exp, bias=0.0,
                                         scale=C_NEGHALF), inc=ac_s)
            r3b = ac_s.n
            r.wait('dve', ac_s, r3b)
            r.dve(lambda e: e.tensor_tensor(T_T0, T_RSTD, HRSX, ALU.mult), inc=dv_s)
            r.dve(lambda e, b=xb: e.tensor_scalar(s2[:, 0:2048], b[:, 0:2048], T_NEGM,
                                                  T_T0, ALU.add, ALU.mult), inc=dv_s)
            _, qx_mark = quant_tail(SQ)
            r.wait('pe', dv_s, qx_mark)
            trm = {0: 0, 1: 0}
            for ec in range(NEC):
                pt = pTa if ec % 2 == 0 else pTb
                if trm[ec % 2]:
                    r.wait('pe', dv_s, trm[ec % 2])
                r.pe(lambda e, ec=ec, pt=pt: e.transpose(
                    pt[:, 0:128], SQ[:, ec * 128:(ec + 1) * 128], idb[:]), inc=pe_s)
                r.wait('dve', pe_s, pe_s.n)
                r.dve(lambda e, ec=ec, pt=pt, cs=cs: e.tensor_copy(
                    pool_a[:, ec * 2048 + cs: ec * 2048 + cs + 128], pt[:, 0:128]),
                    inc=dv_s)
                trm[ec % 2] = dv_s.n
            eq_marks.append(dv_s.n)
        eq_all = dv_s.n

        em_bankfree = [0, 0]
        em_out_marks = []
        r.wait('pe', dv_s, max(eq_all, qw_all, bias_mark))
        for t in range(NCHUNK):
            cs = t * 128
            xb = xst[:, 0:2048] if t % 2 == 0 else wpf
            xp = XP[t % 2]
            if t >= 2:
                r.wait('sp', dv_s, em_out_marks[t - 2])
            else:
                r.wait('sp', dv_s, eq_all)
            r.sp(lambda e, t=t, b=xb: e.dma_start(
                out=b[:, 0:2048], in_=x1_d[t * 128:(t + 1) * 128, :]), inc=xp, k=16)
            x_ld = xp.n
            for ng in range(4):
                bk = banks4[ng % 2]
                if em_bankfree[ng % 2]:
                    r.wait('pe', ac_s, em_bankfree[ng % 2])
                for ec in range(NEC):
                    r.pe(lambda e, ec=ec, ng=ng, cs=cs, bk=bk: e.matmul(
                        bk[:, 0:512],
                        pool_a[:, ec * 2048 + cs: ec * 2048 + cs + 128],
                        pool_b[:, ec * 2048 + ng * 512: ec * 2048 + (ng + 1) * 512],
                        start=(ec == 0), stop=False), inc=None)
                r.pe(lambda e, ng=ng, bk=bk: e.matmul(
                    bk[:, 0:512], onesb[0:1, 0:128],
                    BIAS_HI[0:1, ng * 512:(ng + 1) * 512],
                    start=False, stop=False), inc=None)
                r.pe(lambda e, ng=ng, bk=bk: e.matmul(
                    bk[:, 0:512], onesb[0:1, 0:128],
                    BIAS_LO[0:1, ng * 512:(ng + 1) * 512],
                    start=False, stop=True), inc=pe_s)
                g_mm = pe_s.n
                r.wait('act', pe_s, g_mm)
                if ng == 0 and t >= 1:
                    r.wait('act', dv_s, em_out_marks[-1])
                r.act(lambda e, ng=ng, bk=bk: e.activation(
                    s1[:, ng * 512:(ng + 1) * 512], bk[:, 0:512], AF.Identity,
                    bias=0.0, scale=AVEC, accum_out=vec[:, 45 + ng:46 + ng]), inc=ac_s)
                em_bankfree[ng % 2] = ac_s.n
            e3_evac = ac_s.n
            r.wait('dve', ac_s, e3_evac)
            r.dve(lambda e: e.tensor_tensor(vec[:, 45:46], vec[:, 45:46],
                                            vec[:, 46:47], ALU.add), inc=dv_s)
            r.dve(lambda e: e.tensor_tensor(vec[:, 47:48], vec[:, 47:48],
                                            vec[:, 48:49], ALU.add), inc=dv_s)
            r.dve(lambda e: e.tensor_tensor(vec[:, 45:46], vec[:, 45:46],
                                            vec[:, 47:48], ALU.add), inc=dv_s)
            r.dve(lambda e: e.tensor_scalar(T_M4, vec[:, 45:46], -1.0 / 2048.0, None,
                                            ALU.mult), inc=dv_s)
            m4m = dv_s.n
            r.wait('act', dv_s, m4m)
            r.act(lambda e: e.activation(s2[:, 0:2048], s1[:, 0:2048], AF.Square,
                                         accum_out=T_SSQ), inc=ac_s)
            sq4 = ac_s.n
            r.wait('dve', ac_s, sq4)
            r.dve(lambda e: e.tensor_scalar(T_MSQ, T_M4, T_M4, None, ALU.mult), inc=dv_s)
            r.dve(lambda e: e.tensor_scalar(T_V4, T_SSQ, 1.0 / 2048.0, None, ALU.mult), inc=dv_s)
            r.dve(lambda e: e.tensor_tensor(T_V4, T_V4, T_MSQ, ALU.subtract), inc=dv_s)
            v4m = dv_s.n
            r.wait('act', dv_s, v4m)
            r.act(lambda e: e.activation(T_V4, T_V4, AF.Ln, bias=C_EPSLN, scale=1.0),
                  inc=ac_s)
            r.act(lambda e: e.activation(T_RSTD, T_V4, AF.Exp, bias=0.0,
                                         scale=C_NEGHALF), inc=ac_s)
            r4m = ac_s.n
            r.wait('dve', ac_s, r4m)
            r.wait('dve', xp, x_ld)
            if t >= 1:
                r.wait('dve', OUTP[(t - 1) % 2], e3_sp_marks[-1])
            r.dve(lambda e: e.tensor_scalar(s2[:, 0:2048], s1[:, 0:2048], T_M4,
                                            T_RSTD, ALU.add, ALU.mult), inc=dv_s)
            r.dve(lambda e, b=xb: e.tensor_tensor(s3[:, 0:2048], s2[:, 0:2048],
                                                  b[:, 0:2048], ALU.add), inc=dv_s)
            out_mark = dv_s.n
            em_out_marks.append(out_mark)
            r.wait('sp', dv_s, out_mark)
            outp = OUTP[t % 2]
            r.sp(lambda e, t=t: e.dma_start(out=out_d[t * 128:(t + 1) * 128, :],
                                            in_=s3[:, 0:2048]), inc=outp, k=16)
            e3_sp_marks.append(outp.n)

        r.wait('sp', OUTP[0], OUTP[0].n)
        r.wait('sp', OUTP[1], OUTP[1].n)
        r.wait('pool', OUTP[0], OUTP[0].n)
        r.wait('pool', OUTP[1], OUTP[1].n)

        @block.tensor
        def _(t):
            r.replay('pe', t)

        @block.vector
        def _(v):
            r.replay('dve', v)

        @block.scalar
        def _(s):
            r.replay('act', s)

        @block.gpsimd
        def _(g):
            r.replay('pool', g)

        @block.sync
        def _(s):
            r.replay('sp', s)

    return nc


def _stage_weight_qkv(w):
    t = w.reshape(H, 128, NEC, 128)
    return np.ascontiguousarray(t.transpose(0, 3, 2, 1)).reshape(H, 128, D).astype(np.float32)


def _stage_weight_out(w):
    return np.ascontiguousarray(w.T.reshape(NEC, 128, D)).astype(np.float32)


def kernel(L0_attn_hidden_states, L0_attn_memories, L0_attn_memory_norms,
           L0_attn_w_q, L0_attn_w_k, L0_attn_w_v, L0_attn_w_o, L0_attn_gate,
           L0_ffn_weight, L0_ffn_bias):
    x = np.ascontiguousarray(np.asarray(L0_attn_hidden_states, dtype=np.float32))
    mem = np.asarray(L0_attn_memories, dtype=np.float32)
    mn = np.asarray(L0_attn_memory_norms, dtype=np.float32)
    wq = np.asarray(L0_attn_w_q, dtype=np.float32)
    wk = np.asarray(L0_attn_w_k, dtype=np.float32)
    wv = np.asarray(L0_attn_w_v, dtype=np.float32)
    wo = np.asarray(L0_attn_w_o, dtype=np.float32)
    gate = np.asarray(L0_attn_gate, dtype=np.float32)
    wf = np.asarray(L0_ffn_weight, dtype=np.float32)
    bias = np.asarray(L0_ffn_bias, dtype=np.float32)

    wq_r = _stage_weight_qkv(wq)
    wk_r = _stage_weight_qkv(wk)
    wv_r = _stage_weight_qkv(wv)
    wo_r = _stage_weight_out(wo)
    wf_r = _stage_weight_out(wf)
    mem_r = np.ascontiguousarray(
        mem.transpose(1, 2, 0, 3).reshape(H, 128, M * 128)).astype(np.float32)
    mnT_r = np.ascontiguousarray(
        mn.transpose(2, 1, 0).reshape(128, H * M)).astype(np.float32)
    gate_r = gate.reshape(1, H).astype(np.float32)
    bias_r = bias.reshape(1, D).astype(np.float32)
    maskT_r = np.triu(np.ones((128, 128), dtype=ml_dtypes.bfloat16))
    idf_r = np.eye(128, dtype=np.float32)
    idb_r = np.eye(128, dtype=ml_dtypes.bfloat16)
    ones_r = np.ones((128, 128), dtype=ml_dtypes.bfloat16)

    nc = build_nc()
    in_maps = []
    for core in range(8):
        b, half = core // 2, core % 2
        xs = np.ascontiguousarray(x[b, half * TOK:(half + 1) * TOK, :])
        in_maps.append({
            "x_tm": xs,
            "x_fm": np.ascontiguousarray(xs.T),
            "wq_r": wq_r, "wk_r": wk_r, "wv_r": wv_r,
            "wo_r": wo_r, "wf_r": wf_r,
            "mem_r": mem_r, "mnT_r": mnT_r,
            "gate_r": gate_r, "bias_r": bias_r,
            "maskT_r": maskT_r, "idf_r": idf_r,
            "idb_r": idb_r, "ones_r": ones_r,
            "flag_r": np.full((1, 1), float(half), dtype=np.float32),
        })

    res = run_bass_kernel_spmd(nc, in_maps, core_ids=list(range(8)))
    out = np.empty((B, S, D), dtype=np.float32)
    for core in range(8):
        b, half = core // 2, core % 2
        out[b, half * TOK:(half + 1) * TOK, :] = res.results[core]["out"]
    return out


if __name__ == "__main__":
    rng = np.random.default_rng(0)
    ins = {
        "L0_attn_hidden_states": rng.standard_normal((B, S, D)).astype(np.float32),
        "L0_attn_memories": rng.standard_normal((M, H, HD, HD)).astype(np.float32),
        "L0_attn_memory_norms": rng.random((M, H, HD)).astype(np.float32),
        "L0_attn_w_q": (rng.standard_normal((D, D)) * 0.02).astype(np.float32),
        "L0_attn_w_k": (rng.standard_normal((D, D)) * 0.02).astype(np.float32),
        "L0_attn_w_v": (rng.standard_normal((D, D)) * 0.02).astype(np.float32),
        "L0_attn_w_o": (rng.standard_normal((D, D)) * 0.02).astype(np.float32),
        "L0_attn_gate": rng.standard_normal((H,)).astype(np.float32),
        "L0_ffn_weight": rng.standard_normal((D, D)).astype(np.float32),
        "L0_ffn_bias": rng.standard_normal((D,)).astype(np.float32),
    }
    y = kernel(**ins)
    print("ok", y.shape, y.dtype, float(np.abs(y).max()))


# version 09
# speedup vs baseline: 1.0443x; 1.0012x over previous
import contextlib

import numpy as np
import ml_dtypes

import concourse.bass as bass
import concourse.mybir as mybir
from concourse.bass_utils import run_bass_kernel_spmd

F32 = mybir.dt.float32
BF16 = mybir.dt.bfloat16
FP8 = mybir.dt.float8e4
U8 = mybir.dt.uint8
AF = mybir.ActivationFunctionType
ALU = mybir.AluOpType
AX = mybir.AxisListType

B, S, D = 4, 4096, 2048
H, HD = 16, 128
M = 4
NCHUNK = 16
NEC = 16
TOK = 2048
SCALE = np.float32(np.float32(HD) ** np.float32(-0.5))
EPS_A = 1e-6
EPS_LN = 1e-5
FP8_C = float(np.float32(448.0) * np.float32(0.8))
MAGIC = float(np.float32(1.5 * 2.0 ** 23))
RFP8_C = float(np.float32(1.0) / np.float32(448.0 * np.float32(0.8)))
NEG_BIG = -1e30
_SIM_ALL8 = None
_SIM_LOCAL_AMOUT = False


class Sem:
    def __init__(self, handle):
        self.h = handle
        self.n = 0


class Rec:

    def __init__(self):
        self.ops = []
        self.own = {}

    def _em(self, key, fn, inc=None, k=1):
        own = self.own.get(key)
        if own is not None and own.n > 0:
            self.ops.append((key, lambda e, h=own.h, v=own.n: e.wait_ge(h, v)))
        if inc is not None:
            inc.n += k
            h, kk = inc.h, k
            self.ops.append((key, lambda e, f=fn, h=h, kk=kk: f(e).then_inc(h, kk)))
        else:
            self.ops.append((key, fn))

    def pe(self, fn, inc=None, k=1):
        self._em('pe', fn, inc, k)

    def dve(self, fn, inc=None, k=1):
        self._em('dve', fn, inc, k)

    def act(self, fn, inc=None, k=1):
        self._em('act', fn, inc, k)

    def pool(self, fn, inc=None, k=1):
        self._em('pool', fn, inc, k)

    def sp(self, fn, inc=None, k=1):
        self._em('sp', fn, inc, k)

    def wait(self, key, sem, val=None):
        v = sem.n if val is None else val
        if v <= 0:
            return
        self.ops.append((key, lambda e, h=sem.h, v=v: e.wait_ge(h, v)))

    def replay(self, key, eng):
        for k, fn in self.ops:
            if k == key:
                fn(eng)


def build_nc():
    nc = bass.Bass("TRN2", target_bir_lowering=False, debug=False)

    def din(name, shape, dt=F32):
        return nc.dram_tensor(name, shape, dt, kind="ExternalInput").ap()

    x_tm = din("x_tm", [TOK, D])
    x_fm = din("x_fm", [D, TOK])
    wq_r = din("wq_r", [H, 128, D])
    wk_r = din("wk_r", [H, 128, D])
    wv_r = din("wv_r", [H, 128, D])
    wo_r = din("wo_r", [H, 128, D])
    wf_r = din("wf_r", [NEC, 128, D])
    mem_r = din("mem_r", [H, 128, M * 128])
    mnT_r = din("mnT_r", [128, H * M])
    gate_r = din("gate_r", [1, H])
    bias_r = din("bias_r", [1, D])
    maskT_r = din("maskT_r", [128, 128], BF16)
    idf_r = din("idf_r", [128, 128])
    flag_r = din("flag_r", [1, 1])
    idb_r = din("idb_r", [128, 128], BF16)
    ones_r = din("ones_r", [128, 128], BF16)

    out_d = nc.dram_tensor("out", [TOK, D], F32, kind="ExternalOutput").ap()

    x1_d = nc.dram_tensor("x1_d", [TOK, D], F32, kind="Internal").ap()
    statb_m = nc.dram_tensor("statb_m", [1, TOK], F32, kind="Internal").ap()
    statb_r = nc.dram_tensor("statb_r", [1, TOK], F32, kind="Internal").ap()
    gb_d = nc.dram_tensor("gb_d", [1, 2 * H], F32, kind="Internal").ap()
    wgb_d = nc.dram_tensor("wgb_d", [1, H * M], F32, kind="Internal").ap()
    scb_d = nc.dram_tensor("scb_d", [1, 8], F32, kind="Internal").ap()
    sta_in = [nc.dram_tensor(f"sta_in{h}", [128, 129], F32, kind="Internal").ap()
              for h in range(H)]
    sta_out = [nc.dram_tensor(f"sta_out{h}", [256, 129], F32, kind="Internal").ap()
               for h in range(H)]
    rel_in = nc.dram_tensor("rel_in", [4, 16], F32, kind="Internal").ap()
    rel_out = nc.dram_tensor("rel_out", [8, 16], F32, kind="Internal").ap()
    am_in = nc.dram_tensor("am_in", [1, 1], F32, kind="Internal").ap()
    am_out = nc.dram_tensor("am_out", [1, 1], F32, kind="Internal",
                            **({} if _SIM_LOCAL_AMOUT else {"addr_space": "Shared"})).ap()

    PAIRS = [[0, 1], [2, 3], [4, 5], [6, 7]]
    ALL8 = _SIM_ALL8 or [[0, 1, 2, 3, 4, 5, 6, 7]]

    with contextlib.ExitStack() as _st:
        pool_a = _st.enter_context(nc.sbuf_tensor("pool_a", [128, 32768], BF16))
        pool_b = _st.enter_context(nc.sbuf_tensor("pool_b", [128, 32768], BF16))
        wp = _st.enter_context(nc.sbuf_tensor("wp", [128, 13376], BF16))
        xst = _st.enter_context(nc.sbuf_tensor("xst", [128, 2048], F32))
        xfm2 = _st.enter_context(nc.sbuf_tensor("xfm2", [128, 4096], F32))
        s1 = _st.enter_context(nc.sbuf_tensor("s1", [128, 2048], F32))
        s2 = _st.enter_context(nc.sbuf_tensor("s2", [128, 2048], F32))
        s3 = _st.enter_context(nc.sbuf_tensor("s3", [128, 2048], F32))
        vec = _st.enter_context(nc.sbuf_tensor("vec", [128, 56], F32))
        stats_m = _st.enter_context(nc.sbuf_tensor("stats_m", [128, 16], F32))
        stats_r = _st.enter_context(nc.sbuf_tensor("stats_r", [128, 16], F32))
        stat_t = _st.enter_context(nc.sbuf_tensor("stat_t", [16, 128], F32))
        mem_tm = _st.enter_context(nc.sbuf_tensor("mem_tm", [128, 128], F32))
        loc_tm = _st.enter_context(nc.sbuf_tensor("loc_tm", [128, 128], F32))
        maskT = _st.enter_context(nc.sbuf_tensor("maskT", [128, 128], BF16))
        idf = _st.enter_context(nc.sbuf_tensor("idf", [128, 128], F32))
        Wb = _st.enter_context(nc.sbuf_tensor("Wb", [128, 64], F32))
        Wb4 = _st.enter_context(nc.sbuf_tensor("Wb4", [128, 4], F32))
        gWb = _st.enter_context(nc.sbuf_tensor("gWb", [128, 16], F32))
        gLb = _st.enter_context(nc.sbuf_tensor("gLb", [128, 16], F32))
        flagv = _st.enter_context(nc.sbuf_tensor("flagv", [128, 1], F32))
        relp = _st.enter_context(nc.sbuf_tensor("relp", [4, 16], F32))
        relB = _st.enter_context(nc.sbuf_tensor("relB", [4, 16], F32))
        relT = _st.enter_context(nc.sbuf_tensor("relT", [16, 4], F32))
        relT2 = _st.enter_context(nc.sbuf_tensor("relT2", [16, 4], F32))
        grow = _st.enter_context(nc.sbuf_tensor("grow", [1, 2 * H], F32))
        wgrow = _st.enter_context(nc.sbuf_tensor("wgrow", [16, 4], F32))
        mrow = _st.enter_context(nc.sbuf_tensor("mrow", [1, 4], F32))
        mrow_b = _st.enter_context(nc.sbuf_tensor("mrow_b", [1, 4], BF16))
        scrow = _st.enter_context(nc.sbuf_tensor("scrow", [1, 8], F32))
        idb = _st.enter_context(nc.sbuf_tensor("idb", [128, 128], BF16))
        onesb = _st.enter_context(nc.sbuf_tensor("onesb", [128, 128], BF16))
        outc_b = _st.enter_context(nc.sbuf_tensor("outc_b", [128, 128], BF16))
        scb16 = _st.enter_context(nc.sbuf_tensor("scb16", [128, 128], BF16))
        sxln_b = _st.enter_context(nc.sbuf_tensor("sxln_b", [128, 16], BF16))
        ph_b = _st.enter_context(nc.sbuf_tensor("ph_b", [128, 1], BF16))
        mnT_b = _st.enter_context(nc.sbuf_tensor("mnT_b", [128, 64], BF16))
        pA0 = _st.enter_context(nc.psum_tensor("pA0", [128, 512], F32))
        pA1 = _st.enter_context(nc.psum_tensor("pA1", [128, 512], F32))
        pB0 = _st.enter_context(nc.psum_tensor("pB0", [128, 512], F32))
        pB1 = _st.enter_context(nc.psum_tensor("pB1", [128, 512], F32))
        pC0 = _st.enter_context(nc.psum_tensor("pC0", [128, 512], F32))
        pC1 = _st.enter_context(nc.psum_tensor("pC1", [128, 512], F32))
        pTa = _st.enter_context(nc.psum_tensor("pTa", [128, 1024], BF16))
        pTb = _st.enter_context(nc.psum_tensor("pTb", [128, 1024], BF16))
        _sp = _st.enter_context(nc.semaphore("m_sp"))
        _gq = _st.enter_context(nc.semaphore("m_gq"))
        _cc = _st.enter_context(nc.semaphore("m_cc"))
        _pe = _st.enter_context(nc.semaphore("m_pe"))
        _dve = _st.enter_context(nc.semaphore("m_dve"))
        _act = _st.enter_context(nc.semaphore("m_act"))
        _pool = _st.enter_context(nc.semaphore("m_pool"))
        block = _st.enter_context(nc.Block())

        r = Rec()
        cc_s = Sem(_cc)
        pe_s, dv_s, ac_s, po_s = Sem(_pe), Sem(_dve), Sem(_act), Sem(_pool)
        r.own = {'dve': dv_s, 'act': ac_s, 'pool': po_s}
        MS = Sem(_st.enter_context(nc.semaphore("m_ms")))
        XP = [Sem(_st.enter_context(nc.semaphore("m_x0"))),
              Sem(_st.enter_context(nc.semaphore("m_x1")))]
        WP = [Sem(_st.enter_context(nc.semaphore("m_w0"))),
              Sem(_st.enter_context(nc.semaphore("m_w1")))]
        XSTP = [Sem(_st.enter_context(nc.semaphore("m_xs0"))),
                Sem(_st.enter_context(nc.semaphore("m_xs1")))]
        OUTP = [Sem(_st.enter_context(nc.semaphore("m_o0"))),
                Sem(_st.enter_context(nc.semaphore("m_o1")))]
        GWQ = Sem(_st.enter_context(nc.semaphore("m_gwq")))
        GWK = Sem(_st.enter_context(nc.semaphore("m_gwk")))
        GWV = Sem(_st.enter_context(nc.semaphore("m_gwv")))
        GME = Sem(_st.enter_context(nc.semaphore("m_gme")))
        GWO = Sem(_st.enter_context(nc.semaphore("m_gwo")))
        GST = Sem(_st.enter_context(nc.semaphore("m_gst")))
        GRC = Sem(_st.enter_context(nc.semaphore("m_grc")))
        GMC = Sem(_st.enter_context(nc.semaphore("m_gmc")))

        def V(i):
            return vec[:, i:i + 1]
        C_EPSLN, C_NEGHALF, C_TWO, C_NEGEPSA, C_NEGONE = V(0), V(1), V(2), V(3), V(4)
        T_SUM, T_SSQ, T_NEGM, T_RSTD, T_MSQ, T_T0 = V(5), V(6), V(7), V(8), V(9), V(10)
        T_R, T_DEN = V(11), V(12)
        AMX_ACC, T_DEV = V(13), V(14)
        AW_ACC, AW_T = V(15), V(16)
        SC_AX, SC_AW, SC_SX, SC_SW, SC_A = V(17), V(18), V(19), V(20), V(21)
        HRSX, HRSW, AVEC, RAVEC = V(22), V(23), V(24), V(25)
        T_M4, T_V4 = V(26), V(27)

        SQ = wp[:, 0:2048]
        KPF = wp[:, 2048:4096]
        KPT = wp[:, 4096:6144]
        VAUG = wp[:, 6144:8208]
        WKH = wp[:, 8208:10256]
        WVH = wp[:, 10256:12304]
        MEMA = [wp[:, 12304:12816], wp[:, 12816:13328]]
        BIAS_HI = wp[0:1, 2048:4096]
        BIAS_LO = wp[0:1, 4096:6144]
        wpf = wp[:, 8208:12304].bitcast(F32)
        msku = wp[:, 12304:13328].bitcast(U8)
        t2_tm = s3[:, 1664:1792]
        recvf = s2[:, 0:129]
        totf = s2[:, 256:385]
        state_b = s2[:, 512:577].bitcast(BF16)[:, 0:129]
        partner_b = s2[:, 640:705].bitcast(BF16)[:, 0:129]
        fp8t = wp[:, 6144:7168].bitcast(FP8)

        r.wait('sp', MS, MS.n)
        r.sp(lambda e: e.dma_start(out=maskT[:], in_=maskT_r), inc=MS, k=16)
        r.wait('sp', MS, MS.n)
        r.sp(lambda e: e.dma_start(out=idf[:], in_=idf_r), inc=MS, k=16)
        r.wait('sp', MS, MS.n)
        r.sp(lambda e: e.dma_start(out=idb[:], in_=idb_r), inc=MS, k=16)
        r.wait('sp', MS, MS.n)
        r.sp(lambda e: e.dma_start(out=onesb[:], in_=ones_r), inc=MS, k=16)
        r.wait('sp', MS, MS.n)
        r.sp(lambda e: e.dma_start(out=grow[:, 0:16], in_=gate_r), inc=MS, k=16)
        r.wait('sp', MS, MS.n)
        r.sp(lambda e: e.dma_start(out=flagv[:],
                                   in_=flag_r[0:1, 0:1].partition_broadcast(128)),
             inc=MS, k=16)
        con_loads = MS.n

        for val, ap in [(EPS_LN, C_EPSLN), (-0.5, C_NEGHALF), (2.0, C_TWO),
                        (-EPS_A, C_NEGEPSA), (-1.0, C_NEGONE), (0.0, AMX_ACC),
                        (0.0, AW_ACC)]:
            r.dve(lambda e, v=val, a=ap: e.memset(a, v), inc=dv_s)
        r.dve(lambda e: e.memset(VAUG, 1.0), inc=dv_s)
        r.dve(lambda e: e.memset(state_b, 0.0), inc=dv_s)
        memset_done = dv_s.n

        r.wait('act', MS, con_loads)
        r.wait('act', dv_s, memset_done)
        r.act(lambda e: e.activation(grow[0:1, 16:32], grow[0:1, 0:16], AF.Exp,
                                     bias=0.0, scale=C_NEGONE[0:1, :]), inc=ac_s)
        g_exp = ac_s.n
        r.wait('dve', ac_s, g_exp)
        r.dve(lambda e: e.tensor_scalar(grow[0:1, 16:32], grow[0:1, 16:32], 1.0, None,
                                        ALU.add), inc=dv_s)
        r.dve(lambda e: e.reciprocal(grow[0:1, 0:16], grow[0:1, 16:32]), inc=dv_s)
        r.dve(lambda e: e.tensor_scalar(grow[0:1, 16:32], grow[0:1, 0:16], 1.0, -1.0,
                                        ALU.subtract, ALU.mult), inc=dv_s)
        g_done = dv_s.n
        r.wait('sp', dv_s, g_done)
        r.wait('sp', MS, MS.n)
        r.sp(lambda e: e.dma_start(out=gb_d, in_=grow[:]), inc=MS, k=16)
        gb_out = MS.n
        r.wait('sp', MS, gb_out)
        r.wait('sp', MS, MS.n)
        r.sp(lambda e: e.dma_start(out=gWb[:],
                                   in_=gb_d[0:1, 0:16].partition_broadcast(128)),
             inc=MS, k=16)
        r.wait('sp', MS, MS.n)
        r.sp(lambda e: e.dma_start(out=gLb[:],
                                   in_=gb_d[0:1, 16:32].partition_broadcast(128)),
             inc=MS, k=16)
        gl_rep = MS.n

        pa_marks = []
        for t in range(NCHUNK):
            buf = xst if t % 2 == 0 else s1
            if t >= 2:
                r.wait('sp', ac_s, pa_marks[t - 2][0])
                r.wait('sp', dv_s, pa_marks[t - 2][1])
            xp = XP[t % 2]
            r.sp(lambda e, t=t, b=buf: e.dma_start(
                out=b[:], in_=x_tm[t * 128:(t + 1) * 128, :]), inc=xp, k=16)
            ld = xp.n
            r.wait('dve', xp, ld)
            r.dve(lambda e, b=buf: e.tensor_reduce(T_SUM, b[:], AX.X, ALU.add), inc=dv_s)
            red_mark = dv_s.n
            r.wait('act', xp, ld)
            r.act(lambda e, b=buf: e.activation(s2[:, 0:2048], b[:], AF.Square,
                                                accum_out=T_SSQ), inc=ac_s)
            sq_mark = ac_s.n
            r.dve(lambda e: e.tensor_scalar(T_NEGM, T_SUM, -1.0 / 2048.0, None,
                                            ALU.mult), inc=dv_s)
            r.wait('dve', ac_s, sq_mark)
            r.dve(lambda e: e.tensor_scalar(T_MSQ, T_NEGM, T_NEGM, None, ALU.mult), inc=dv_s)
            r.dve(lambda e: e.tensor_scalar(T_T0, T_SSQ, 1.0 / 2048.0, None, ALU.mult), inc=dv_s)
            r.dve(lambda e: e.tensor_tensor(T_T0, T_T0, T_MSQ, ALU.subtract), inc=dv_s)
            var_mark = dv_s.n
            r.wait('act', dv_s, var_mark)
            r.act(lambda e: e.activation(T_T0, T_T0, AF.Ln, bias=C_EPSLN, scale=1.0), inc=ac_s)
            r.act(lambda e: e.activation(T_RSTD, T_T0, AF.Exp, bias=0.0,
                                         scale=C_NEGHALF), inc=ac_s)
            rstd_mark = ac_s.n
            r.dve(lambda e, t=t: e.tensor_copy(stats_m[:, t:t + 1], T_NEGM), inc=dv_s)
            r.wait('dve', ac_s, rstd_mark)
            r.dve(lambda e, t=t: e.tensor_copy(stats_r[:, t:t + 1], T_RSTD), inc=dv_s)
            pa_marks.append((sq_mark, dv_s.n))
        pa_dve = dv_s.n
        pa_act = ac_s.n

        r.wait('pe', dv_s, pa_dve)
        r.wait('pe', MS, con_loads)
        r.pe(lambda e: e.transpose(pA0[0:16, 0:128], stats_m[:], idf[:]), inc=pe_s)
        tr1 = pe_s.n
        r.wait('dve', pe_s, tr1)
        r.dve(lambda e: e.tensor_copy(stat_t[:], pA0[0:16, 0:128]), inc=dv_s)
        ev1 = dv_s.n
        r.wait('sp', dv_s, ev1)
        r.wait('sp', MS, MS.n)
        r.sp(lambda e: e.dma_start(
            out=statb_m.rearrange("a (p c) -> (a p) c", c=128), in_=stat_t[:]),
            inc=MS, k=16)
        st1 = MS.n
        r.wait('pe', dv_s, ev1)
        r.pe(lambda e: e.transpose(pA0[0:16, 0:128], stats_r[:], idf[:]), inc=pe_s)
        tr2 = pe_s.n
        r.wait('dve', pe_s, tr2)
        r.wait('dve', MS, st1)
        r.dve(lambda e: e.tensor_copy(stat_t[:], pA0[0:16, 0:128]), inc=dv_s)
        ev2 = dv_s.n
        r.wait('sp', dv_s, ev2)
        r.wait('sp', MS, MS.n)
        r.sp(lambda e: e.dma_start(
            out=statb_r.rearrange("a (p c) -> (a p) c", c=128), in_=stat_t[:]),
            inc=MS, k=16)
        st2 = MS.n
        r.wait('sp', MS, st2)
        r.wait('sp', dv_s, pa_dve)
        r.wait('sp', ac_s, pa_act)
        r.wait('sp', MS, MS.n)
        r.sp(lambda e: e.dma_start(out=s1[:],
                                   in_=statb_m[0:1, :].partition_broadcast(128)),
             inc=MS, k=16)
        r.wait('sp', MS, MS.n)
        r.sp(lambda e: e.dma_start(out=s2[:],
                                   in_=statb_r[0:1, :].partition_broadcast(128)),
             inc=MS, k=16)
        bc_done = MS.n

        pb_marks = []
        for ec in range(NEC):
            half = xfm2[:, (ec % 2) * 2048:((ec % 2) + 1) * 2048]
            if ec >= 2:
                r.wait('sp', dv_s, pb_marks[ec - 2])
            wp_ = WP[ec % 2]
            r.sp(lambda e, ec=ec, b=half: e.dma_start(
                out=b, in_=x_fm[ec * 128:(ec + 1) * 128, :]), inc=wp_, k=16)
            ld = wp_.n
            r.wait('dve', wp_, ld)
            r.wait('dve', MS, bc_done)
            r.dve(lambda e, b=half: e.tensor_tensor(s3[:, 0:2048], b, s1[:], ALU.add),
                  inc=dv_s)
            acc_col = vec[:, 28 + (ec % 2):29 + (ec % 2)]
            r.dve(lambda e, ec=ec, ac=acc_col: e.scalar_tensor_tensor(
                pool_a[:, ec * 2048:(ec + 1) * 2048], s3[:, 0:2048], 1.0, s2[:],
                ALU.mult, ALU.mult, accum_out=ac), inc=dv_s)
            r.dve(lambda e, ec=ec, ac=acc_col: e.tensor_copy(sxln_b[:, ec:ec + 1], ac),
                  inc=dv_s)
            pb_marks.append(dv_s.n)
        pb_dve = dv_s.n

        for h in range(H):
            r.pool(lambda e, h=h: e.dma_start(
                out=pool_b[:, h * 2048:(h + 1) * 2048], in_=wq_r[h]), inc=GWQ, k=16)
        r.pool(lambda e: e.dma_start(out=mnT_b[:], in_=mnT_r), inc=GWQ, k=16)
        wq_all_ld = GWQ.n

        r.wait('pe', dv_s, pb_dve)
        r.wait('pe', GWQ, wq_all_ld)
        ph_ev = 0
        for h in range(H):
            if ph_ev:
                r.wait('pe', dv_s, ph_ev)
            for ec in range(NEC):
                r.pe(lambda e, h=h, ec=ec: e.matmul(
                    pA0[0:128, 0:1],
                    pool_b[:, h * 2048 + ec * 128: h * 2048 + (ec + 1) * 128],
                    sxln_b[:, ec:ec + 1],
                    start=(ec == 0), stop=(ec == NEC - 1)),
                    inc=(pe_s if ec == NEC - 1 else None))
            ph_mm = pe_s.n
            r.wait('dve', pe_s, ph_mm)
            r.dve(lambda e: e.tensor_copy(ph_b[:], pA0[0:128, 0:1]), inc=dv_s)
            ph_ev = dv_s.n
            r.wait('pe', dv_s, ph_ev)
            r.pe(lambda e, h=h: e.matmul(pA1[0:4, h:h + 1],
                                         mnT_b[:, h * 4:(h + 1) * 4], ph_b[:],
                                         start=True, stop=True), inc=pe_s)
        rel_mm = pe_s.n
        r.wait('dve', pe_s, rel_mm)
        r.dve(lambda e: e.tensor_copy(relp[:], pA1[0:4, 0:16]), inc=dv_s)
        relp_ev = dv_s.n
        r.wait('pool', dv_s, relp_ev)
        r.pool(lambda e: e.dma_start(out=rel_in, in_=relp[:]), inc=GMC, k=16)
        r.wait('pool', GMC, GMC.n)
        r.pool(lambda e: e.collective_compute(
            "AllGather", ALU.bypass, replica_groups=PAIRS,
            ins=[rel_in.opt()], outs=[rel_out.opt()]), inc=cc_s, k=1)
        r.wait('pool', cc_s, cc_s.n)
        r.pool(lambda e: e.dma_start(out=relp[:], in_=rel_out[0:4, :]), inc=GMC, k=16)
        r.pool(lambda e: e.dma_start(out=relB[:], in_=rel_out[4:8, :]), inc=GMC, k=16)
        rel_back = GMC.n
        r.wait('dve', GMC, rel_back)
        r.dve(lambda e: e.tensor_tensor(relp[:], relp[:], relB[:], ALU.add), inc=dv_s)
        rel_sum = dv_s.n

        r.wait('pe', dv_s, relp_ev)
        for h in range(H):
            r.pe(lambda e, h=h: e.matmul(pB0[0:4, 0:1], mnT_b[:, h * 4:(h + 1) * 4],
                                         onesb[:, 0:1], start=(h == 0),
                                         stop=(h == H - 1)),
                 inc=(pe_s if h == H - 1 else None))
        mn_mm = pe_s.n
        r.wait('act', pe_s, mn_mm)
        r.act(lambda e: e.activation(vec[0:4, 30:31], pB0[0:4, 0:1], AF.Sign,
                                     bias=C_NEGEPSA[0:4, :], scale=1.0), inc=ac_s)
        sign_mark = ac_s.n
        r.wait('dve', ac_s, sign_mark)
        r.dve(lambda e: e.tensor_scalar(vec[0:4, 30:31], vec[0:4, 30:31], 1.0, 0.5,
                                        ALU.add, ALU.mult), inc=dv_s)
        r.dve(lambda e: e.tensor_scalar(vec[0:4, 31:32], vec[0:4, 30:31], 1.0, NEG_BIG,
                                        ALU.subtract, ALU.mult), inc=dv_s)
        r.wait('dve', dv_s, rel_sum)
        r.dve(lambda e: e.tensor_scalar(relp[:], relp[:], float(SCALE / np.float32(S)),
                                        vec[0:4, 31:32], ALU.mult, ALU.add), inc=dv_s)
        reladj = dv_s.n
        r.wait('pe', dv_s, reladj)
        r.pe(lambda e: e.transpose(pB0[0:16, 0:4], relp[:], idf[0:4, 0:4]), inc=pe_s)
        r.pe(lambda e: e.transpose(pB0[0:1, 16:20], vec[0:4, 30:31], idf[0:4, 0:4]),
             inc=pe_s)
        trel = pe_s.n
        r.wait('dve', pe_s, trel)
        r.dve(lambda e: e.tensor_copy(relT[:], pB0[0:16, 0:4]), inc=dv_s)
        r.dve(lambda e: e.tensor_copy(mrow[:], pB0[0:1, 16:20]), inc=dv_s)
        r.dve(lambda e: e.tensor_reduce(vec[0:16, 32:33], relT[:], AX.X, ALU.max), inc=dv_s)
        r.dve(lambda e: e.tensor_scalar(relT[:], relT[:], vec[0:16, 32:33], None,
                                        ALU.subtract), inc=dv_s)
        sm1 = dv_s.n
        r.wait('act', dv_s, sm1)
        r.act(lambda e: e.activation(relT2[:], relT[:], AF.Exp, bias=0.0, scale=1.0),
              inc=ac_s)
        sm2 = ac_s.n
        r.wait('dve', ac_s, sm2)
        r.dve(lambda e: e.tensor_reduce(vec[0:16, 32:33], relT2[:], AX.X, ALU.add), inc=dv_s)
        r.dve(lambda e: e.reciprocal(vec[0:16, 33:34], vec[0:16, 32:33]), inc=dv_s)
        r.dve(lambda e: e.tensor_scalar(relT2[:], relT2[:], vec[0:16, 33:34], None,
                                        ALU.mult), inc=dv_s)
        r.dve(lambda e: e.tensor_copy(mrow_b[:], mrow[:]), inc=dv_s)
        mrb = dv_s.n
        r.wait('pe', dv_s, mrb)
        r.pe(lambda e: e.matmul(pB1[0:16, 0:4], onesb[0:1, 0:16], mrow_b[:],
                                start=True, stop=True), inc=pe_s)
        mi_mm = pe_s.n
        r.wait('dve', pe_s, mi_mm)
        r.dve(lambda e: e.tensor_tensor(wgrow[:], relT2[:], pB1[0:16, 0:4], ALU.mult),
              inc=dv_s)
        wg_done = dv_s.n
        r.wait('sp', dv_s, wg_done)
        r.wait('sp', MS, MS.n)
        r.sp(lambda e: e.dma_start(
            out=wgb_d.rearrange("a (p c) -> (a p) c", c=4), in_=wgrow[:]),
            inc=MS, k=16)
        wg_out = MS.n
        r.wait('sp', MS, wg_out)
        r.wait('sp', MS, MS.n)
        r.sp(lambda e: e.dma_start(out=Wb[:],
                                   in_=wgb_d[0:1, :].partition_broadcast(128)),
             inc=MS, k=16)
        wb_rep = MS.n

        wk_marks, wv_marks, mem_marks = {}, {}, {}

        def prefetch_head(h, wait_pe=0):
            if wait_pe:
                r.wait('pool', pe_s, wait_pe)
            r.wait('pool', GWK, GWK.n)
            r.wait('pool', GWV, GWV.n)
            r.wait('pool', GME, GME.n)
            r.pool(lambda e, h=h: e.dma_start(out=WKH, in_=wk_r[h]), inc=GWK, k=16)
            wk_marks[h] = GWK.n
            r.pool(lambda e, h=h: e.dma_start(out=WVH, in_=wv_r[h]), inc=GWV, k=16)
            wv_marks[h] = GWV.n
            r.pool(lambda e, h=h: e.dma_start(out=MEMA[h % 2], in_=mem_r[h]),
                   inc=GME, k=16)
            mem_marks[h] = GME.n

        prefetch_head(0)

        last_xln_use_pe = 0
        wb4_done = 0

        def elu_evac(psum, dst_slice, n):
            r.dve(lambda e, p=psum, n=n: e.tensor_scalar(
                s3[:, 0:n], p[:, 0:n], 0.0, None, ALU.min), inc=dv_s)
            t0 = dv_s.n
            r.wait('act', dv_s, t0)
            r.act(lambda e, n=n: e.activation(s3[:, 1024:1024 + n], s3[:, 0:n], AF.Exp,
                                              bias=0.0, scale=1.0), inc=ac_s)
            ee = ac_s.n
            r.wait('dve', ac_s, ee)
            r.dve(lambda e, p=psum, d=dst_slice, n=n: e.scalar_tensor_tensor(
                d, p[:, 0:n], 0.0, s3[:, 1024:1024 + n], ALU.max, ALU.add), inc=dv_s)
            return dv_s.n

        for h in range(H):
            hs = h * 2048
            bank_mark = {0: 0, 1: 0}

            r.wait('pe', ac_s, ac_s.n)
            r.wait('dve', MS, gl_rep)
            for g in range(4):
                pp = pA0 if g % 2 == 0 else pA1
                if bank_mark[g % 2]:
                    r.wait('pe', dv_s, bank_mark[g % 2])
                for ec in range(NEC):
                    r.pe(lambda e, g=g, ec=ec, pp=pp, hs=hs: e.matmul(
                        pp[:, 0:512],
                        pool_b[:, hs + ec * 128: hs + (ec + 1) * 128],
                        pool_a[:, ec * 2048 + g * 512: ec * 2048 + (g + 1) * 512],
                        start=(ec == 0), stop=(ec == NEC - 1)),
                        inc=(pe_s if ec == NEC - 1 else None))
                r.wait('dve', pe_s, pe_s.n)
                bank_mark[g % 2] = elu_evac(pp, SQ[:, g * 512:(g + 1) * 512], 512)
            r.wait('pe', GWK, wk_marks[h])
            for g in range(4):
                pp = pA0 if g % 2 == 0 else pA1
                r.wait('pe', dv_s, bank_mark[g % 2])
                for ec in range(NEC):
                    r.pe(lambda e, g=g, ec=ec, pp=pp: e.matmul(
                        pp[:, 0:512],
                        WKH[:, ec * 128:(ec + 1) * 128],
                        pool_a[:, ec * 2048 + g * 512: ec * 2048 + (g + 1) * 512],
                        start=(ec == 0), stop=(ec == NEC - 1)),
                        inc=(pe_s if ec == NEC - 1 else None))
                r.wait('dve', pe_s, pe_s.n)
                bank_mark[g % 2] = elu_evac(pp, KPF[:, g * 512:(g + 1) * 512], 512)
            tr_mark = {0: 0, 1: 0}
            r.wait('pe', dv_s, bank_mark[0])
            r.wait('pe', dv_s, bank_mark[1])
            for c in range(NCHUNK):
                pt = pTa if c % 2 == 0 else pTb
                if tr_mark[c % 2]:
                    r.wait('pe', dv_s, tr_mark[c % 2])
                r.pe(lambda e, c=c, pt=pt: e.transpose(
                    pt[:, 0:128], KPF[:, c * 128:(c + 1) * 128], idb[:]), inc=pe_s)
                r.wait('dve', pe_s, pe_s.n)
                r.dve(lambda e, c=c, pt=pt: e.tensor_copy(
                    KPT[:, c * 128:(c + 1) * 128], pt[:, 0:128]), inc=dv_s)
                tr_mark[c % 2] = dv_s.n
            r.wait('pe', GWV, wv_marks[h])
            for c in range(NCHUNK):
                pp = pA0 if c % 2 == 0 else pA1
                r.wait('pe', dv_s, bank_mark[c % 2])
                for ec in range(NEC):
                    r.pe(lambda e, c=c, ec=ec, pp=pp: e.matmul(
                        pp[:, 0:128],
                        pool_a[:, ec * 2048 + c * 128: ec * 2048 + (c + 1) * 128],
                        WVH[:, ec * 128:(ec + 1) * 128],
                        start=(ec == 0), stop=(ec == NEC - 1)),
                        inc=(pe_s if ec == NEC - 1 else None))
                r.wait('dve', pe_s, pe_s.n)
                r.dve(lambda e, c=c, pp=pp: e.tensor_copy(
                    VAUG[:, c * 129:c * 129 + 128], pp[:, 0:128]), inc=dv_s)
                bank_mark[c % 2] = dv_s.n
            vaug_done = dv_s.n
            if h == H - 1:
                last_xln_use_pe = pe_s.n

            r.wait('pe', dv_s, vaug_done)
            for c in range(NCHUNK):
                r.pe(lambda e, c=c: e.matmul(
                    pA0[:, 256:385], KPT[:, c * 128:(c + 1) * 128],
                    VAUG[:, c * 129:(c + 1) * 129],
                    start=(c == 0), stop=(c == NCHUNK - 1)),
                    inc=(pe_s if c == NCHUNK - 1 else None))
            kvt_mm = pe_s.n
            r.wait('dve', pe_s, kvt_mm)
            r.dve(lambda e: e.tensor_copy(totf, pA0[:, 256:385]), inc=dv_s)
            tot_ev = dv_s.n
            r.wait('pool', dv_s, tot_ev)
            r.pool(lambda e, h=h: e.dma_start(out=sta_in[h], in_=totf), inc=GST, k=16)
            tot_dma = GST.n
            r.wait('pool', GST, tot_dma)
            r.pool(lambda e, h=h: e.collective_compute(
                "AllGather", ALU.bypass, replica_groups=PAIRS,
                ins=[sta_in[h].opt()], outs=[sta_out[h].opt()]), inc=cc_s, k=1)
            r.wait('pool', cc_s, cc_s.n)
            r.pool(lambda e, h=h: e.dma_start(out=recvf, in_=sta_out[h][0:128, :]),
                   inc=GRC, k=16)
            recv_done = GRC.n
            r.wait('dve', GST, tot_dma)
            r.dve(lambda e: e.memset(totf, 0.0), inc=dv_s)
            r.dve(lambda e: e.memset(state_b, 0.0), inc=dv_s)
            r.wait('dve', GRC, recv_done)
            r.dve(lambda e: e.tensor_scalar(partner_b, recvf, flagv[:, 0:1],
                                            None, ALU.mult), inc=dv_s)
            r.wait('dve', MS, wb_rep)
            if wb4_done:
                r.wait('dve', ac_s, ac_s.n)
            r.dve(lambda e, h=h: e.tensor_copy(Wb4[:], Wb[:, h * 4:(h + 1) * 4]), inc=dv_s)
            wb4_done = dv_s.n
            pre_loop_dve = dv_s.n

            if h + 1 < H:
                prefetch_head(h + 1, wait_pe=kvt_mm)

            bankA = [pA0, pA1]
            bankB = [pB0, pB1]
            bankC = [pC0, pC1]
            mk_scores = [0, 0]
            mk_B = [0, 0]
            mk_memA = [0, 0]
            mk_memD = [0, 0]
            mk_tr = [tr_mark[0], tr_mark[1]]
            mk_d5b = 0
            mk_d4 = 0
            mk_Ba = 0
            mk_Ba_hist = [0, 0]
            s9_prev = 0
            s9_pending = None
            r.wait('pe', dv_s, pre_loop_dve)
            r.wait('pe', ac_s, ac_s.n)
            r.wait('pe', GME, mem_marks[h])
            for c in range(NCHUNK):
                par = c % 2
                pA_, pB_, pC_ = bankA[par], bankB[par], bankC[par]
                cs = c * 128
                if mk_scores[par]:
                    r.wait('pe', dv_s, mk_scores[par])
                r.pe(lambda e, pA_=pA_, cs=cs: e.matmul(
                    pA_[:, 0:128], KPF[:, cs:cs + 128], SQ[:, cs:cs + 128],
                    start=True, stop=True), inc=pe_s)
                s1_mm = pe_s.n
                r.pe(lambda e, pA_=pA_, c=c: e.matmul(
                    pA_[:, 256:385], KPT[:, c * 128:(c + 1) * 128],
                    VAUG[:, c * 129:(c + 1) * 129],
                    start=True, stop=True), inc=pe_s)
                s10_mm = pe_s.n
                if mk_memA[par] or mk_memD[par]:
                    r.wait('pe', ac_s, mk_memA[par])
                    r.wait('pe', dv_s, mk_memD[par])
                for m in range(M):
                    r.pe(lambda e, pC_=pC_, cs=cs, m=m, h=h: e.matmul(
                        pC_[:, m * 128:(m + 1) * 128], SQ[:, cs:cs + 128],
                        MEMA[h % 2][:, m * 128:(m + 1) * 128],
                        start=True, stop=True), inc=(pe_s if m == M - 1 else None))
                s6_mm = pe_s.n
                if mk_B[par]:
                    r.wait('pe', dv_s, mk_B[par])
                if mk_Ba_hist[par]:
                    r.wait('pe', ac_s, mk_Ba_hist[par])
                r.pe(lambda e, pB_=pB_, cs=cs, h=h: e.matmul(
                    pB_[:, 256:260], SQ[:, cs:cs + 128], mnT_b[:, h * 4:(h + 1) * 4],
                    start=True, stop=True), inc=pe_s)
                r.wait('pe', ac_s, mk_d5b)
                r.pe(lambda e, pB_=pB_, cs=cs: e.matmul(
                    pB_[:, 0:129], SQ[:, cs:cs + 128], state_b,
                    start=True, stop=False), inc=pe_s)
                s3_mm = pe_s.n
                r.pe(lambda e, pB_=pB_, cs=cs: e.matmul(
                    pB_[:, 0:129], SQ[:, cs:cs + 128], partner_b,
                    start=False, stop=False), inc=pe_s)
                if s9_pending is not None:
                    ppar, pcs = s9_pending
                    pt_prev = pTa if ppar == 0 else pTb
                    r.wait('pe', dv_s, mk_d4)
                    if mk_tr[ppar]:
                        r.wait('pe', dv_s, mk_tr[ppar])
                    r.pe(lambda e, pt=pt_prev: e.transpose(pt[:, 0:128], outc_b[:],
                                                           idb[:]), inc=pe_s)
                    s9_prev = pe_s.n
                    r.wait('dve', pe_s, s9_prev)
                    r.dve(lambda e, pt=pt_prev, hs=hs, pcs=pcs: e.tensor_copy(
                        pool_b[:, hs + pcs: hs + pcs + 128], pt[:, 0:128]), inc=dv_s)
                    mk_tr[ppar] = dv_s.n
                r.wait('dve', pe_s, s10_mm)
                r.dve(lambda e, pA_=pA_: e.tensor_tensor(scb16[:], pA_[:, 0:128],
                                                         maskT[:], ALU.mult), inc=dv_s)
                d1 = dv_s.n
                r.wait('pe', dv_s, d1)
                r.pe(lambda e, pB_=pB_, c=c: e.matmul(
                    pB_[:, 0:129], scb16[:], VAUG[:, c * 129:(c + 1) * 129],
                    start=False, stop=True), inc=pe_s)
                s4_mm = pe_s.n
                r.wait('dve', pe_s, s10_mm)
                r.dve(lambda e, pA_=pA_: e.tensor_tensor(totf, totf,
                                                         pA_[:, 256:385], ALU.add),
                      inc=dv_s)
                kv_read = dv_s.n
                mk_scores[par] = max(d1, kv_read)
                r.wait('act', pe_s, s3_mm)
                r.wait('act', dv_s, dv_s.n)
                r.act(lambda e: e.activation(state_b, totf, AF.Identity), inc=ac_s)
                mk_d5b = ac_s.n
                r.wait('dve', pe_s, s6_mm)
                r.dve(lambda e, pB_=pB_: e.tensor_scalar(vec[:, 34:38], pB_[:, 256:260],
                                                         EPS_A, None, ALU.max), inc=dv_s)
                r.dve(lambda e: e.reciprocal(vec[:, 38:42], vec[:, 34:38]), inc=dv_s)
                r.dve(lambda e: e.tensor_tensor(vec[:, 34:38], vec[:, 38:42], Wb4[:],
                                                ALU.mult), inc=dv_s)
                f_done = dv_s.n
                r.wait('act', dv_s, max(f_done, mk_d4))
                r.act(lambda e, pC_=pC_: e.activation(mem_tm[:], pC_[:, 0:128],
                                                      AF.Identity, bias=0.0,
                                                      scale=vec[:, 34:35]), inc=ac_s)
                mk_memA[par] = ac_s.n
                r.wait('dve', ac_s, mk_memA[par])
                for m in range(1, M):
                    r.dve(lambda e, pC_=pC_, m=m: e.scalar_tensor_tensor(
                        mem_tm[:], pC_[:, m * 128:(m + 1) * 128], vec[:, 34 + m:35 + m],
                        mem_tm[:], ALU.mult, ALU.add), inc=dv_s)
                mk_memD[par] = dv_s.n
                r.wait('dve', pe_s, s4_mm)
                r.dve(lambda e, pB_=pB_: e.tensor_scalar(T_DEN, pB_[:, 128:129],
                                                         EPS_A, None, ALU.max), inc=dv_s)
                r.dve(lambda e: e.reciprocal(T_R, T_DEN), inc=dv_s)
                den_mark = dv_s.n
                r.wait('act', dv_s, den_mark)
                if mk_d4:
                    r.wait('act', dv_s, mk_d4)
                r.act(lambda e, pB_=pB_: e.activation(loc_tm[:], pB_[:, 0:128],
                                                      AF.Identity, bias=0.0,
                                                      scale=T_R), inc=ac_s)
                mk_B[par] = den_mark
                mk_Ba = ac_s.n
                mk_Ba_hist[par] = mk_Ba
                r.wait('dve', ac_s, mk_Ba)
                if s9_prev:
                    r.wait('dve', pe_s, s9_prev)
                r.dve(lambda e, h=h: e.tensor_scalar(t2_tm, mem_tm[:],
                                                     gWb[:, h:h + 1], None, ALU.mult),
                      inc=dv_s)
                r.dve(lambda e, h=h: e.scalar_tensor_tensor(
                    outc_b[:], loc_tm[:], gLb[:, h:h + 1], t2_tm,
                    ALU.mult, ALU.add), inc=dv_s)
                mk_d4 = dv_s.n
                s9_pending = (par, cs)
            ppar, pcs = s9_pending
            pt_prev = pTa if ppar == 0 else pTb
            r.wait('pe', dv_s, mk_d4)
            if mk_tr[ppar]:
                r.wait('pe', dv_s, mk_tr[ppar])
            r.pe(lambda e, pt=pt_prev: e.transpose(pt[:, 0:128], outc_b[:], idb[:]),
                 inc=pe_s)
            s9_prev = pe_s.n
            r.wait('dve', pe_s, s9_prev)
            r.dve(lambda e, pt=pt_prev, hs=hs, pcs=pcs: e.tensor_copy(
                pool_b[:, hs + pcs: hs + pcs + 128], pt[:, 0:128]), inc=dv_s)
            mk_tr[ppar] = dv_s.n
        pc_dve = dv_s.n
        pc_act = ac_s.n

        wo_all = 0
        r.wait('pool', pe_s, last_xln_use_pe)
        for hh in range(H):
            r.pool(lambda e, hh=hh: e.dma_start(
                out=pool_a[:, hh * 2048:(hh + 1) * 2048], in_=wo_r[hh]), inc=GWO, k=16)
        wo_all = GWO.n

        wf_marks = []
        for dc in range(NEC):
            half = xfm2[:, (dc % 2) * 2048:((dc % 2) + 1) * 2048]
            if dc >= 2:
                r.wait('sp', dv_s, wf_marks[dc - 2])
            else:
                r.wait('sp', dv_s, pb_dve)
            wp_ = WP[dc % 2]
            r.sp(lambda e, dc=dc, b=half: e.dma_start(out=b, in_=wf_r[dc]),
                 inc=wp_, k=16)
            ld = wp_.n
            r.wait('dve', wp_, ld)
            r.dve(lambda e, b=half: e.tensor_reduce(AW_T, b, AX.X, ALU.max,
                                                    apply_absolute_value=True), inc=dv_s)
            r.dve(lambda e: e.tensor_tensor(AW_ACC, AW_ACC, AW_T, ALU.max), inc=dv_s)
            wf_marks.append(dv_s.n)
        amaxw_done = dv_s.n

        r.wait('pe', dv_s, pc_dve)
        r.wait('pe', ac_s, pc_act)
        r.wait('pe', GWO, wo_all)
        x1_sp_marks = []
        x1_dve_marks = []
        pd_evac_mark = 0
        pd_bankfree = [0, 0]
        banks4 = [pA0, pA1, pB0, pB1]
        for t in range(NCHUNK):
            cs = t * 128
            xb = xst[:, 0:2048] if t % 2 == 0 else wpf
            if t >= 2:
                r.wait('sp', dv_s, x1_dve_marks[t - 2])
            xp = XP[t % 2]
            r.sp(lambda e, t=t, b=xb: e.dma_start(
                out=b[:, 0:2048], in_=x_tm[t * 128:(t + 1) * 128, :]), inc=xp, k=16)
            x_ld = xp.n
            for ng in range(4):
                bk = banks4[ng % 2]
                if pd_bankfree[ng % 2]:
                    r.wait('pe', ac_s, pd_bankfree[ng % 2])
                for hh in range(H):
                    r.pe(lambda e, hh=hh, ng=ng, cs=cs, bk=bk: e.matmul(
                        bk[:, 0:512],
                        pool_b[:, hh * 2048 + cs: hh * 2048 + cs + 128],
                        pool_a[:, hh * 2048 + ng * 512: hh * 2048 + (ng + 1) * 512],
                        start=(hh == 0), stop=(hh == H - 1)),
                        inc=(pe_s if hh == H - 1 else None))
                g_mm = pe_s.n
                r.wait('act', pe_s, g_mm)
                if ng == 0 and t >= 1:
                    r.wait('act', dv_s, x1_dve_marks[-1])
                r.act(lambda e, ng=ng, bk=bk: e.activation(
                    s1[:, ng * 512:(ng + 1) * 512], bk[:, 0:512], AF.Identity,
                    accum_out=vec[:, 45 + ng:46 + ng]), inc=ac_s)
                pd_bankfree[ng % 2] = ac_s.n
            pd_evac_mark = ac_s.n
            r.wait('dve', ac_s, pd_evac_mark)
            r.dve(lambda e: e.tensor_tensor(vec[:, 45:46], vec[:, 45:46],
                                            vec[:, 46:47], ALU.add), inc=dv_s)
            r.dve(lambda e: e.tensor_tensor(vec[:, 47:48], vec[:, 47:48],
                                            vec[:, 48:49], ALU.add), inc=dv_s)
            r.dve(lambda e: e.tensor_tensor(vec[:, 45:46], vec[:, 45:46],
                                            vec[:, 47:48], ALU.add), inc=dv_s)
            r.dve(lambda e: e.tensor_scalar(T_NEGM, vec[:, 45:46], -1.0 / 2048.0, None,
                                            ALU.mult), inc=dv_s)
            nm2 = dv_s.n
            r.wait('act', dv_s, nm2)
            r.act(lambda e: e.activation(s2[:, 0:2048], s1[:, 0:2048], AF.Square,
                                         accum_out=T_SSQ), inc=ac_s)
            ssq2 = ac_s.n
            r.wait('dve', ac_s, ssq2)
            r.dve(lambda e: e.tensor_scalar(T_MSQ, T_NEGM, T_NEGM, None, ALU.mult), inc=dv_s)
            r.dve(lambda e: e.tensor_scalar(T_T0, T_SSQ, 1.0 / 2048.0, None, ALU.mult), inc=dv_s)
            r.dve(lambda e: e.tensor_tensor(T_T0, T_T0, T_MSQ, ALU.subtract), inc=dv_s)
            v2m = dv_s.n
            r.wait('act', dv_s, v2m)
            r.act(lambda e: e.activation(T_T0, T_T0, AF.Ln, bias=C_EPSLN, scale=1.0),
                  inc=ac_s)
            r.act(lambda e: e.activation(T_RSTD, T_T0, AF.Exp, bias=0.0,
                                         scale=C_NEGHALF), inc=ac_s)
            r2m = ac_s.n
            r.wait('dve', ac_s, r2m)
            r.dve(lambda e: e.tensor_scalar(s2[:, 0:2048], s1[:, 0:2048], T_NEGM,
                                            T_RSTD, ALU.add, ALU.mult), inc=dv_s)
            r.wait('dve', XP[t % 2], x_ld)
            if t >= 1:
                r.wait('dve', XSTP[(t - 1) % 2], x1_sp_marks[-1])
            r.dve(lambda e, b=xb: e.scalar_tensor_tensor(
                s3[:, 0:2048], s2[:, 0:2048], 1.0, b[:, 0:2048], ALU.mult, ALU.add,
                accum_out=vec[:, 49:50]), inc=dv_s)
            x1_mark = dv_s.n
            r.wait('sp', dv_s, x1_mark)
            xstp = XSTP[t % 2]
            r.sp(lambda e, t=t: e.dma_start(out=x1_d[t * 128:(t + 1) * 128, :],
                                            in_=s3[:, 0:2048]), inc=xstp, k=16)
            x1_sp_marks.append(xstp.n)
            r.dve(lambda e: e.tensor_scalar(T_NEGM, vec[:, 49:50], -1.0 / 2048.0, None,
                                            ALU.mult), inc=dv_s)
            r.dve(lambda e: e.tensor_scalar(s2[:, 0:2048], s3[:, 0:2048], T_NEGM, None,
                                            ALU.add), inc=dv_s)
            d3m = dv_s.n
            r.dve(lambda e: e.tensor_reduce(T_DEV, s2[:, 0:2048], AX.X, ALU.max,
                                            apply_absolute_value=True), inc=dv_s)
            r.wait('act', dv_s, d3m)
            r.act(lambda e: e.activation(s1[:, 0:2048], s2[:, 0:2048], AF.Square,
                                         accum_out=T_SSQ), inc=ac_s)
            sq3 = ac_s.n
            r.wait('dve', ac_s, sq3)
            r.dve(lambda e: e.tensor_scalar(T_T0, T_SSQ, 1.0 / 2048.0, None, ALU.mult),
                  inc=dv_s)
            v3m = dv_s.n
            r.wait('act', dv_s, v3m)
            r.act(lambda e: e.activation(T_T0, T_T0, AF.Ln, bias=C_EPSLN, scale=1.0),
                  inc=ac_s)
            r.act(lambda e: e.activation(T_RSTD, T_T0, AF.Exp, bias=0.0,
                                         scale=C_NEGHALF), inc=ac_s)
            r3m = ac_s.n
            r.wait('dve', ac_s, r3m)
            r.dve(lambda e: e.tensor_tensor(T_DEV, T_DEV, T_RSTD, ALU.mult), inc=dv_s)
            r.dve(lambda e: e.tensor_tensor(AMX_ACC, AMX_ACC, T_DEV, ALU.max), inc=dv_s)
            x1_dve_marks.append(dv_s.n)
        pd_dve = dv_s.n
        pd_pe = pe_s.n

        r.wait('pe', dv_s, pd_dve)
        r.pe(lambda e: e.transpose(pC0[0:1, 0:128], AMX_ACC, idf[:]), inc=pe_s)
        amt = pe_s.n
        r.wait('dve', pe_s, amt)
        r.wait('dve', XSTP[0], XSTP[0].n)
        r.wait('dve', XSTP[1], XSTP[1].n)
        r.dve(lambda e: e.tensor_copy(s3[0:1, 0:128], pC0[0:1, 0:128]), inc=dv_s)
        r.dve(lambda e: e.tensor_reduce(SC_AX[0:1, :], s3[0:1, 0:128], AX.X, ALU.max),
              inc=dv_s)
        amx_mark = dv_s.n
        r.wait('pool', dv_s, amx_mark)
        r.pool(lambda e: e.dma_start(out=am_in, in_=SC_AX[0:1, :]), inc=GST, k=16)
        r.wait('pool', GST, GST.n)
        r.pool(lambda e: e.collective_compute(
            "AllReduce", ALU.max, replica_groups=ALL8,
            ins=[am_in.opt()], outs=[am_out.opt()]), inc=cc_s, k=1)
        r.wait('pool', cc_s, cc_s.n)
        r.pool(lambda e: e.dma_start(out=SC_AX[0:1, :], in_=am_out), inc=GRC, k=16)
        amax_back = GRC.n

        r.wait('pe', dv_s, max(amx_mark, amaxw_done))
        r.pe(lambda e: e.transpose(pC0[0:1, 0:128], AW_ACC, idf[:]), inc=pe_s)
        awt = pe_s.n
        r.wait('dve', pe_s, awt)
        r.dve(lambda e: e.tensor_copy(s3[0:1, 0:128], pC0[0:1, 0:128]), inc=dv_s)
        r.dve(lambda e: e.tensor_reduce(SC_AW[0:1, :], s3[0:1, 0:128], AX.X, ALU.max),
              inc=dv_s)
        r.wait('dve', GRC, amax_back)
        r.dve(lambda e: e.tensor_scalar(SC_SX[0:1, :], SC_AX[0:1, :], RFP8_C, None,
                                        ALU.mult), inc=dv_s)
        r.dve(lambda e: e.tensor_tensor(SC_SW[0:1, :], SC_AX[0:1, :], SC_AW[0:1, :],
                                        ALU.max), inc=dv_s)
        r.dve(lambda e: e.tensor_scalar(SC_SW[0:1, :], SC_SW[0:1, :], RFP8_C, None,
                                        ALU.mult), inc=dv_s)
        r.dve(lambda e: e.tensor_tensor(SC_A[0:1, :], SC_SX[0:1, :], SC_SW[0:1, :],
                                        ALU.mult), inc=dv_s)
        r.dve(lambda e: e.reciprocal(scrow[0:1, 3:4], SC_A[0:1, :]), inc=dv_s)
        r.dve(lambda e: e.reciprocal(scrow[0:1, 0:1], SC_SX[0:1, :]), inc=dv_s)
        r.dve(lambda e: e.tensor_scalar(scrow[0:1, 0:1], scrow[0:1, 0:1], 0.5, None,
                                        ALU.mult), inc=dv_s)
        r.dve(lambda e: e.reciprocal(scrow[0:1, 1:2], SC_SW[0:1, :]), inc=dv_s)
        r.dve(lambda e: e.tensor_scalar(scrow[0:1, 1:2], scrow[0:1, 1:2], 0.5, None,
                                        ALU.mult), inc=dv_s)
        r.dve(lambda e: e.tensor_copy(scrow[0:1, 2:3], SC_A[0:1, :]), inc=dv_s)
        sc_mark = dv_s.n
        r.wait('sp', dv_s, sc_mark)
        r.wait('sp', MS, MS.n)
        r.sp(lambda e: e.dma_start(out=scb_d[0:1, 0:4], in_=scrow[0:1, 0:4]), inc=MS, k=16)
        r.wait('sp', MS, MS.n)
        r.wait('sp', MS, MS.n)
        r.sp(lambda e: e.dma_start(out=vec[:, 22:26],
                                   in_=scb_d[0:1, 0:4].partition_broadcast(128)),
             inc=MS, k=16)
        sc_rep = MS.n

        def quant_tail(dst_slice):
            r.dve(lambda e: e.tensor_copy(fp8t, s2[:, 0:2048]), inc=dv_s)
            f8 = dv_s.n
            r.wait('act', dv_s, f8)
            r.act(lambda e: e.activation(s1[:, 0:2048], fp8t, AF.Identity,
                                         bias=0.0, scale=C_TWO), inc=ac_s)
            big = ac_s.n
            r.dve(lambda e: e.tensor_tensor(s3[:, 0:2048], s2[:, 0:2048],
                                            s2[:, 0:2048], ALU.mult), inc=dv_s)
            r.dve(lambda e: e.tensor_scalar(msku, s3[:, 0:2048], float(2.0 ** -12),
                                            None, ALU.is_lt), inc=dv_s)
            r.dve(lambda e: e.tensor_scalar(s3[:, 0:2048], s2[:, 0:2048], 1024.0,
                                            MAGIC, ALU.mult, ALU.add), inc=dv_s)
            r.dve(lambda e: e.tensor_scalar(s3[:, 0:2048], s3[:, 0:2048], MAGIC,
                                            1.0 / 512.0, ALU.subtract, ALU.mult), inc=dv_s)
            r.wait('dve', ac_s, big)
            r.dve(lambda e, d=dst_slice: e.select(d, msku, s3[:, 0:2048],
                                                  s1[:, 0:2048], add_drain=True), inc=dv_s)
            return ac_s.n, dv_s.n

        r.wait('dve', XSTP[0], XSTP[0].n)
        r.wait('dve', XSTP[1], XSTP[1].n)
        r.wait('dve', pe_s, pd_pe)
        qw_marks = []
        qw_act = 0
        for dc in range(NEC):
            half = xfm2[:, (dc % 2) * 2048:((dc % 2) + 1) * 2048]
            if dc >= 2:
                r.wait('sp', dv_s, qw_marks[dc - 2])
            else:
                r.wait('sp', dv_s, amaxw_done)
            wp_ = WP[dc % 2]
            r.sp(lambda e, dc=dc, b=half: e.dma_start(out=b, in_=wf_r[dc]),
                 inc=wp_, k=16)
            ld = wp_.n
            r.wait('dve', wp_, ld)
            r.wait('dve', MS, sc_rep)
            r.dve(lambda e, b=half: e.tensor_scalar(s2[:, 0:2048], b, HRSW, None,
                                                    ALU.mult), inc=dv_s)
            qw_act, mrk = quant_tail(pool_b[:, dc * 2048:(dc + 1) * 2048])
            qw_marks.append(mrk)
        qw_all = dv_s.n

        r.wait('sp', dv_s, qw_all)
        r.wait('sp', ac_s, qw_act)
        r.wait('sp', MS, MS.n)
        r.sp(lambda e: e.dma_start(out=s1[0:1, 0:2048], in_=bias_r), inc=MS, k=16)
        b_ld = MS.n
        r.wait('dve', MS, b_ld)
        r.dve(lambda e: e.tensor_scalar(s2[0:1, 0:2048], s1[0:1, 0:2048],
                                        RAVEC[0:1, :], None, ALU.mult), inc=dv_s)
        r.dve(lambda e: e.tensor_copy(BIAS_HI, s2[0:1, 0:2048]), inc=dv_s)
        r.dve(lambda e: e.tensor_copy(s3[0:1, 0:2048], BIAS_HI), inc=dv_s)
        r.dve(lambda e: e.tensor_tensor(s2[0:1, 0:2048], s2[0:1, 0:2048],
                                        s3[0:1, 0:2048], ALU.subtract), inc=dv_s)
        r.dve(lambda e: e.tensor_copy(BIAS_LO, s2[0:1, 0:2048]), inc=dv_s)
        bias_mark = dv_s.n

        e3_sp_marks = []
        eq_marks = []
        r.wait('dve', MS, sc_rep)
        for t in range(NCHUNK):
            cs = t * 128
            xb = xst[:, 0:2048] if t % 2 == 0 else wpf
            xp = XP[t % 2]
            if t >= 2:
                r.wait('sp', dv_s, eq_marks[t - 2])
            else:
                r.wait('sp', dv_s, pd_dve)
                r.wait('sp', XSTP[0], XSTP[0].n)
                r.wait('sp', XSTP[1], XSTP[1].n)
            r.sp(lambda e, t=t, b=xb: e.dma_start(
                out=b[:, 0:2048], in_=x1_d[t * 128:(t + 1) * 128, :]), inc=xp, k=16)
            ld = xp.n
            r.wait('dve', xp, ld)
            r.dve(lambda e, b=xb: e.tensor_reduce(T_SUM, b[:, 0:2048], AX.X, ALU.add),
                  inc=dv_s)
            r.dve(lambda e: e.tensor_scalar(T_NEGM, T_SUM, -1.0 / 2048.0, None,
                                            ALU.mult), inc=dv_s)
            nm3 = dv_s.n
            r.wait('act', xp, ld)
            r.wait('act', dv_s, nm3)
            r.act(lambda e, b=xb: e.activation(s1[:, 0:2048], b[:, 0:2048], AF.Square,
                                               accum_out=T_SSQ), inc=ac_s)
            sq3b = ac_s.n
            r.wait('dve', ac_s, sq3b)
            r.dve(lambda e: e.tensor_scalar(T_MSQ, T_NEGM, T_NEGM, None, ALU.mult), inc=dv_s)
            r.dve(lambda e: e.tensor_scalar(T_T0, T_SSQ, 1.0 / 2048.0, None, ALU.mult), inc=dv_s)
            r.dve(lambda e: e.tensor_tensor(T_T0, T_T0, T_MSQ, ALU.subtract), inc=dv_s)
            v3b = dv_s.n
            r.wait('act', dv_s, v3b)
            r.act(lambda e: e.activation(T_T0, T_T0, AF.Ln, bias=C_EPSLN, scale=1.0),
                  inc=ac_s)
            r.act(lambda e: e.activation(T_RSTD, T_T0, AF.Exp, bias=0.0,
                                         scale=C_NEGHALF), inc=ac_s)
            r3b = ac_s.n
            r.wait('dve', ac_s, r3b)
            r.dve(lambda e: e.tensor_tensor(T_T0, T_RSTD, HRSX, ALU.mult), inc=dv_s)
            r.dve(lambda e, b=xb: e.tensor_scalar(s2[:, 0:2048], b[:, 0:2048], T_NEGM,
                                                  T_T0, ALU.add, ALU.mult), inc=dv_s)
            _, qx_mark = quant_tail(SQ)
            r.wait('pe', dv_s, qx_mark)
            trm = {0: 0, 1: 0}
            for ec in range(NEC):
                pt = pTa if ec % 2 == 0 else pTb
                if trm[ec % 2]:
                    r.wait('pe', dv_s, trm[ec % 2])
                r.pe(lambda e, ec=ec, pt=pt: e.transpose(
                    pt[:, 0:128], SQ[:, ec * 128:(ec + 1) * 128], idb[:]), inc=pe_s)
                r.wait('dve', pe_s, pe_s.n)
                r.dve(lambda e, ec=ec, pt=pt, cs=cs: e.tensor_copy(
                    pool_a[:, ec * 2048 + cs: ec * 2048 + cs + 128], pt[:, 0:128]),
                    inc=dv_s)
                trm[ec % 2] = dv_s.n
            eq_marks.append(dv_s.n)
        eq_all = dv_s.n

        em_bankfree = [0, 0]
        em_out_marks = []
        r.wait('pe', dv_s, max(eq_all, qw_all, bias_mark))
        for t in range(NCHUNK):
            cs = t * 128
            xb = xst[:, 0:2048] if t % 2 == 0 else wpf
            xp = XP[t % 2]
            if t >= 2:
                r.wait('sp', dv_s, em_out_marks[t - 2])
            else:
                r.wait('sp', dv_s, eq_all)
            r.sp(lambda e, t=t, b=xb: e.dma_start(
                out=b[:, 0:2048], in_=x1_d[t * 128:(t + 1) * 128, :]), inc=xp, k=16)
            x_ld = xp.n
            for ng in range(4):
                bk = banks4[ng % 2]
                if em_bankfree[ng % 2]:
                    r.wait('pe', ac_s, em_bankfree[ng % 2])
                for ec in range(NEC):
                    r.pe(lambda e, ec=ec, ng=ng, cs=cs, bk=bk: e.matmul(
                        bk[:, 0:512],
                        pool_a[:, ec * 2048 + cs: ec * 2048 + cs + 128],
                        pool_b[:, ec * 2048 + ng * 512: ec * 2048 + (ng + 1) * 512],
                        start=(ec == 0), stop=False), inc=None)
                r.pe(lambda e, ng=ng, bk=bk: e.matmul(
                    bk[:, 0:512], onesb[0:1, 0:128],
                    BIAS_HI[0:1, ng * 512:(ng + 1) * 512],
                    start=False, stop=False), inc=None)
                r.pe(lambda e, ng=ng, bk=bk: e.matmul(
                    bk[:, 0:512], onesb[0:1, 0:128],
                    BIAS_LO[0:1, ng * 512:(ng + 1) * 512],
                    start=False, stop=True), inc=pe_s)
                g_mm = pe_s.n
                r.wait('act', pe_s, g_mm)
                if ng == 0 and t >= 1:
                    r.wait('act', dv_s, em_out_marks[-1])
                r.act(lambda e, ng=ng, bk=bk: e.activation(
                    s1[:, ng * 512:(ng + 1) * 512], bk[:, 0:512], AF.Identity,
                    bias=0.0, scale=AVEC, accum_out=vec[:, 45 + ng:46 + ng]), inc=ac_s)
                em_bankfree[ng % 2] = ac_s.n
            e3_evac = ac_s.n
            r.wait('dve', ac_s, e3_evac)
            r.dve(lambda e: e.tensor_tensor(vec[:, 45:46], vec[:, 45:46],
                                            vec[:, 46:47], ALU.add), inc=dv_s)
            r.dve(lambda e: e.tensor_tensor(vec[:, 47:48], vec[:, 47:48],
                                            vec[:, 48:49], ALU.add), inc=dv_s)
            r.dve(lambda e: e.tensor_tensor(vec[:, 45:46], vec[:, 45:46],
                                            vec[:, 47:48], ALU.add), inc=dv_s)
            r.dve(lambda e: e.tensor_scalar(T_M4, vec[:, 45:46], -1.0 / 2048.0, None,
                                            ALU.mult), inc=dv_s)
            m4m = dv_s.n
            r.wait('act', dv_s, m4m)
            r.act(lambda e: e.activation(s2[:, 0:2048], s1[:, 0:2048], AF.Square,
                                         accum_out=T_SSQ), inc=ac_s)
            sq4 = ac_s.n
            r.wait('dve', ac_s, sq4)
            r.dve(lambda e: e.tensor_scalar(T_MSQ, T_M4, T_M4, None, ALU.mult), inc=dv_s)
            r.dve(lambda e: e.tensor_scalar(T_V4, T_SSQ, 1.0 / 2048.0, None, ALU.mult), inc=dv_s)
            r.dve(lambda e: e.tensor_tensor(T_V4, T_V4, T_MSQ, ALU.subtract), inc=dv_s)
            v4m = dv_s.n
            r.wait('act', dv_s, v4m)
            r.act(lambda e: e.activation(T_V4, T_V4, AF.Ln, bias=C_EPSLN, scale=1.0),
                  inc=ac_s)
            r.act(lambda e: e.activation(T_RSTD, T_V4, AF.Exp, bias=0.0,
                                         scale=C_NEGHALF), inc=ac_s)
            r4m = ac_s.n
            r.wait('dve', ac_s, r4m)
            r.wait('dve', xp, x_ld)
            if t >= 1:
                r.wait('dve', OUTP[(t - 1) % 2], e3_sp_marks[-1])
            r.dve(lambda e: e.tensor_scalar(s2[:, 0:2048], s1[:, 0:2048], T_M4,
                                            T_RSTD, ALU.add, ALU.mult), inc=dv_s)
            r.dve(lambda e, b=xb: e.tensor_tensor(s3[:, 0:2048], s2[:, 0:2048],
                                                  b[:, 0:2048], ALU.add), inc=dv_s)
            out_mark = dv_s.n
            em_out_marks.append(out_mark)
            r.wait('sp', dv_s, out_mark)
            outp = OUTP[t % 2]
            r.sp(lambda e, t=t: e.dma_start(out=out_d[t * 128:(t + 1) * 128, :],
                                            in_=s3[:, 0:2048]), inc=outp, k=16)
            e3_sp_marks.append(outp.n)

        r.wait('sp', OUTP[0], OUTP[0].n)
        r.wait('sp', OUTP[1], OUTP[1].n)
        r.wait('pool', OUTP[0], OUTP[0].n)
        r.wait('pool', OUTP[1], OUTP[1].n)

        @block.tensor
        def _(t):
            r.replay('pe', t)

        @block.vector
        def _(v):
            r.replay('dve', v)

        @block.scalar
        def _(s):
            r.replay('act', s)

        @block.gpsimd
        def _(g):
            r.replay('pool', g)

        @block.sync
        def _(s):
            r.replay('sp', s)

    return nc


def _stage_weight_qkv(w):
    t = w.reshape(H, 128, NEC, 128)
    return np.ascontiguousarray(t.transpose(0, 3, 2, 1)).reshape(H, 128, D).astype(np.float32)


def _stage_weight_out(w):
    return np.ascontiguousarray(w.T.reshape(NEC, 128, D)).astype(np.float32)


def kernel(L0_attn_hidden_states, L0_attn_memories, L0_attn_memory_norms,
           L0_attn_w_q, L0_attn_w_k, L0_attn_w_v, L0_attn_w_o, L0_attn_gate,
           L0_ffn_weight, L0_ffn_bias):
    x = np.ascontiguousarray(np.asarray(L0_attn_hidden_states, dtype=np.float32))
    mem = np.asarray(L0_attn_memories, dtype=np.float32)
    mn = np.asarray(L0_attn_memory_norms, dtype=np.float32)
    wq = np.asarray(L0_attn_w_q, dtype=np.float32)
    wk = np.asarray(L0_attn_w_k, dtype=np.float32)
    wv = np.asarray(L0_attn_w_v, dtype=np.float32)
    wo = np.asarray(L0_attn_w_o, dtype=np.float32)
    gate = np.asarray(L0_attn_gate, dtype=np.float32)
    wf = np.asarray(L0_ffn_weight, dtype=np.float32)
    bias = np.asarray(L0_ffn_bias, dtype=np.float32)

    wq_r = _stage_weight_qkv(wq)
    wk_r = _stage_weight_qkv(wk)
    wv_r = _stage_weight_qkv(wv)
    wo_r = _stage_weight_out(wo)
    wf_r = _stage_weight_out(wf)
    mem_r = np.ascontiguousarray(
        mem.transpose(1, 2, 0, 3).reshape(H, 128, M * 128)).astype(np.float32)
    mnT_r = np.ascontiguousarray(
        mn.transpose(2, 1, 0).reshape(128, H * M)).astype(np.float32)
    gate_r = gate.reshape(1, H).astype(np.float32)
    bias_r = bias.reshape(1, D).astype(np.float32)
    maskT_r = np.triu(np.ones((128, 128), dtype=ml_dtypes.bfloat16))
    idf_r = np.eye(128, dtype=np.float32)
    idb_r = np.eye(128, dtype=ml_dtypes.bfloat16)
    ones_r = np.ones((128, 128), dtype=ml_dtypes.bfloat16)

    nc = build_nc()
    in_maps = []
    for core in range(8):
        b, half = core // 2, core % 2
        xs = np.ascontiguousarray(x[b, half * TOK:(half + 1) * TOK, :])
        in_maps.append({
            "x_tm": xs,
            "x_fm": np.ascontiguousarray(xs.T),
            "wq_r": wq_r, "wk_r": wk_r, "wv_r": wv_r,
            "wo_r": wo_r, "wf_r": wf_r,
            "mem_r": mem_r, "mnT_r": mnT_r,
            "gate_r": gate_r, "bias_r": bias_r,
            "maskT_r": maskT_r, "idf_r": idf_r,
            "idb_r": idb_r, "ones_r": ones_r,
            "flag_r": np.full((1, 1), float(half), dtype=np.float32),
        })

    res = run_bass_kernel_spmd(nc, in_maps, core_ids=list(range(8)))
    out = np.empty((B, S, D), dtype=np.float32)
    for core in range(8):
        b, half = core // 2, core % 2
        out[b, half * TOK:(half + 1) * TOK, :] = res.results[core]["out"]
    return out


if __name__ == "__main__":
    rng = np.random.default_rng(0)
    ins = {
        "L0_attn_hidden_states": rng.standard_normal((B, S, D)).astype(np.float32),
        "L0_attn_memories": rng.standard_normal((M, H, HD, HD)).astype(np.float32),
        "L0_attn_memory_norms": rng.random((M, H, HD)).astype(np.float32),
        "L0_attn_w_q": (rng.standard_normal((D, D)) * 0.02).astype(np.float32),
        "L0_attn_w_k": (rng.standard_normal((D, D)) * 0.02).astype(np.float32),
        "L0_attn_w_v": (rng.standard_normal((D, D)) * 0.02).astype(np.float32),
        "L0_attn_w_o": (rng.standard_normal((D, D)) * 0.02).astype(np.float32),
        "L0_attn_gate": rng.standard_normal((H,)).astype(np.float32),
        "L0_ffn_weight": rng.standard_normal((D, D)).astype(np.float32),
        "L0_ffn_bias": rng.standard_normal((D,)).astype(np.float32),
    }
    y = kernel(**ins)
    print("ok", y.shape, y.dtype, float(np.abs(y).max()))
